# version 02 (final)
# baseline (speedup 1.0000x reference)
import contextlib

import numpy as np
import ml_dtypes

import concourse.bass as bass
import concourse.mybir as mybir
from concourse.bass_utils import run_bass_kernel_spmd
from concourse.alu_op_type import AluOpType as AL

F32 = mybir.dt.float32
BF16 = mybir.dt.bfloat16
I32 = mybir.dt.int32
AF = mybir.ActivationFunctionType
AXX = mybir.AxisListType.X

BF = ml_dtypes.bfloat16

B, T, D = 2, 4096, 2048
NCORE = 8
TOK = 1024
H, HKV, DH = 32, 8, 64
BS, WIN = 64, 512
NB = T // BS
E0, I0, GRP0 = 16, 1024, 16
E1, F1 = 16, 1024
LN_EPS = 1e-5



def _rope_tables():
    inv = 1.0 / (10000.0 ** (np.arange(0, DH, 2, dtype=np.float32) / DH))
    t = np.arange(T, dtype=np.float32)
    fr = np.outer(t, inv)
    return np.cos(fr), np.sin(fr)


def _perm_qk_cols(n_heads):
    idx_x0, idx_x1 = [], []
    for h in range(n_heads):
        for i in range(DH // 2):
            idx_x0.append(h * DH + 2 * i)
            idx_x1.append(h * DH + 2 * i + 1)
    return np.array(idx_x0), np.array(idx_x1)


def _perm64():
    return np.array([2 * i for i in range(32)] + [2 * i + 1 for i in range(32)])


def _window_masks(qoff):
    masks = np.zeros((2, 8, 128, 512), np.float32)
    sl = np.arange(128)[:, None]
    tl = np.arange(512)[None, :]
    for tt in range(2):
        for r in range(8):
            d = 512 + tl - 128 * r - sl
            mk = ((d >= 0) & (d <= 511)).astype(np.float32)
            if qoff == 0:
                ks = 128 * (4 * tt + r) + sl
                mk = mk * (ks >= 512)
            masks[tt, r] = mk
    return masks.astype(BF)


def host_prepare(inp):
    x = np.asarray(inp["L0_attn_x"])
    cos, sin = _rope_tables()

    x0q, x1q = _perm_qk_cols(H)
    x0k, x1k = _perm_qk_cols(HKV)
    Wq = np.asarray(inp["L0_attn_W_q"])
    Wk = np.asarray(inp["L0_attn_W_k"])
    Wv = np.asarray(inp["L0_attn_W_v"])
    WKV = np.ascontiguousarray(
        np.concatenate([Wk[:, x0k], Wk[:, x1k], Wv], axis=1))
    qcols = np.zeros(2048, np.int64)
    for par in range(2):
        for j in range(4):
            for half in range(2):
                for q in range(4):
                    g = half * 4 + q
                    h = 4 * g + j
                    for i in range(32):
                        cp = par * 1024 + j * 256 + half * 128 + q * 32 + i
                        qcols[cp] = h * 64 + 2 * i + par
    WQ = np.ascontiguousarray(Wq[:, qcols])
    WGATE = np.ascontiguousarray(np.asarray(inp["L0_attn_W_gate"]))
    WO = np.ascontiguousarray(np.asarray(inp["L0_attn_W_o"]))

    p64 = _perm64()
    Wck = np.asarray(inp["L0_attn_W_ck"]).reshape(BS, DH, DH)
    Wck = Wck[:, p64, :][:, :, p64]
    wck0 = Wck[:, :32, :].transpose(1, 0, 2).reshape(32, BS * DH)
    wck1 = Wck[:, 32:, :].transpose(1, 0, 2).reshape(32, BS * DH)
    WCK0 = np.ascontiguousarray(np.tile(wck0, (4, 1)))
    WCK1 = np.ascontiguousarray(np.tile(wck1, (4, 1)))
    Wcv = np.asarray(inp["L0_attn_W_cv"]).reshape(BS, DH, DH)
    wcv = Wcv.transpose(1, 0, 2).reshape(DH, BS * DH)
    WCV = np.ascontiguousarray(np.tile(wcv, (2, 1)))

    GW0T = np.ascontiguousarray(np.asarray(inp["L0_ffn_gate_weight"]).T)
    qup = np.ascontiguousarray(np.asarray(inp["L0_ffn_fc_up_qweights"]))
    qdn = np.ascontiguousarray(np.asarray(inp["L0_ffn_fc_down_qweights"]))
    SUP = np.ascontiguousarray(
        np.repeat(np.asarray(inp["L0_ffn_fc_up_blocks_scale"]), GRP0, axis=1))
    SDN = np.ascontiguousarray(
        np.repeat(np.asarray(inp["L0_ffn_fc_down_blocks_scale"]), GRP0, axis=1))
    gup = np.ascontiguousarray(
        np.asarray(inp["L0_ffn_fc_up_global_scale"]).reshape(1, 1))
    gdn = np.ascontiguousarray(
        np.asarray(inp["L0_ffn_fc_down_global_scale"]).reshape(1, 1))

    l1 = {k.lower(): np.ascontiguousarray(np.asarray(inp[k])) for k in [
        "L1_attn_Wqk", "L1_attn_Wv", "L1_attn_Wu", "L1_attn_Wo",
        "L1_ffn_gate_w", "L1_ffn_shared_gate_w", "L1_ffn_shared_up_w",
        "L1_ffn_shared_down_w", "L1_ffn_experts_gate_up", "L1_ffn_experts_down",
    ]}
    vecs = {k.lower(): np.ascontiguousarray(np.asarray(inp[k]).reshape(-1, 1))
            for k in ["L1_attn_bqk", "L1_attn_bv", "L1_attn_bu", "L1_attn_bo",
                      "L1_attn_ln_g", "L1_attn_ln_b", "L1_ffn_expert_bias"]}

    ones_blk32 = np.zeros((128, 4), np.float32)
    for g in range(4):
        ones_blk32[32 * g:32 * (g + 1), g] = 1.0

    shared = dict(
        wkv=WKV, wq=WQ, wgate=WGATE, wo=WO,
        wck0=WCK0, wck1=WCK1, wcv=WCV,
        gw0t=GW0T, qup=qup, qdn=qdn, sup=SUP, sdn=SDN, gup=gup, gdn=gdn,

        identbf=np.eye(128, dtype=np.float32).astype(BF),
        sel3=np.eye(3, dtype=np.float32).astype(BF),
        selb=np.ascontiguousarray(np.repeat(np.eye(16, dtype=np.float32), 128,
                                            axis=1).astype(BF)),
        rep32=np.ascontiguousarray(np.tile(np.eye(32, dtype=np.float32), (1, 4)).astype(BF)),
        identf32=np.eye(128, dtype=np.float32),
        onesblk=ones_blk32.astype(BF),
        **l1, **vecs,
    )

    in_maps = []
    for c in range(NCORE):
        b, qoff = c // 4, (c % 4) * TOK
        shift = qoff - 512
        xr = np.roll(x[b], -shift, axis=0)
        pos = (np.arange(T) + shift) % T
        m = dict(shared)
        m["masks"] = np.ascontiguousarray(
            _window_masks(qoff).reshape(16 * 128, 512))
        m["x"] = np.ascontiguousarray(xr)
        m["cos"] = np.ascontiguousarray(np.tile(cos[pos].T, (4, 1)).astype(BF))
        m["sin"] = np.ascontiguousarray(np.tile(sin[pos].T, (4, 1)).astype(BF))
        in_maps.append(m)
    return in_maps



class Prog:

    def __init__(self, nc, block, stack):
        self.nc, self.block, self.stack = nc, block, stack
        self._sems = {}
        self.cnt = {}
        self.base = {}

    def sem(self, name):
        if name not in self._sems:
            self._sems[name] = self.stack.enter_context(self.nc.semaphore(name))
            self.cnt[name] = 0
        return self._sems[name]

    def inc(self, instr, name, k=1):
        instr.then_inc(self.sem(name), k)
        self.cnt[name] += k
        return self.cnt[name]

    def n(self, name):
        self.sem(name)
        return self.cnt[name]

    def rebase_all(self):
        for name in list(self.cnt):
            self.base[name] = self.cnt[name]

    def wge(self, eng, name, expr):
        eng.wait_ge(self.sem(name), self.base.get(name, 0) + expr)

    def vfence(self, v, ins):
        n = self.inc(ins, "vfence", 1)
        v.wait_ge(self.sem("vfence"), n)
        return ins

    def vred(self, v, out, in_, op, **kw):
        ins = v.tensor_reduce(out, in_, axis=AXX, op=op, **kw)
        n = self.inc(ins, "vfence", 1)
        v.wait_ge(self.sem("vfence"), n)
        return ins


def sb(stack, nc, name, shape, dtype):
    return stack.enter_context(nc.sbuf_tensor(name, shape, dtype))


def ps(stack, nc, name, shape, dtype=F32):
    return stack.enter_context(nc.psum_tensor(name, shape, dtype))


class SlotRing:

    def __init__(self, p, name, banks):
        self.p, self.banks = p, banks
        self.q = name + "q"
        self.e = name + "e"
        self.k = 0
        self.ke = 0

    def pre_transpose(self, eng):
        if self.k >= 16:
            eng.wait_ge(self.p.sem(self.e), self.k - 8)
        b, s = (self.k // 8) % 2, self.k % 8
        return self.banks[b][:, 128 * s:128 * (s + 1)]

    def post_transpose(self, instr):
        self.p.inc(instr, self.q, 1)
        self.k += 1

    def pre_evac(self, eng):
        eng.wait_ge(self.p.sem(self.q), (self.ke // 8 + 1) * 8)
        b, s = (self.ke // 8) % 2, self.ke % 8
        return self.banks[b][:, 128 * s:128 * (s + 1)]

    def post_evac(self, instr):
        self.p.inc(instr, self.e, 1)
        self.ke += 1



def emit_prelude(p, env, ph):
    p.rebase_all()
    nc, blk = p.nc, p.block
    st = p.stack

    env["KX0"] = [sb(st, nc, f"KX0_{i}", [128, T], BF16) for i in range(2)]
    env["KX1"] = [sb(st, nc, f"KX1_{i}", [128, T], BF16) for i in range(2)]
    env["VT"] = sb(st, nc, "VT", [128, 32 * 8 * 65], BF16)
    env["XNO"] = sb(st, nc, "XNO", [128, 16 * TOK], BF16)
    env["IDB"] = sb(st, nc, "IDB", [128, 128], BF16)
    env["ONESB"] = sb(st, nc, "ONESB", [128, 4], BF16)
    env["KMAX"] = sb(st, nc, "KMAX", [4, 16], F32)
    env["ECH"] = sb(st, nc, "ECH", [1, 2], F32)
    env["COS"] = sb(ph, nc, "COS", [128, T], BF16)
    env["SIN"] = sb(ph, nc, "SIN", [128, T], BF16)

    KX0, KX1, VT = env["KX0"], env["KX1"], env["VT"]
    XNO = env["XNO"]
    env["IDF"] = None
    COS, SIN, IDB = env["COS"], env["SIN"], env["IDB"]

    env["dbg_s1"] = None
    xtm = [sb(ph, nc, f"xtm{i}", [128, D], F32) for i in range(2)]
    env["_stats"] = (s1, s2, mean, nbias, stdv, rstd) if False else None
    xntm = [sb(ph, nc, f"xntm{i}", [128, D], BF16) for i in range(2)]
    s1 = sb(ph, nc, "s1", [128, 32], F32)
    s2 = sb(ph, nc, "s2", [128, 32], F32)
    mean = sb(ph, nc, "mean", [128, 32], F32)
    nbias = sb(ph, nc, "nbias", [128, 32], F32)
    stdv = sb(ph, nc, "stdv", [128, 32], F32)
    rstd = sb(ph, nc, "rstd", [128, 32], F32)
    WKVT = sb(ph, nc, "WKVT", [128, 16 * 1024], BF16)
    XNF = [sb(ph, nc, f"XNF{i}", [128, 16 * 512], BF16) for i in range(2)]
    vev = sb(ph, nc, "vev", [128, 4 * 512], BF16)
    rs0 = sb(ph, nc, "rs0", [128, 512], BF16)
    rs1 = sb(ph, nc, "rs1", [128, 512], BF16)
    ksq, ksqB = rs0, rs1

    env["_stats"] = dict(s1=s1, s2=s2, mean=mean, nbias=nbias, stdv=stdv,
                         rstd=rstd)
    pj = [ps(ph, nc, f"pj{i}", [128, 512], F32) for i in range(4)]
    ptp = [ps(ph, nc, f"ptp{i}", [128, 8 * 128], BF16) for i in range(2)]
    pkn = ps(ph, nc, "pkn", [4, 512], F32)

    OWN_TB = (1, 2)
    rtp = SlotRing(p, "tp", ptp)

    @blk.gpsimd
    def _(g):
        p.inc(g.memset(VT[:, :], 1.0), "vtm", 1)
        for kc in range(16):
            p.inc(g.dma_start(out=WKVT[:, 1024 * kc:1024 * (kc + 1)],
                              in_=env["wkv"][128 * kc:128 * (kc + 1), :]),
                  "wkv", 16)

    @blk.sync
    def _(sy):
        for name, dst in [("cos", COS), ("sin", SIN), ("identbf", IDB),
                          ("onesblk", env["ONESB"])]:
            p.inc(sy.dma_start(out=dst[:, :], in_=env[name][:, :]), "cst", 16)
        for u in range(32):
            if u >= 2:
                p.wge(sy, "xn", u - 1)
                p.wge(sy, "st2", u - 1)
                p.wge(sy, "tpq", 80 * ((u - 2) // 4) + 16 * ((u - 2) % 4 + 1))
            p.inc(sy.dma_start(out=xtm[u % 2][:, :],
                               in_=env["x"][128 * u:128 * (u + 1), :]),
                  f"xl{u % 2}", 16)

    for tb in range(8):
        u0 = tb * 4

        for uu in range(4):
            u = u0 + uu

            @blk.scalar
            def _(s, u=u):
                p.wge(s, f"xl{u % 2}", 16 * (u // 2 + 1))
                if u >= 2:
                    p.wge(s, "tpq", 80 * ((u - 2) // 4) + 16 * ((u - 2) % 4 + 1))
                s.activation(xntm[u % 2][:, :], xtm[u % 2][:, :], AF.Square,
                             accum_out=s2[:, u:u + 1])
                p.wge(s, "st1", u + 1)
                p.inc(s.activation(stdv[:, u:u + 1], s2[:, u:u + 1], AF.Sqrt,
                                   bias=nbias[:, u:u + 1], scale=1.0 / D),
                      "st2", 1)

            @blk.vector
            def _(v, u=u):
                p.wge(v, f"xl{u % 2}", 16 * (u // 2 + 1))
                p.vred(v, s1[:, u:u + 1], xtm[u % 2][:, :], AL.add)
                p.vfence(v, v.tensor_scalar(mean[:, u:u + 1], s1[:, u:u + 1],
                                            1.0 / D, None, op0=AL.mult))
                p.vfence(v, v.tensor_tensor(nbias[:, u:u + 1], mean[:, u:u + 1],
                                             mean[:, u:u + 1], op=AL.mult))
                p.inc(v.tensor_scalar(nbias[:, u:u + 1], nbias[:, u:u + 1],
                                      -1.0, LN_EPS, op0=AL.mult, op1=AL.add),
                      "st1", 1)
                p.wge(v, "st2", u + 1)
                p.vfence(v, v.reciprocal(rstd[:, u:u + 1], stdv[:, u:u + 1]))
                if u >= 2:
                    p.wge(v, "tpq", 80 * ((u - 2) // 4) + 16 * ((u - 2) % 4 + 1))
                p.inc(v.tensor_scalar(xntm[u % 2][:, :], xtm[u % 2][:, :],
                                      mean[:, u:u + 1], rstd[:, u:u + 1],
                                      op0=AL.subtract, op1=AL.mult), "xn", 1)

            @blk.tensor
            def _(t, tb=tb, u=u, uu=uu):
                if tb == 0 and uu == 0:
                    t.wait_ge(p.sem("cst"), p.n("cst"))
                if tb >= 2 and uu == 0:
                    p.wge(t, "vtd", 4 * (tb - 1))
                    p.wge(t, "xno", len([o for o in OWN_TB if o <= tb - 2]))
                p.wge(t, "xn", u + 1)
                for cc in range(16):
                    dstap = rtp.pre_transpose(t)
                    ins = t.transpose(dstap,
                                      xntm[u % 2][:, 128 * cc:128 * (cc + 1)],
                                      IDB[:, :])
                    rtp.post_transpose(ins)

            @blk.scalar
            def _(s, tb=tb, uu=uu):
                for cc in range(16):
                    srcap = rtp.pre_evac(s)
                    ins = s.copy(XNF[tb % 2][:, 512 * cc + 128 * uu:
                                             512 * cc + 128 * (uu + 1)],
                                 srcap)
                    rtp.post_evac(ins)

        @blk.tensor
        def _(t, tb=tb):
            p.wge(t, "tpe", 80 * tb + 64)
            if tb == 0:
                p.wge(t, "wkv", 256)
            if tb >= 1:
                p.wge(t, "vtd", 4 * tb)
            for kc in range(16):
                for oc in range(4):
                    ins = t.matmul(pj[oc][:, :],
                                   WKVT[:, 1024 * kc + 128 * oc:
                                        1024 * kc + 128 * (oc + 1)],
                                   XNF[tb % 2][:, 512 * kc:512 * (kc + 1)],
                                   start=(kc == 0), stop=(kc == 15))
                    if kc == 15:
                        p.inc(ins, "kpj", 1)

        @blk.vector
        def _(v, tb=tb):
            v.wait_ge(p.sem("kpj"), p.n("kpj"))
            sl = slice(512 * tb, 512 * (tb + 1))
            for half in range(2):
                px0, px1 = pj[0 + half], pj[2 + half]
                v.tensor_tensor(rs0[:, :], px0[:, :], COS[:, sl], op=AL.mult)
                v.tensor_tensor(rs1[:, :], px1[:, :], SIN[:, sl], op=AL.mult)
                v.tensor_tensor(KX0[half][:, sl], rs0[:, :], rs1[:, :],
                                op=AL.subtract)
                v.tensor_tensor(rs0[:, :], px0[:, :], SIN[:, sl], op=AL.mult)
                v.tensor_tensor(rs1[:, :], px1[:, :], COS[:, sl], op=AL.mult)
                p.inc(v.tensor_tensor(KX1[half][:, sl], rs0[:, :], rs1[:, :],
                                      op=AL.add), "rpd", 1)

        @blk.tensor
        def _(t, tb=tb):
            t.wait_ge(p.sem("rpd"), p.n("rpd"))
            for kc in range(16):
                for oc in range(4):
                    ins = t.matmul(pj[oc][:, :],
                                   WKVT[:, 1024 * kc + 128 * (4 + oc):
                                        1024 * kc + 128 * (5 + oc)],
                                   XNF[tb % 2][:, 512 * kc:512 * (kc + 1)],
                                   start=(kc == 0), stop=(kc == 15))
                    if kc == 15:
                        p.inc(ins, "vpj", 1)

        @blk.vector
        def _(v, tb=tb):
            v.wait_ge(p.sem("vpj"), p.n("vpj"))
            if tb >= 1:
                p.wge(v, "vfs", 64 * tb)
            for oc in range(4):
                p.inc(v.tensor_copy(vev[:, 512 * oc:512 * (oc + 1)],
                                    pj[oc][:, :]), "vtd", 1)

        @blk.gpsimd
        def _(g, tb=tb):
            g.wait_ge(p.sem("vtd"), p.n("vtd"))
            for oc in range(4):
                p.inc(g.dma_start(
                    out=env["vfm_scr"][128 * oc:128 * (oc + 1),
                                       512 * tb:512 * (tb + 1)],
                    in_=vev[:, 512 * oc:512 * (oc + 1)]), "vfs", 16)

        @blk.tensor
        def _(t, tb=tb):
            t.wait_ge(p.sem("vtd"), p.n("vtd"))
            for vc in range(4):
                for ts4 in range(4):
                    dstap = rtp.pre_transpose(t)
                    ins = t.transpose(dstap,
                                      vev[:, 512 * vc + 128 * ts4:
                                          512 * vc + 128 * (ts4 + 1)],
                                      IDB[:, :])
                    rtp.post_transpose(ins)

        @blk.scalar
        def _(s, tb=tb):
            if tb == 0:
                p.wge(s, "vtm", 1)
            for vc in range(4):
                for ts4 in range(4):
                    bk, sl = (rtp.ke // 8) % 2, rtp.ke % 8
                    rtp.pre_evac(s)
                    stile = tb * 4 + ts4
                    dst = bass.AP(VT, (stile * 8 + 2 * vc) * 65,
                                  [[32 * 8 * 65, 128], [65, 2], [1, 64]])
                    sap = bass.AP(ptp[bk], 128 * sl,
                                  [[8 * 128, 128], [64, 2], [1, 64]])
                    ins = s.copy(dst, sap)
                    rtp.post_evac(ins)

        if tb in OWN_TB:
            @blk.vector
            def _(v, tb=tb):
                p.wge(v, "tpe", 80 * tb + 64)
                ot = OWN_TB.index(tb)
                for cc in range(16):
                    ins = v.tensor_copy(XNO[:, 1024 * cc + 512 * ot:
                                            1024 * cc + 512 * (ot + 1)],
                                        XNF[tb % 2][:, 512 * cc:512 * (cc + 1)])
                    if cc == 15:
                        p.inc(ins, "xno", 1)

    for step in range(16):
        tsl, half = step // 2, step % 2
        sl = slice(512 * tsl, 512 * (tsl + 1))

        @blk.vector
        def _(v, step=step, tsl=tsl, half=half, sl=sl):
            if step >= 1:
                p.wge(v, "knq", step)
            t0, t1 = KX0[half], KX1[half]
            v.tensor_tensor(ksq[:, :], t0[:, sl], t0[:, sl], op=AL.mult)
            p.inc(v.tensor_tensor(ksqB[:, :], t1[:, sl], t1[:, sl],
                                  op=AL.mult), "ksqn", 1)

        @blk.tensor
        def _(t, step=step):
            p.wge(t, "ksqn", step + 1)
            if step >= 1:
                p.wge(t, "knm", step)
            t.matmul(pkn[:, :], env["ONESB"][:, :], ksq[:, :],
                     start=True, stop=False)
            p.inc(t.matmul(pkn[:, :], env["ONESB"][:, :],
                           ksqB[:, :], start=False, stop=True),
                  "knq", 1)

        @blk.vector
        def _(v, step=step):
            p.wge(v, "knq", step + 1)
            p.inc(v.tensor_reduce(env["KMAX"][:, step:step + 1],
                                  pkn[:, :], axis=AXX, op=AL.max), "knm", 1)
    return



def emit_attn(p, env, ph):
    p.rebase_all()
    nc, blk = p.nc, p.block
    XNO, VT = env["XNO"], env["VT"]
    KX0, KX1 = env["KX0"], env["KX1"]
    IDB, ONESB = env["IDB"], env["ONESB"]

    QXT = sb(ph, nc, "QXT", [128, 16 * TOK], BF16)
    COSQ = sb(ph, nc, "COSQ", [128, TOK], BF16)
    SINQ = sb(ph, nc, "SINQ", [128, TOK], BF16)
    GN = sb(ph, nc, "GN", [1, 3 * TOK], BF16)
    NEGCB = sb(ph, nc, "NEGCB", [128, 1], F32)
    IDF = sb(ph, nc, "IDF", [128, 128], F32)
    ONE1 = sb(ph, nc, "ONE1", [1, 128], BF16)
    QMX = sb(ph, nc, "QMX", [4, 16], F32)
    AUX = sb(ph, nc, "AUX", [128, 64], F32)
    AUXB = sb(ph, nc, "AUXB", [128, 16], BF16)

    with contextlib.ExitStack() as phA:
        WQS = [sb(phA, nc, f"WQS{i}", [128, 2 * 2048], BF16) for i in range(2)]
        WGT = sb(phA, nc, "WGT", [128, 48], BF16)
        SEL3 = sb(phA, nc, "SEL3", [3, 3], BF16)
        GEXP = sb(phA, nc, "GEXP", [3, TOK], BF16)
        rs0 = sb(phA, nc, "ars0", [128, 512], BF16)
        rs1 = sb(phA, nc, "ars1", [128, 512], BF16)
        QSQ, QSQ2 = rs0, rs1

        pq = [ps(phA, nc, f"apq{i}", [128, 512], F32) for i in range(4)]
        pg = ps(phA, nc, "apg", [4, 512], F32)
        pgs = ps(phA, nc, "apgs", [4, 512], F32)
        pb1 = ps(phA, nc, "apb1", [128, 512], F32)

        def prelude_gate(eng):
            eng.wait_ge(p.sem("knm"), p.n("knm"))
            eng.wait_ge(p.sem("knq"), p.n("knq"))
            eng.wait_ge(p.sem("tpe"), p.n("tpe"))
            eng.wait_ge(p.sem("vfs"), p.n("vfs"))
            eng.wait_ge(p.sem("xno"), p.n("xno"))

        @blk.gpsimd
        def _(g):
            prelude_gate(g)
            p.inc(g.dma_start(out=COSQ[:, :], in_=env["cos"][:, 512:1536]),
                  "acq", 16)
            p.inc(g.dma_start(out=SINQ[:, :], in_=env["sin"][:, 512:1536]),
                  "acq", 16)
            p.inc(g.memset(ONE1[:, :], 1.0), "aon", 1)
            p.inc(g.memset(AUXB[:, :], 1.0), "aob", 1)
            p.inc(g.memset(AUX[:, 8:10], 0.0), "aoz", 1)
            p.inc(g.dma_start(out=IDF[:, :], in_=env["identf32"][:, :]),
                  "aid", 16)
            p.inc(g.dma_start(out=SEL3[:, :], in_=env["sel3"][:, :]), "agw", 16)
            for kc in range(16):
                p.inc(g.dma_start(out=WGT[:, 3 * kc:3 * (kc + 1)],
                                  in_=env["wgate"][128 * kc:128 * (kc + 1), :]),
                      "agw", 16)

        @blk.tensor
        def _(t):
            t.wait_ge(p.sem("agw"), p.n("agw"))
            for tt in range(2):
                for kc in range(16):
                    ins = t.matmul(pg[:3, :], WGT[:, 3 * kc:3 * (kc + 1)],
                                   XNO[:, 1024 * kc + 512 * tt:
                                       1024 * kc + 512 * (tt + 1)],
                                   start=(kc == 0), stop=(kc == 15))
                    if kc == 15:
                        p.inc(ins, "agp", 1)
                if tt == 0:
                    p.wge(t, "age", 1)

        @blk.scalar
        def _(s):
            for tt in range(2):
                p.wge(s, "agp", tt + 1)
                p.inc(s.activation(GEXP[:, 512 * tt:512 * (tt + 1)], pg[:3, :],
                                   AF.Exp), "age", 1)

        @blk.tensor
        def _(t):
            p.wge(t, "aob", 1)
            p.wge(t, "age", 2)
            for tt in range(2):
                if tt == 1:
                    p.wge(t, "agn", 1)
                p.inc(t.matmul(pgs[:1, 0:512], AUXB[:3, 0:1],
                               GEXP[:3, 512 * tt:512 * (tt + 1)],
                               start=True, stop=True,
                               skip_group_check=True), "ags", 1)
                for b in range(3):
                    p.inc(t.matmul(pb1[:1, 0:512], SEL3[:, b:b + 1],
                                   GEXP[:3, 512 * tt:512 * (tt + 1)],
                                   start=True, stop=True,
                                   skip_group_check=True), "agr", 1)
                    if not (tt == 1 and b == 2):
                        p.wge(t, "agv", 3 * tt + b + 1)

        RR = sb(phA, nc, "RR", [1, 512], F32)

        @blk.vector
        def _(v):
            for tt in range(2):
                sl = slice(512 * tt, 512 * (tt + 1))
                for b in range(3):
                    p.wge(v, "agr", 3 * tt + b + 1)
                    p.inc(v.tensor_copy(GN[0:1, 1024 * b + 512 * tt:
                                        1024 * b + 512 * (tt + 1)],
                                        pb1[:1, 0:512]), "agv", 1)
                p.wge(v, "ags", tt + 1)
                p.vfence(v, v.reciprocal(RR[0:1, :], pgs[0:1, 0:512]))
                for b in range(3):
                    ins = v.tensor_tensor(GN[0:1, 1024 * b + 512 * tt:
                                          1024 * b + 512 * (tt + 1)],
                                          GN[0:1, 1024 * b + 512 * tt:
                                          1024 * b + 512 * (tt + 1)],
                                          RR[0:1, :], op=AL.mult)
                    if b == 2:
                        p.inc(ins, "agn", 1)

        @blk.gpsimd
        def _(g):
            for cpair in range(8):
                if cpair >= 2:
                    p.wge(g, "aqp", 4 * (cpair - 1))
                for kc in range(16):
                    for xi in range(2):
                        p.inc(g.dma_start(
                            out=WQS[cpair % 2][:, 2048 * xi + 128 * kc:
                                               2048 * xi + 128 * (kc + 1)],
                            in_=env["wq"][128 * kc:128 * (kc + 1),
                                          128 * (8 * xi + cpair):
                                          128 * (8 * xi + cpair + 1)]),
                            f"awq{cpair % 2}", 16)

        for cpair in range(8):
            for tt in range(2):
                step = 2 * cpair + tt
                qx0 = 1024 * cpair + 512 * tt
                qx1 = 1024 * (8 + cpair) + 512 * tt

                @blk.tensor
                def _(t, cpair=cpair, tt=tt, step=step):
                    if step == 0:
                        p.wge(t, "agn", 2)
                    if tt == 0:
                        p.wge(t, f"awq{cpair % 2}", 16 * 32 * (cpair // 2 + 1))
                    if step >= 2:
                        p.wge(t, "aqe", step - 1)
                    for kc in range(16):
                        for xi in range(2):
                            ins = t.matmul(
                                pq[2 * (cpair % 2) + xi][:, :],
                                WQS[cpair % 2][:, 2048 * xi + 128 * kc:
                                               2048 * xi + 128 * (kc + 1)],
                                XNO[:, 1024 * kc + 512 * tt:
                                    1024 * kc + 512 * (tt + 1)],
                                start=(kc == 0), stop=(kc == 15))
                            if kc == 15:
                                p.inc(ins, "aqp", 1)

                @blk.vector
                def _(v, cpair=cpair, tt=tt, step=step, qx0=qx0, qx1=qx1):
                    v.wait_ge(p.sem("aqp"), p.n("aqp"))
                    if step == 0:
                        p.wge(v, "acq", 32)
                    px0, px1 = pq[2 * (cpair % 2)], pq[2 * (cpair % 2) + 1]
                    csl = slice(512 * tt, 512 * (tt + 1))
                    v.tensor_tensor(rs0[:, :], px0[:, :], COSQ[:, csl], op=AL.mult)
                    v.tensor_tensor(rs1[:, :], px1[:, :], SINQ[:, csl], op=AL.mult)
                    v.tensor_tensor(QXT[:, qx0:qx0 + 512], rs0[:, :], rs1[:, :],
                                    op=AL.subtract)
                    v.tensor_tensor(rs0[:, :], px0[:, :], SINQ[:, csl], op=AL.mult)
                    v.tensor_tensor(rs1[:, :], px1[:, :], COSQ[:, csl], op=AL.mult)
                    p.inc(v.tensor_tensor(QXT[:, qx1:qx1 + 512], rs0[:, :],
                                          rs1[:, :], op=AL.add), "aqe", 1)
                    v.tensor_tensor(QSQ[:, :], QXT[:, qx0:qx0 + 512],
                                    QXT[:, qx0:qx0 + 512], op=AL.mult)
                    p.inc(v.tensor_tensor(QSQ2[:, :], QXT[:, qx1:qx1 + 512],
                                          QXT[:, qx1:qx1 + 512], op=AL.mult),
                          "aqs", 1)

                @blk.tensor
                def _(t, step=step):
                    p.wge(t, "aqs", step + 1)
                    if step >= 1:
                        p.wge(t, "aqn", step)
                    t.matmul(pg[:, :], ONESB[:, :], QSQ[:, :], start=True,
                             stop=False)
                    p.inc(t.matmul(pg[:, :], ONESB[:, :], QSQ2[:, :],
                                   start=False, stop=True), "aqm", 1)

                @blk.vector
                def _(v, step=step):
                    p.wge(v, "aqm", step + 1)
                    p.inc(v.tensor_reduce(QMX[:, step:step + 1], pg[:, :],
                                          axis=AXX, op=AL.max), "aqn", 1)

        @blk.vector
        def _(v):
            v.wait_ge(p.sem("aqn"), p.n("aqn"))
            p.wge(v, "aoz", 1)
            p.vred(v, AUX[:4, 8:9], QMX[:, :], AL.max)
            p.inc(v.tensor_copy(AUX[0:1, 25:26], AUX[0:1, 8:9]), "aclA", 1)

        @blk.vector
        def _(v):
            p.vred(v, AUX[:4, 9:10], env["KMAX"][:, :], AL.max)
            p.inc(v.tensor_copy(AUX[0:1, 26:27], AUX[0:1, 9:10]), "acl", 1)

        @blk.tensor
        def _(t):
            p.wge(t, "acl", 1)
            p.wge(t, "aclA", 1)
            p.wge(t, "aid", 16)
            t.transpose(pb1[0:1, 0:128], AUX[:, 8:9], IDF[:, :])
            p.inc(t.transpose(pb1[0:1, 128:256], AUX[:, 9:10], IDF[:, :]),
                  "act", 1)

        @blk.vector
        def _(v):
            p.wge(v, "act", 1)
            p.vred(v, AUX[0:1, 2:3], pb1[0:1, 0:128], AL.max)
            p.vred(v, AUX[0:1, 3:4], pb1[0:1, 128:256], AL.max)
            p.vfence(v, v.tensor_tensor(AUX[0:1, 4:5], AUX[0:1, 2:3],
                                        AUX[0:1, 3:4], op=AL.mult))
            p.inc(v.tensor_copy(AUX[0:1, 5:6], AUX[0:1, 4:5]), "acm", 1)

        @blk.scalar
        def _(s):
            p.wge(s, "acm", 1)
            s.activation(AUX[0:1, 6:7], AUX[0:1, 5:6], AF.Sqrt)
            p.inc(s.mul(AUX[0:1, 7:8], AUX[0:1, 6:7], -0.125), "acs", 1)

        @blk.vector
        def _(v):
            p.wge(v, "acs", 1)
            p.inc(v.tensor_copy(AUXB[0:1, 1:2], AUX[0:1, 7:8]), "acb", 1)

        @blk.tensor
        def _(t):
            p.wge(t, "acb", 1)
            p.wge(t, "aon", 1)
            p.inc(t.matmul(pb1[:, 0:1], ONE1[:, :], AUXB[0:1, 1:2],
                           start=True, stop=True, skip_group_check=True),
                  "acp", 1)

        @blk.vector
        def _(v):
            p.wge(v, "acp", 1)
            p.inc(v.tensor_copy(NEGCB[:, :], pb1[:, 0:1]), "acv", 1)

    with contextlib.ExitStack() as phB:
        WCK0T = sb(phB, nc, "WCK0T", [128, BS * DH], BF16)
        WCK1T = sb(phB, nc, "WCK1T", [128, BS * DH], BF16)
        WCVT = sb(phB, nc, "WCVT", [128, BS * DH], BF16)
        REP32 = sb(phB, nc, "REP32", [32, 128], BF16)
        MASKS = sb(phB, nc, "MASKS", [128, 16 * 512], BF16)
        VFG = sb(phB, nc, "VFG", [64, T], BF16)
        KCS0 = sb(phB, nc, "KCS0", [128, 64], BF16)
        KCS1 = sb(phB, nc, "KCS1", [128, 64], BF16)
        KA = sb(phB, nc, "KA", [32, 64], BF16)
        KB = sb(phB, nc, "KB", [32, 64], BF16)
        KQ = sb(phB, nc, "KQ", [32, 64], BF16)
        KQ2 = sb(phB, nc, "KQ2", [32, 64], BF16)
        VCA = sb(phB, nc, "VCA", [64, 65], BF16)
        NEGCCB = sb(phB, nc, "NEGCCB", [64, 1], F32)
        PT = sb(phB, nc, "PT", [128, 4 * 512], BF16)
        PTC = sb(phB, nc, "PTC", [64, 512], BF16)
        PW = PT
        SRB = sb(phB, nc, "SRB", [1, 3 * 512], BF16)
        RR2 = sb(phB, nc, "RR2", [1, 512], F32)
        MBC = sb(phB, nc, "MBC", [128, 512], BF16)
        SOT = sb(phB, nc, "SOT", [64, 512], F32)
        SOST = [sb(phB, nc, f"SOST{i}", [64, 512], BF16) for i in range(2)]

        def stageA_gate(eng):
            eng.wait_ge(p.sem("aqp"), p.n("aqp"))
            eng.wait_ge(p.sem("aqn"), p.n("aqn"))
            eng.wait_ge(p.sem("acv"), p.n("acv"))
            eng.wait_ge(p.sem("agv"), p.n("agv"))
            eng.wait_ge(p.sem("agn"), p.n("agn"))
            eng.wait_ge(p.sem("aqe"), p.n("aqe"))

        @blk.sync
        def _(sy):
            stageA_gate(sy)
            p.inc(sy.dma_start(out=REP32[:, :], in_=env["rep32"][:, :]),
                  "bcw", 16)
            for r in range(16):
                p.inc(sy.dma_start(out=MASKS[:, 512 * r:512 * (r + 1)],
                                   in_=env["masks"][128 * r:128 * (r + 1), :]),
                      "bcw", 16)

        @blk.gpsimd
        def _(g):
            stageA_gate(g)
            for nm, dst in [("wck0", WCK0T), ("wck1", WCK1T), ("wcv", WCVT)]:
                p.inc(g.dma_start(out=dst[:, :], in_=env[nm][:, :]), "bcwg", 16)

        @blk.gpsimd
        def _(g):
            p.inc(g.memset(VCA[:, 64:65], 1.0), "bvo", 1)

        for g in range(8):
            q4 = g % 4
            half = g // 4
            KXa, KXb = KX0[half], KX1[half]

            with contextlib.ExitStack() as psB1:
                PKC = ps(psB1, nc, f"PKC{g}", [64, 64], F32)
                PVC = ps(psB1, nc, f"PVC{g}", [64, 64], F32)
                PRK = ps(psB1, nc, f"PRK{g}", [128, 64], F32)
                PNC = ps(psB1, nc, f"PNC{g}", [64, 64], F32)

                @blk.gpsimd
                def _(gg, g=g):
                    if g >= 1:
                        p.wge(gg, "bvf", g)
                    p.inc(gg.dma_start(out=VFG[:, :],
                                       in_=env["vfm_scr"][64 * g:64 * (g + 1), :]),
                          "bvl", 16)

                @blk.tensor
                def _(t, g=g, q4=q4, KXa=KXa, KXb=KXb):
                    if g == 0:
                        t.wait_ge(p.sem("bcw"), p.n("bcw"))
                        t.wait_ge(p.sem("bcwg"), p.n("bcwg"))
                        p.wge(t, "acv", 1)
                    else:
                        p.wge(t, "bka", g)
                        p.wge(t, "mgd", 8 * g)
                    for par in range(2):
                        W = WCK0T if par == 0 else WCK1T
                        KX = KXa if par == 0 else KXb
                        for pp in range(BS):
                            ins = t.matmul(
                                PKC[:, :],
                                W[32 * q4:32 * (q4 + 1), 64 * pp:64 * (pp + 1)],
                                bass.AP(KX, 32 * q4 * T + pp,
                                        [[T, 32], [64, 64]]),
                                start=(par == 0 and pp == 0),
                                stop=(par == 1 and pp == BS - 1),
                                tile_position=(32 * q4, 0),
                                skip_group_check=True)
                            if par == 1 and pp == BS - 1:
                                p.inc(ins, "bkc", 1)
                    p.wge(t, "bvl", 16 * (g + 1))
                    if g >= 1:
                        p.wge(t, "bve", g)
                    for pp in range(BS):
                        ins = t.matmul(
                            PVC[:, :],
                            bass.AP(VFG, pp, [[T, 64], [64, 64]]),
                            WCVT[0:64, 64 * pp:64 * (pp + 1)],
                            start=(pp == 0), stop=(pp == BS - 1),
                            skip_group_check=True)
                        if pp == BS - 1:
                            p.inc(ins, "bvc", 1)

                @blk.vector
                def _(v, g=g):
                    p.wge(v, "bkc", g + 1)
                    v.tensor_copy(KA[:, :], PKC[0:32, :])
                    p.inc(v.tensor_copy(KB[:, :], PKC[32:64, :]), "bka", 1)
                    p.wge(v, "bvc", g + 1)
                    if g == 0:
                        p.wge(v, "bvo", 1)
                    p.inc(v.tensor_copy(VCA[:, 0:64], PVC[:, :]), "bve", 1)
                    p.inc(v.tensor_tensor(KQ[:, :], KA[:, :], KA[:, :],
                                          op=AL.mult), "bvf", 1)
                    p.inc(v.tensor_tensor(KQ2[:, :], KB[:, :], KB[:, :],
                                          op=AL.mult), "bkq", 1)

                @blk.tensor
                def _(t, g=g):
                    p.wge(t, "bka", g + 1)
                    p.inc(t.matmul(PRK[:, :], REP32[:, :], KA[:, :],
                                   start=True, stop=True), "brk", 1)

                @blk.vector
                def _(v, g=g):
                    p.wge(v, "brk", 2 * g + 1)
                    p.inc(v.tensor_copy(KCS0[:, :], PRK[:, :]), "bk0", 1)

                @blk.tensor
                def _(t, g=g):
                    p.wge(t, "bk0", g + 1)
                    p.inc(t.matmul(PRK[:, :], REP32[:, :], KB[:, :],
                                   start=True, stop=True), "brk", 1)

                @blk.vector
                def _(v, g=g):
                    p.wge(v, "brk", 2 * g + 2)
                    p.inc(v.tensor_copy(KCS1[:, :], PRK[:, :]), "bk1", 1)

                @blk.tensor
                def _(t, g=g):
                    p.wge(t, "bkq", g + 1)
                    if g >= 1:
                        p.wge(t, "bnr", g)
                    t.matmul(PNC[0:1, 0:64], ONESB[0:32, 0:1], KQ[:, :],
                             start=True, stop=False)
                    p.inc(t.matmul(PNC[0:1, 0:64], ONESB[0:32, 0:1], KQ2[:, :],
                                   start=False, stop=True), "bnq", 1)

                @blk.vector
                def _(v, g=g):
                    p.wge(v, "bnq", g + 1)
                    p.vred(v, AUX[0:1, 16:17], PNC[0:1, 0:64], AL.max)
                    p.vfence(v, v.tensor_tensor(AUX[0:1, 17:18],
                                                AUX[0:1, 16:17],
                                                AUX[0:1, 2:3], op=AL.mult))
                    p.inc(v.tensor_copy(AUX[0:1, 18:19], AUX[0:1, 17:18]),
                          "bns", 1)

                @blk.scalar
                def _(s, g=g):
                    p.wge(s, "bns", g + 1)
                    s.activation(AUX[0:1, 19:20], AUX[0:1, 18:19], AF.Sqrt)
                    p.inc(s.mul(AUX[0:1, 20:21], AUX[0:1, 19:20], -0.125),
                          "bnt", 1)

                @blk.vector
                def _(v, g=g):
                    p.wge(v, "bnt", g + 1)
                    p.inc(v.tensor_copy(AUXB[0:1, 2:3], AUX[0:1, 20:21]),
                          "bnu", 1)

                @blk.tensor
                def _(t, g=g):
                    p.wge(t, "bnu", g + 1)
                    p.inc(t.matmul(PNC[0:64, 1:2], ONE1[0:1, 0:64],
                                   AUXB[0:1, 2:3], start=True, stop=True,
                                   skip_group_check=True), "bnb", 1)

                @blk.vector
                def _(v, g=g):
                    p.wge(v, "bnb", g + 1)
                    p.inc(v.tensor_copy(NEGCCB[:, :], PNC[0:64, 1:2]), "bnr", 1)

            with contextlib.ExitStack() as psB2:
                SC = [ps(psB2, nc, f"SC{g}_{i}", [128, 512], F32) for i in range(4)]
                ACCD = ps(psB2, nc, f"ACCD{g}", [65, 512], F32)
                ACCW = ps(psB2, nc, f"ACCW{g}", [65, 512], F32)
                ACCC = ps(psB2, nc, f"ACCC{g}", [65, 512], F32)
                MB = ps(psB2, nc, f"MB{g}", [128, 512], F32)

                for hj in range(4):
                    for tt in range(2):
                        it = p.cnt.get("_attn_it", 0)
                        p.cnt["_attn_it"] = it + 1
                        qx0 = 1024 * (2 * hj + half) + 512 * tt
                        qx1 = 1024 * (8 + 2 * hj + half) + 512 * tt
                        band = (tt, tt + 1)

                        @blk.tensor
                        def _(t, g=g, q4=q4, hj=hj, tt=tt, it=it, qx0=qx0,
                              qx1=qx1, band=band, KXa=KXa, KXb=KXb):
                            if it >= 1:
                                p.wge(t, "mgd", it)
                            if it == 0:
                                p.wge(t, "bk1", 1)
                                p.wge(t, "bnr", 1)
                                p.wge(t, "bve", 1)
                            for G in range(8):
                                for si in range(4):
                                    s4 = 4 * G + si
                                    if it >= 1 or G >= 1:
                                        pass
                                    t.matmul(SC[si][:, :],
                                             KXa[32 * q4:32 * (q4 + 1),
                                                 128 * s4:128 * (s4 + 1)],
                                             QXT[32 * q4:32 * (q4 + 1),
                                                 qx0:qx0 + 512],
                                             start=True, stop=False,
                                             tile_position=(32 * q4, 0),
                                             skip_group_check=True)
                                    ins = t.matmul(SC[si][:, :],
                                                   KXb[32 * q4:32 * (q4 + 1),
                                                       128 * s4:128 * (s4 + 1)],
                                                   QXT[32 * q4:32 * (q4 + 1),
                                                       qx1:qx1 + 512],
                                                   start=False, stop=True,
                                                   tile_position=(32 * q4, 0),
                                                   skip_group_check=True)
                                    if si == 3:
                                        p.inc(ins, "scg", 1)
                                p.wge(t, "xpg", 9 * it + G + 1)
                                for si in range(4):
                                    s4 = 4 * G + si
                                    ins = t.matmul(
                                        ACCD[:, :],
                                        bass.AP(VT, (s4 * 8 + g) * 65,
                                                [[32 * 8 * 65, 128], [1, 65]]),
                                        PT[:, 512 * si:512 * (si + 1)],
                                        start=(s4 == 0), stop=(s4 == 31),
                                        skip_group_check=True)
                                    if si == 3:
                                        p.inc(ins, "pvg", 1)
                                if G in band:
                                    p.wge(t, "wtg", 2 * it + (1 if G == band[0] else 2))
                                    for si in range(4):
                                        s4 = 4 * G + si
                                        ins = t.matmul(
                                            ACCW[:, :],
                                            bass.AP(VT, (s4 * 8 + g) * 65,
                                                    [[32 * 8 * 65, 128], [1, 65]]),
                                            PW[:, 512 * si:512 * (si + 1)],
                                            start=(G == band[0] and si == 0),
                                            stop=(G == band[1] and si == 3),
                                            skip_group_check=True)
                                        if si == 3:
                                            p.inc(ins, "wpg", 1)
                            p.wge(t, "xpg", 9 * it + 8)
                            t.matmul(SC[0][0:64, :],
                                     KCS0[32 * q4:32 * (q4 + 1), :],
                                     QXT[32 * q4:32 * (q4 + 1), qx0:qx0 + 512],
                                     start=True, stop=False,
                                     tile_position=(32 * q4, 0),
                                     skip_group_check=True)
                            p.inc(t.matmul(SC[0][0:64, :],
                                           KCS1[32 * q4:32 * (q4 + 1), :],
                                           QXT[32 * q4:32 * (q4 + 1),
                                               qx1:qx1 + 512],
                                           start=False, stop=True,
                                           tile_position=(32 * q4, 0),
                                           skip_group_check=True), "csq", 1)

                        @blk.scalar
                        def _(s, it=it, band=band):
                            for G in range(8):
                                p.wge(s, "scg", 8 * it + G + 1)
                                if it >= 1 or G >= 1:
                                    p.wge(s, "pvg", 8 * it + G)
                                    nwt = 2 * it + len([b for b in band if b < G])
                                    p.wge(s, "wtg", nwt)
                                    p.wge(s, "wpg", nwt)
                                for si in range(4):
                                    ins = s.activation(
                                        PT[:, 512 * si:512 * (si + 1)],
                                        SC[si][:, :], AF.Exp,
                                        bias=NEGCB[:, :], scale=0.125)
                                    if si == 3:
                                        p.inc(ins, "xpg", 1)
                            p.wge(s, "csq", it + 1)
                            p.wge(s, "pcv", it)
                            p.inc(s.activation(PTC[:, :], SC[0][0:64, :], AF.Exp,
                                               bias=NEGCCB[:, :], scale=0.125),
                                  "xpg", 1)

                        @blk.tensor
                        def _(t, it=it):
                            p.wge(t, "xpg", 9 * it + 9)
                            p.inc(t.matmul(ACCC[:, :], VCA[:, :], PTC[:, :],
                                           start=True, stop=True,
                                           skip_group_check=True), "pcv", 1)

                        @blk.vector
                        def _(v, it=it, band=band, tt=tt):
                            for bi, G in enumerate(band):
                                p.wge(v, "xpg", 9 * it + G + 1)
                                p.wge(v, "pvg", 8 * it + G + 1)
                                for si in range(4):
                                    s4 = 4 * G + si
                                    r = 8 * tt + s4 - 4 * tt
                                    ins = v.tensor_tensor(
                                        PW[:, 512 * si:512 * (si + 1)],
                                        PT[:, 512 * si:512 * (si + 1)],
                                        MASKS[:, 512 * r:512 * (r + 1)],
                                        op=AL.mult)
                                    if si == 3:
                                        p.inc(ins, "wtg", 1)

                        @blk.vector
                        def _(v, it=it, tt=tt):
                            p.wge(v, "pvg", 8 * (it + 1))
                            p.wge(v, "wpg", 2 * (it + 1))
                            p.wge(v, "pcv", it + 1)
                            for b, ACC in [(1, ACCD), (2, ACCW), (0, ACCC)]:
                                p.vfence(v, v.reciprocal(RR2[0:1, :],
                                                         ACC[64:65, :]))
                                p.inc(v.tensor_tensor(
                                    SRB[0:1, 512 * b:512 * (b + 1)],
                                    RR2[0:1, :],
                                    GN[0:1, 1024 * b + 512 * tt:
                                       1024 * b + 512 * (tt + 1)],
                                    op=AL.mult), "msr", 1)

                        @blk.tensor
                        def _(t, it=it):
                            for bi in range(3):
                                b = [1, 2, 0][bi]
                                p.wge(t, "msr", 3 * it + bi + 1)
                                if it or bi:
                                    p.wge(t, "mbe", 3 * it + bi)
                                p.inc(t.matmul(MB[:, :], ONE1[:, :],
                                               SRB[0:1, 512 * b:512 * (b + 1)],
                                               start=True, stop=True,
                                               skip_group_check=True), "mbq", 1)

                        @blk.vector
                        def _(v, it=it):
                            if it >= 2:
                                p.wge(v, "sost", 16 * (it - 1))
                            for bi, ACC in enumerate([ACCD, ACCW, ACCC]):
                                p.wge(v, "mbq", 3 * it + bi + 1)
                                p.inc(v.tensor_copy(MBC[:, :], MB[:, :]),
                                      "mbe", 1)
                                if bi == 0:
                                    v.tensor_tensor(SOT[:, :], ACC[0:64, :],
                                                    MBC[0:64, :], op=AL.mult)
                                elif bi == 1:
                                    v.tensor_tensor(MBC[0:64, :], ACC[0:64, :],
                                                    MBC[0:64, :], op=AL.mult)
                                    v.tensor_tensor(SOT[:, :], SOT[:, :],
                                                    MBC[0:64, :], op=AL.add)
                                else:
                                    v.tensor_tensor(MBC[0:64, :], ACC[0:64, :],
                                                    MBC[0:64, :], op=AL.mult)
                                    p.inc(v.tensor_tensor(SOST[it % 2][:, :],
                                                          SOT[:, :],
                                                          MBC[0:64, :],
                                                          op=AL.add), "mgd", 1)

                        @blk.sync
                        def _(sy, it=it, hj=hj, tt=tt, g=g):
                            h = 4 * g + hj
                            p.wge(sy, "mgd", it + 1)
                            p.inc(sy.dma_start(
                                out=env["so_scr"][64 * h:64 * (h + 1),
                                                  512 * tt:512 * (tt + 1)],
                                in_=SOST[it % 2][:, :]), "sost", 16)

    with contextlib.ExitStack() as phC:
        WOS = [sb(phC, nc, f"WOS{i}", [128, 2048], BF16) for i in range(2)]
        SOC = sb(phC, nc, "SOC", [128, 16 * 512], BF16)
        S0E = sb(phC, nc, "S0E", [128, 2 * 512], BF16)
        pw = [ps(phC, nc, f"cpw{i}", [128, 512], F32) for i in range(4)]

        def stageB_gate(eng):
            for nm in ["mgd", "sost", "wtg", "mbe", "msr", "bnr", "bk1",
                       "bve", "xpg", "pcv", "pvg", "wpg", "bvf", "bkq"]:
                eng.wait_ge(p.sem(nm), p.n(nm))

        @blk.gpsimd
        def _(g):
            stageB_gate(g)
            for tt in range(2):
                for jc in range(16):
                    idx = tt * 16 + jc
                    if idx >= 2:
                        p.wge(g, "cso", idx - 1)
                    for kc in range(16):
                        p.inc(g.dma_start(
                            out=WOS[idx % 2][:, 128 * kc:128 * (kc + 1)],
                            in_=env["wo"][128 * kc:128 * (kc + 1),
                                          128 * jc:128 * (jc + 1)]),
                            f"cwo{idx % 2}", 16)

        for tt in range(2):
            @blk.sync
            def _(sy, tt=tt):
                if tt == 0:
                    stageB_gate(sy)
                sy.wait_ge(p.sem("sost"), p.n("sost"))
                if tt == 1:
                    p.wge(sy, "cso", 16)
                for kc in range(16):
                    p.inc(sy.dma_start(
                        out=SOC[:, 512 * kc:512 * (kc + 1)],
                        in_=env["so_scr"][128 * kc:128 * (kc + 1),
                                          512 * tt:512 * (tt + 1)]),
                        f"csl{tt}", 16)

            for jc in range(16):
                idx = tt * 16 + jc

                @blk.tensor
                def _(t, jc=jc, tt=tt, idx=idx):
                    if jc == 0:
                        p.wge(t, f"csl{tt}", 256)
                    p.wge(t, f"cwo{idx % 2}", 16 * 16 * (idx // 2 + 1))
                    if idx >= 4:
                        p.wge(t, "cse", idx - 3)
                    for kc in range(16):
                        ins = t.matmul(pw[idx % 4][:, :],
                                       WOS[idx % 2][:, 128 * kc:128 * (kc + 1)],
                                       SOC[:, 512 * kc:512 * (kc + 1)],
                                       start=(kc == 0), stop=(kc == 15))
                        if kc == 15:
                            p.inc(ins, "cso", 1)

                @blk.vector
                def _(v, idx=idx):
                    p.wge(v, "cso", idx + 1)
                    if idx >= 2:
                        p.wge(v, "cst0", 16 * (idx - 1))
                    p.inc(v.tensor_copy(S0E[:, 512 * (idx % 2):
                                            512 * (idx % 2 + 1)],
                                        pw[idx % 4][:, :]), "cse", 1)

                @blk.sync
                def _(sy, jc=jc, tt=tt, idx=idx):
                    p.wge(sy, "cse", idx + 1)
                    p.inc(sy.dma_start(
                        out=env["s0_scr"][128 * jc:128 * (jc + 1),
                                          512 * tt:512 * (tt + 1)],
                        in_=S0E[:, 512 * (idx % 2):512 * (idx % 2 + 1)]),
                        "cst0", 16)

        @blk.sync
        def _(sy):
            sy.wait_ge(p.sem("cst0"), p.n("cst0"))

    return



def emit_ffn0(p, env, ph):
    p.rebase_all()
    nc, blk = p.nc, p.block
    XNO = env["XNO"]

    GT = sb(ph, nc, "GT", [128, 16 * 16], BF16)
    SELB = sb(ph, nc, "SELB", [16, 16 * 128], BF16)
    IDF0 = sb(ph, nc, "IDF0", [128, 128], F32)
    ONE1F = sb(ph, nc, "ONE1F", [1, 128], BF16)
    ZR = sb(ph, nc, "ZR", [16, TOK], F32)
    ZT = sb(ph, nc, "ZT", [128, 8 * 16], F32)
    ET1 = sb(ph, nc, "ET1", [128, 16], F32)
    ET2 = sb(ph, nc, "ET2", [128, 16], F32)
    ZMK = sb(ph, nc, "ZMK", [128, 16], F32)
    WDT = sb(ph, nc, "WDT", [128, 8 * 16], F32)
    WD = sb(ph, nc, "WD", [16, TOK], BF16)
    FC = sb(ph, nc, "FC", [128, 24], F32)
    WB = sb(ph, nc, "WB", [128, 2 * 512], BF16)
    GUPB = sb(ph, nc, "GUPB", [128, 1], F32)
    GDNB = sb(ph, nc, "GDNB", [128, 1], F32)
    GSB = sb(ph, nc, "GSB", [1, 2], BF16)
    QW = [sb(ph, nc, f"QW{i}", [128, 1024], BF16) for i in range(2)]
    SW = [sb(ph, nc, f"SW{i}", [128, 1024], BF16) for i in range(2)]
    WUP = sb(ph, nc, "WUP", [128, 16 * 512], BF16)
    WDN = sb(ph, nc, "WDN", [128, 8 * 1024], BF16)
    RL = sb(ph, nc, "RL", [128, 512], BF16)
    RL2 = sb(ph, nc, "RL2", [128, 512], BF16)
    HH = sb(ph, nc, "HH", [128, 8 * 1024], BF16)
    YSB = sb(ph, nc, "YSB", [128, 16 * 1024], BF16)
    G0E = sb(ph, nc, "G0E", [128, 2 * 512], BF16)

    pr = [ps(ph, nc, f"fpr{i}", [128, 512], F32) for i in range(4)]
    pd = [ps(ph, nc, f"fpd{i}", [128, 512], F32) for i in range(2)]
    pt1 = ps(ph, nc, "fpt1", [128, 128], F32)
    pb = ps(ph, nc, "fpb", [128, 512], F32)

    def attn_gate(eng):
        for nm in ["cst0", "cse", "cso", "csl0", "csl1", "sost", "mgd"]:
            eng.wait_ge(p.sem(nm), p.n(nm))

    @blk.sync
    def _(sy):
        attn_gate(sy)
        p.inc(sy.dma_start(out=SELB[:, :], in_=env["selb"][:, :]), "cst", 16)
        p.inc(sy.dma_start(out=IDF0[:, :], in_=env["identf32"][:, :]),
              "cst", 16)

    @blk.gpsimd
    def _(g):
        for kc in range(16):
            p.inc(g.dma_start(out=GT[:, 16 * kc:16 * (kc + 1)],
                              in_=env["gw0t"][128 * kc:128 * (kc + 1), :]),
                  "fglg", 16)

    @blk.gpsimd
    def _(g):
        attn_gate(g)
        p.inc(g.memset(ONE1F[:, :], 1.0), "aon", 1)
        p.inc(g.dma_start(out=FC[0:1, 0:1], in_=env["gup"][:, :]), "acq", 16)
        p.inc(g.dma_start(out=FC[0:1, 1:2], in_=env["gdn"][:, :]), "acq", 16)

    @blk.vector
    def _(v):
        p.wge(v, "acq", 32)
        p.inc(v.tensor_copy(GSB[0:1, :], FC[0:1, 0:2]), "acb", 1)

    @blk.tensor
    def _(t):
        p.wge(t, "acb", 1)
        p.wge(t, "aon", 1)
        t.matmul(pb[:, 0:1], ONE1F[:, :], GSB[0:1, 0:1], start=True, stop=True,
                 skip_group_check=True)
        p.inc(t.matmul(pb[:, 1:2], ONE1F[:, :], GSB[0:1, 1:2], start=True,
                       stop=True, skip_group_check=True), "acp", 1)

    @blk.vector
    def _(v):
        p.wge(v, "acp", 1)
        v.tensor_copy(GUPB[:, :], pb[:, 0:1])
        p.inc(v.tensor_copy(GDNB[:, :], pb[:, 1:2]), "acv", 1)

    @blk.tensor
    def _(t):
        t.wait_ge(p.sem("cst"), p.n("cst"))
        t.wait_ge(p.sem("fglg"), p.n("fglg"))
        for tt in range(2):
            for kc in range(16):
                ins = t.matmul(pr[tt][:16, :], GT[:, 16 * kc:16 * (kc + 1)],
                               XNO[:, 1024 * kc + 512 * tt:
                                   1024 * kc + 512 * (tt + 1)],
                               start=(kc == 0), stop=(kc == 15))
                if kc == 15:
                    p.inc(ins, "agp", 1)

    @blk.vector
    def _(v):
        for tt in range(2):
            p.wge(v, "agp", tt + 1)
            p.inc(v.tensor_copy(ZR[:, 512 * tt:512 * (tt + 1)],
                                pr[tt][:16, :]), "age", 1)

    @blk.tensor
    def _(t):
        p.wge(t, "age", 2)
        for b8 in range(8):
            if b8 >= 1:
                p.wge(t, "acl", b8)
            p.inc(t.transpose(pt1[:, 0:16], ZR[:, 128 * b8:128 * (b8 + 1)],
                              IDF0[0:16, 0:16]), "act", 1)

    @blk.vector
    def _(v):
        for b8 in range(8):
            p.wge(v, "act", b8 + 1)
            p.inc(v.tensor_copy(ZT[:, 16 * b8:16 * (b8 + 1)],
                                bass.AP(pt1, 0, [[128, 128], [1, 16]])),
                  "acl", 1)

    for b8 in range(8):
        @blk.vector
        def _(v, b8=b8):
            p.wge(v, "acl", b8 + 1)
            if b8 >= 1:
                p.wge(v, "aqe", b8)
            zb = ZT[:, 16 * b8:16 * (b8 + 1)]
            p.vred(v, FC[:, 8:9], zb, AL.max)
            p.vfence(v, v.tensor_scalar(ET1[:, :], zb, FC[:, 8:9], None,
                                        op0=AL.is_equal))
            p.vfence(v, v.scalar_tensor_tensor(ZMK[:, :], ET1[:, :], -1e30,
                                               zb, op0=AL.mult, op1=AL.add))
            p.vred(v, FC[:, 9:10], ZMK[:, :], AL.max)
            p.vfence(v, v.tensor_scalar(ET2[:, :], zb, FC[:, 9:10], None,
                                        op0=AL.is_equal))
            v.tensor_scalar(FC[:, 10:11], FC[:, 8:9], -1.0, None, op0=AL.mult)
            p.inc(v.tensor_scalar(FC[:, 11:12], FC[:, 9:10], -1.0, None,
                                  op0=AL.mult), "st1", 1)

        @blk.scalar
        def _(s, b8=b8):
            p.wge(s, "st1", b8 + 1)
            s.activation(FC[:, 12:13], FC[:, 10:11], AF.Exp)
            s.activation(FC[:, 12:13], FC[:, 12:13], AF.Ln, bias=1.0)
            s.activation(FC[:, 13:14], FC[:, 11:12], AF.Exp)
            p.inc(s.activation(FC[:, 13:14], FC[:, 13:14], AF.Ln, bias=1.0),
                  "st2", 1)

        @blk.vector
        def _(v, b8=b8):
            p.wge(v, "st2", b8 + 1)
            p.inc(v.tensor_tensor(FC[:, 14:15], FC[:, 13:14], FC[:, 12:13],
                                  op=AL.subtract), "aqs", 1)

        @blk.scalar
        def _(s, b8=b8):
            p.wge(s, "aqs", b8 + 1)
            p.inc(s.activation(FC[:, 16:17], FC[:, 14:15], AF.Sigmoid),
                  "aqm", 1)

        @blk.vector
        def _(v, b8=b8):
            p.wge(v, "aqm", b8 + 1)
            p.vfence(v, v.tensor_scalar(FC[:, 17:18], FC[:, 16:17], -1.0, 1.0,
                                        op0=AL.mult, op1=AL.add))
            p.vfence(v, v.tensor_scalar(WDT[:, 16 * b8:16 * (b8 + 1)],
                                        ET1[:, :], FC[:, 16:17], None,
                                        op0=AL.mult))
            p.vfence(v, v.tensor_scalar(ZMK[:, :], ET2[:, :], FC[:, 17:18],
                                        None, op0=AL.mult))
            p.inc(v.tensor_tensor(WDT[:, 16 * b8:16 * (b8 + 1)],
                                  WDT[:, 16 * b8:16 * (b8 + 1)],
                                  ZMK[:, :], op=AL.add), "aqe", 1)

    @blk.tensor
    def _(t):
        for b8 in range(8):
            p.wge(t, "aqe", b8 + 1)
            if b8 >= 1:
                p.wge(t, "aqp", b8)
            p.inc(t.transpose(pt1[0:16, 0:128], WDT[:, 16 * b8:16 * (b8 + 1)],
                              IDF0[:, :]), "aqn", 1)

    @blk.vector
    def _(v):
        for b8 in range(8):
            p.wge(v, "aqn", b8 + 1)
            p.inc(v.tensor_copy(WD[:, 128 * b8:128 * (b8 + 1)],
                                pt1[0:16, 0:128]), "aqp", 1)

    for e in range(E0):
        @blk.tensor
        def _(t, e=e):
            if e == 0:
                p.wge(t, "aqp", 8)
                p.wge(t, "acv", 1)
            else:
                p.wge(t, "wtg", 16 * e)
            for tt in range(2):
                if e or tt:
                    p.wge(t, "mbe", 2 * e + tt)
                p.inc(t.matmul(pb[:, :], SELB[:, 128 * e:128 * (e + 1)],
                               WD[:, 512 * tt:512 * (tt + 1)],
                               start=True, stop=True, skip_group_check=True),
                      "mbq", 1)

        @blk.vector
        def _(v, e=e):
            for tt in range(2):
                p.wge(v, "mbq", 2 * e + tt + 1)
                p.inc(v.tensor_copy(WB[:, 512 * tt:512 * (tt + 1)], pb[:, :]),
                      "mbe", 1)

        for ih in range(2):
            eh = 2 * e + ih

            @blk.gpsimd
            def _(g, e=e, ih=ih, eh=eh):
                for kc in range(16):
                    par = kc % 2
                    if eh * 16 + kc >= 2:
                        p.wge(g, "bkc", eh * 16 + kc - 1)
                    if ih == 0 and e >= 1 and kc < 2:
                        p.wge(g, "bnq", 16 * e)
                    p.inc(g.dma_start(
                        out=QW[par][:, 0:512],
                        in_=env["qup"][e, 128 * kc:128 * (kc + 1),
                                       512 * ih:512 * (ih + 1)]),
                        ("wkv", "vfs")[par], 16)
                    p.inc(g.dma_start(
                        out=SW[par][:, 0:512],
                        in_=env["sup"][e, 128 * kc:128 * (kc + 1),
                                       512 * ih:512 * (ih + 1)]),
                        ("wkv", "vfs")[par], 16)

            @blk.vector
            def _(v, e=e, ih=ih, eh=eh):
                for kc in range(16):
                    par = kc % 2
                    p.wge(v, ("wkv", "vfs")[par], 32 * (eh * 8 + kc // 2 + 1))
                    if kc == 0:
                        p.wge(v, "scg", 8 * eh)
                    p.inc(v.scalar_tensor_tensor(
                        WUP[:, 512 * kc:512 * (kc + 1)],
                        QW[par][:, 0:512], -8.0, SW[par][:, 0:512],
                        op0=AL.add, op1=AL.mult), "bkc", 1)

            @blk.tensor
            def _(t, e=e, ih=ih, eh=eh):
                p.wge(t, "bkc", 16 * (eh + 1))
                for tt in range(2):
                    for ic2 in range(4):
                        idx = tt * 4 + ic2
                        gidx = eh * 8 + idx
                        if gidx >= 4:
                            p.wge(t, "xpg", gidx - 3)
                        for kc in range(16):
                            ins = t.matmul(pr[idx % 4][:, :],
                                           WUP[:, 512 * kc + 128 * ic2:
                                               512 * kc + 128 * (ic2 + 1)],
                                           XNO[:, 1024 * kc + 512 * tt:
                                               1024 * kc + 512 * (tt + 1)],
                                           start=(kc == 0), stop=(kc == 15))
                            if kc == 15:
                                p.inc(ins, "scg", 1)

            @blk.scalar
            def _(s, e=e, ih=ih, eh=eh):
                for idx in range(8):
                    gidx = eh * 8 + idx
                    p.wge(s, "scg", gidx + 1)
                    if gidx >= 1:
                        p.wge(s, "wtg", gidx)
                    p.inc(s.activation(RL[:, :], pr[idx % 4][:, :], AF.Relu,
                                       scale=GUPB[:, :]), "xpg", 1)

            @blk.vector
            def _(v, e=e, ih=ih, eh=eh):
                for idx in range(8):
                    tt, ic2 = idx // 4, idx % 4
                    ic = 4 * ih + ic2
                    gidx = eh * 8 + idx
                    p.wge(v, "xpg", gidx + 1)
                    if idx == 0 and ih == 0:
                        p.wge(v, "mbe", 2 * (e + 1))
                        p.wge(v, "pvg", 32 * e)
                    v.tensor_tensor(RL2[:, :], RL[:, :], RL[:, :], op=AL.mult)
                    p.inc(v.tensor_tensor(HH[:, 1024 * ic + 512 * tt:
                                              1024 * ic + 512 * (tt + 1)],
                                          RL2[:, :],
                                          WB[:, 512 * tt:512 * (tt + 1)],
                                          op=AL.mult), "wtg", 1)

        for oh in range(2):
            eo = 2 * e + oh

            @blk.gpsimd
            def _(g, e=e, oh=oh, eo=eo):
                if oh == 0:
                    p.wge(g, "bkc", 16 * 2 * (e + 1))
                for kc in range(8):
                    par = kc % 2
                    if eo * 8 + kc >= 2:
                        p.wge(g, "bnq", 8 * eo + kc - 1)
                    p.inc(g.dma_start(
                        out=QW[par][:, 0:1024],
                        in_=env["qdn"][e, 128 * kc:128 * (kc + 1),
                                       1024 * oh:1024 * (oh + 1)]),
                        ("bvl", "bcwg")[par], 16)
                    p.inc(g.dma_start(
                        out=SW[par][:, 0:1024],
                        in_=env["sdn"][e, 128 * kc:128 * (kc + 1),
                                       1024 * oh:1024 * (oh + 1)]),
                        ("bvl", "bcwg")[par], 16)

            @blk.vector
            def _(v, e=e, oh=oh, eo=eo):
                for kc in range(8):
                    par = kc % 2
                    p.wge(v, ("bvl", "bcwg")[par], 32 * (eo * 4 + kc // 2 + 1))
                    if kc == 0:
                        p.wge(v, "pvg", 16 * eo)
                    p.inc(v.scalar_tensor_tensor(
                        WDN[:, 1024 * kc:1024 * (kc + 1)],
                        QW[par][:, 0:1024], -8.0, SW[par][:, 0:1024],
                        op0=AL.add, op1=AL.mult), "bnq", 1)

            @blk.tensor
            def _(t, e=e, oh=oh, eo=eo):
                p.wge(t, "bnq", 8 * (eo + 1))
                if oh == 0:
                    p.wge(t, "wtg", 16 * (e + 1))
                for tt in range(2):
                    for oc2 in range(8):
                        idx = tt * 8 + oc2
                        gidx = eo * 16 + idx
                        if gidx >= 2:
                            p.wge(t, "mgd", gidx - 1)
                        for kc in range(8):
                            ins = t.matmul(pd[idx % 2][:, :],
                                           WDN[:, 1024 * kc + 128 * oc2:
                                               1024 * kc + 128 * (oc2 + 1)],
                                           HH[:, 1024 * kc + 512 * tt:
                                              1024 * kc + 512 * (tt + 1)],
                                           start=(kc == 0), stop=(kc == 7))
                            if kc == 7:
                                p.inc(ins, "pvg", 1)

            @blk.vector
            def _(v, e=e, oh=oh, eo=eo):
                for idx in range(16):
                    tt, oc2 = idx // 8, idx % 8
                    oc = 8 * oh + oc2
                    gidx = eo * 16 + idx
                    p.wge(v, "pvg", gidx + 1)
                    dst = YSB[:, 1024 * oc + 512 * tt:
                              1024 * oc + 512 * (tt + 1)]
                    if e == 0:
                        p.inc(v.tensor_copy(dst, pd[idx % 2][:, :]), "mgd", 1)
                    else:
                        p.inc(v.tensor_tensor(dst, dst, pd[idx % 2][:, :],
                                              op=AL.add), "mgd", 1)

    for idx in range(32):
        tt, oc = idx % 2, idx // 2

        @blk.scalar
        def _(s, idx=idx, tt=tt, oc=oc):
            oh, oc2 = oc // 8, oc % 8
            p.wge(s, "mgd", (2 * 15 + oh) * 16 + tt * 8 + oc2 + 1)
            if idx >= 2:
                p.wge(s, "cst0", 16 * (idx - 1))
            p.inc(s.activation(G0E[:, 512 * (idx % 2):512 * (idx % 2 + 1)],
                               YSB[:, 1024 * oc + 512 * tt:
                                   1024 * oc + 512 * (tt + 1)],
                               AF.Silu, scale=GDNB[:, :]), "msr", 1)

        @blk.sync
        def _(sy, idx=idx, tt=tt, oc=oc):
            p.wge(sy, "msr", idx + 1)
            p.inc(sy.dma_start(
                out=env["g0_scr"][128 * oc:128 * (oc + 1),
                                  512 * tt:512 * (tt + 1)],
                in_=G0E[:, 512 * (idx % 2):512 * (idx % 2 + 1)]), "cst0", 16)

    @blk.sync
    def _(sy):
        sy.wait_ge(p.sem("cst0"), p.n("cst0"))

    env["_dbg0"] = (WD, GUPB, GDNB, YSB)
    return



def emit_fm_ln(p, env, ph, SRC, DST, banks, tiles, gamma=None, beta=None):
    nc, blk = p.nc, p.block
    PSA, PSB, PBC = banks
    SQ1, ONE1R, ONECL, ONECLB, SRow = tiles

    @blk.gpsimd
    def _(g):
        p.inc(g.memset(ONECL[:, :], 1.0), "aon", 1)
        p.inc(g.memset(ONECLB[:, :], 1.0), "aon", 1)
        p.inc(g.memset(ONE1R[:, :], 1.0), "aon", 1)

    for tt in range(2):
        @blk.tensor
        def _(t, tt=tt):
            t.wait_ge(p.sem("aon"), p.n("aon"))
            if tt == 1:
                p.wge(t, "xn", 16)
            for cc in range(16):
                ins = t.matmul(PSA[0:1, :], ONECL[:, :],
                               SRC[:, 1024 * cc + 512 * tt:
                                   1024 * cc + 512 * (tt + 1)],
                               start=(cc == 0), stop=(cc == 15),
                               skip_group_check=True)
                if cc == 15:
                    p.inc(ins, "st1", 1)

        @blk.vector
        def _(v, tt=tt):
            for cc in range(16):
                if 16 * tt + cc >= 2:
                    p.wge(v, "aqm", 16 * tt + cc - 1)
                p.inc(v.tensor_tensor(SQ1[:, 512 * (cc % 2):512 * (cc % 2 + 1)],
                                      SRC[:, 1024 * cc + 512 * tt:
                                          1024 * cc + 512 * (tt + 1)],
                                      SRC[:, 1024 * cc + 512 * tt:
                                          1024 * cc + 512 * (tt + 1)],
                                      op=AL.mult), "aqs", 1)

        @blk.tensor
        def _(t, tt=tt):
            for cc in range(16):
                p.wge(t, "aqs", 16 * tt + cc + 1)
                ins = t.matmul(PSB[0:1, :], ONECLB[:, :],
                               SQ1[:, 512 * (cc % 2):512 * (cc % 2 + 1)],
                               start=(cc == 0), stop=(cc == 15),
                               skip_group_check=True)
                p.inc(ins, "aqm", 1)

        @blk.vector
        def _(v, tt=tt):
            p.wge(v, "st1", tt + 1)
            p.wge(v, "aqm", 16 * (tt + 1))
            p.vfence(v, v.tensor_scalar(SRow[0:1, 0:512], PSA[0:1, :],
                                        1.0 / D, None, op0=AL.mult))
            p.vfence(v, v.tensor_scalar(SRow[0:1, 512:1024], PSB[0:1, :],
                                        1.0 / D, None, op0=AL.mult))
            p.vfence(v, v.tensor_tensor(SRow[0:1, 1024:1536], SRow[0:1, 0:512],
                                        SRow[0:1, 0:512], op=AL.mult))
            p.inc(v.tensor_tensor(SRow[0:1, 1024:1536], SRow[0:1, 512:1024],
                                  SRow[0:1, 1024:1536], op=AL.subtract),
                  "lns", 1)

        @blk.scalar
        def _(s, tt=tt):
            p.wge(s, "lns", tt + 1)
            p.inc(s.activation(SRow[0:1, 1536:2048], SRow[0:1, 1024:1536],
                               AF.Sqrt, bias=np.float32(LN_EPS)), "lnq", 1)

        @blk.vector
        def _(v, tt=tt):
            p.wge(v, "lnq", tt + 1)
            p.vfence(v, v.reciprocal(SRow[0:1, 2048:2560],
                                     SRow[0:1, 1536:2048]))
            p.inc(v.tensor_copy(SQ1[0:1, 0:512], SRow[0:1, 0:512]), "lnb", 1)
            p.inc(v.tensor_copy(SQ1[0:1, 512:1024], SRow[0:1, 2048:2560]),
                  "lnb", 1)

        @blk.tensor
        def _(t, tt=tt):
            p.wge(t, "lnb", 2 * (tt + 1))
            t.matmul(PBC[:, :], ONE1R[:, :], SQ1[0:1, 0:512], start=True,
                     stop=True, skip_group_check=True)
            p.inc(t.matmul(PSA[:, :], ONE1R[:, :], SQ1[0:1, 512:1024],
                           start=True, stop=True, skip_group_check=True),
                  "acp", 1)

        @blk.vector
        def _(v, tt=tt):
            p.wge(v, "acp", tt + 1)
            for cc in range(16):
                sl = slice(1024 * cc + 512 * tt, 1024 * cc + 512 * (tt + 1))
                v.tensor_tensor(SQ1[:, 0:512], SRC[:, sl], PBC[:, :],
                                op=AL.subtract)
                if gamma is None:
                    p.inc(v.tensor_tensor(DST[:, sl], SQ1[:, 0:512],
                                          PSA[:, :], op=AL.mult), "xn", 1)
                else:
                    v.tensor_tensor(SQ1[:, 512:1024], SQ1[:, 0:512],
                                    PSA[:, :], op=AL.mult)
                    p.inc(v.tensor_scalar(DST[:, sl], SQ1[:, 512:1024],
                                          gamma[cc], beta[cc],
                                          op0=AL.mult, op1=AL.add), "xn", 1)

    return



def emit_combine0(p, env, ph):
    p.rebase_all()
    nc, blk = p.nc, p.block
    XNO = env["XNO"]

    XTM = [sb(ph, nc, f"XTM{i}", [128, D], F32) for i in range(2)]
    X1F = sb(ph, nc, "X1F", [128, 16 * TOK], F32)
    SGL = sb(ph, nc, "SGL", [128, 4 * TOK], BF16)
    IDFC = sb(ph, nc, "IDFC", [128, 128], F32)
    SQ1 = sb(ph, nc, "SQ1", [128, 2 * 512], BF16)
    ONE1R = sb(ph, nc, "ONE1R", [1, 128], BF16)
    ONECL = sb(ph, nc, "ONECL", [128, 1], F32)
    ONECLB = sb(ph, nc, "ONECLB", [128, 1], BF16)
    SRow = sb(ph, nc, "SRow", [1, 5 * 512], F32)

    ptp = [ps(ph, nc, f"cptp{i}", [128, 4 * 128], F32) for i in range(2)]
    PSA = ps(ph, nc, "cPSA", [128, 512], F32)
    PSB = ps(ph, nc, "cPSB", [128, 512], F32)
    PBC = ps(ph, nc, "cPBC", [128, 512], F32)

    def ffn0_gate(eng):
        for nm in ["cst0", "msr", "mgd", "bnq", "pvg", "bkc", "wkv", "vfs",
                   "bvl", "bcwg"]:
            eng.wait_ge(p.sem(nm), p.n(nm))

    ECH = env["ECH"]

    @blk.vector
    def _(v):
        v.wait_ge(p.sem("cst0"), p.n("cst0"))
        p.inc(v.tensor_copy(ECH[0:1, 0:1], env["ONESB"][0:1, 0:1]), "ech", 1)

    def ffn0_gate_g(eng):
        eng.wait_ge(p.sem("ech"), p.n("ech"))
        for nm in ["msr", "mgd", "bnq", "pvg", "bkc", "wkv", "vfs",
                   "bvl", "bcwg"]:
            eng.wait_ge(p.sem(nm), p.n(nm))

    @blk.sync
    def _(sy):
        ffn0_gate(sy)
        p.inc(sy.dma_start(out=IDFC[:, :], in_=env["identf32"][:, :]),
              "cst", 16)
        for u in range(8):
            if u >= 2:
                p.wge(sy, "tpq", 16 * (u - 1))
            p.inc(sy.dma_start(out=XTM[u % 2][:, :],
                               in_=env["x"][512 + 128 * u:
                                            512 + 128 * (u + 1), :]),
                  ("xl0", "xl1")[u % 2], 16)

    tpk = {"n": 0, "ke": 0}

    @blk.tensor
    def _(t):
        p.wge(t, "cst", 16)
        for u in range(8):
            p.wge(t, ("xl0", "xl1")[u % 2], 16 * (u // 2 + 1))
            for cc in range(16):
                k = tpk["n"]
                if k >= 8:
                    p.wge(t, "tpe", k - 4)
                bk, slt = (k // 4) % 2, k % 4
                ins = t.transpose(ptp[bk][:, 128 * slt:128 * (slt + 1)],
                                  XTM[u % 2][:, 128 * cc:128 * (cc + 1)],
                                  IDFC[:, :])
                p.inc(ins, "tpq", 1)
                tpk["n"] += 1

    @blk.scalar
    def _(s):
        for u in range(8):
            for cc in range(16):
                k = tpk["ke"]
                p.wge(s, "tpq", (k // 4 + 1) * 4)
                bk, slt = (k // 4) % 2, k % 4
                ins = s.copy(X1F[:, 1024 * cc + 128 * u:
                                 1024 * cc + 128 * (u + 1)],
                             ptp[bk][:, 128 * slt:128 * (slt + 1)])
                p.inc(ins, "tpe", 1)
                tpk["ke"] += 1

    @blk.gpsimd
    def _(g):
        ffn0_gate_g(g)
        for cc in range(16):
            sl = slice(2048 * (cc % 2), 2048 * (cc % 2) + 1024)
            sl2 = slice(2048 * (cc % 2) + 1024, 2048 * (cc % 2) + 2048)
            if cc >= 2:
                p.wge(g, "age", cc - 1)
            p.inc(g.dma_start(out=SGL[:, sl],
                              in_=env["s0_scr"][128 * cc:128 * (cc + 1), :]),
                  ("wkv", "vfs")[cc % 2], 16)
            p.inc(g.dma_start(out=SGL[:, sl2],
                              in_=env["g0_scr"][128 * cc:128 * (cc + 1), :]),
                  ("wkv", "vfs")[cc % 2], 16)

    @blk.vector
    def _(v):
        for cc in range(16):
            p.wge(v, ("wkv", "vfs")[cc % 2], 32 * (cc // 2 + 1))
            p.wge(v, "tpe", 128)
            sl = slice(2048 * (cc % 2), 2048 * (cc % 2) + 1024)
            sl2 = slice(2048 * (cc % 2) + 1024, 2048 * (cc % 2) + 2048)
            xsl = slice(1024 * cc, 1024 * (cc + 1))
            v.tensor_tensor(SGL[:, sl], SGL[:, sl], SGL[:, sl2], op=AL.mult)
            p.inc(v.tensor_tensor(X1F[:, xsl], X1F[:, xsl], SGL[:, sl],
                                  op=AL.add), "age", 1)

    @blk.sync
    def _(sy):
        for cc in range(16):
            p.wge(sy, "age", cc + 1)
            p.inc(sy.dma_start(out=env["x1_scr"][128 * cc:128 * (cc + 1), :],
                               in_=X1F[:, 1024 * cc:1024 * (cc + 1)]),
                  "cst0", 16)

    @blk.vector
    def _(v):
        p.wge(v, "age", 16)

    emit_fm_ln(p, env, ph, X1F, XNO, (PSA, PSB, PBC),
               (SQ1, ONE1R, ONECL, ONECLB, SRow))
    return



def emit_l1pre(p, env, ph):
    p.rebase_all()
    nc, blk = p.nc, p.block
    XNO = env["XNO"]
    IDB = env["IDB"]

    WQK = [sb(ph, nc, f"l1w{i}", [128, 2048], BF16) for i in range(2)]
    BCL = sb(ph, nc, "BCL", [128, 3 * 16], F32)
    EQK = sb(ph, nc, "EQK", [128, 16 * TOK], BF16)
    VS = sb(ph, nc, "VS", [128, 16 * TOK], BF16)
    ETM = sb(ph, nc, "ETM", [128, 8 * 128], BF16)
    VTM = sb(ph, nc, "VTM", [128, 8 * 128], BF16)
    KVP = sb(ph, nc, "KVP", [128, 16 * 128], F32)
    SCE = [sb(ph, nc, f"SCE{i}", [128, 512], BF16) for i in range(3)]
    UEV = sb(ph, nc, "UEV", [128, 2 * 512], BF16)

    pj = [ps(ph, nc, f"lpj{i}", [128, 512], F32) for i in range(4)]
    ptp = [ps(ph, nc, f"lptp{i}", [128, 8 * 128], BF16) for i in range(2)]
    pkv = ps(ph, nc, "lpkv", [128, 128], F32)

    rtp = SlotRing(p, "tp2", ptp)

    def comb_gate(eng):
        for nm in ["cst0", "xn", "tpe", "age", "acp"]:
            eng.wait_ge(p.sem(nm), p.n(nm))

    ECH = env["ECH"]

    @blk.vector
    def _(v):
        v.wait_ge(p.sem("cst0"), p.n("cst0"))
        p.inc(v.tensor_copy(ECH[0:1, 0:1], env["ONESB"][0:1, 0:1]), "ech", 1)

    def comb_gate_g(eng):
        eng.wait_ge(p.sem("ech"), p.n("ech"))
        for nm in ["xn", "tpe", "age", "acp"]:
            eng.wait_ge(p.sem(nm), p.n(nm))

    @blk.sync
    def _(sy):
        comb_gate(sy)
        for i, nm in enumerate(["l1_attn_bqk", "l1_attn_bv", "l1_attn_bu"]):
            for cc in range(16):
                p.inc(sy.dma_start(out=BCL[:, 16 * i + cc:16 * i + cc + 1],
                                   in_=env[nm][128 * cc:128 * (cc + 1), :]),
                      "cst", 16)

    for wi, (wname, sem) in enumerate([("l1_attn_wqk", "awq0"),
                                       ("l1_attn_wv", "awq0"),
                                       ("l1_attn_wu", "awq0")]):
        @blk.gpsimd
        def _(g, wi=wi, wname=wname):
            if wi == 0:
                comb_gate_g(g)
            for oc in range(16):
                idx = wi * 16 + oc
                if idx >= 2:
                    p.wge(g, "kpj", 2 * (idx - 1))
                for kc in range(16):
                    p.inc(g.dma_start(
                        out=WQK[idx % 2][:, 128 * kc:128 * (kc + 1)],
                        in_=env[wname][128 * kc:128 * (kc + 1),
                                       128 * oc:128 * (oc + 1)]),
                        ("wkv", "vfs")[idx % 2], 16)

        @blk.tensor
        def _(t, wi=wi):
            for oc in range(16):
                idx = wi * 16 + oc
                p.wge(t, ("wkv", "vfs")[idx % 2], 256 * (idx // 2 + 1))
                for tt in range(2):
                    if 2 * idx + tt >= 4:
                        p.wge(t, "age", 2 * idx + tt - 3)
                    for kc in range(16):
                        ins = t.matmul(pj[(2 * idx + tt) % 4][:, :],
                                       WQK[idx % 2][:, 128 * kc:128 * (kc + 1)],
                                       XNO[:, 1024 * kc + 512 * tt:
                                           1024 * kc + 512 * (tt + 1)],
                                       start=(kc == 0), stop=(kc == 15))
                        if kc == 15:
                            p.inc(ins, "kpj", 1)

        if wi == 0:
            @blk.vector
            def _(v, wi=wi):
                p.wge(v, "cst", 256)
                for idx2 in range(32):
                    oc, tt = idx2 // 2, idx2 % 2
                    p.wge(v, "kpj", idx2 + 1)
                    bank = pj[idx2 % 4]
                    v.tensor_scalar(SCE[0][:, :], bank[:, :],
                                    BCL[:, oc:oc + 1], 0.0,
                                    op0=AL.add, op1=AL.max)
                    v.tensor_scalar(SCE[1][:, :], bank[:, :],
                                    BCL[:, oc:oc + 1], None, op0=AL.add)
                    p.inc(v.tensor_tensor(SCE[2][:, :], SCE[1][:, :],
                                          SCE[0][:, :], op=AL.subtract),
                          "rpd", 1)
                    p.wge(v, "knq", idx2 + 1)
                    p.inc(v.scalar_tensor_tensor(
                        EQK[:, 1024 * oc + 512 * tt:1024 * oc + 512 * (tt + 1)],
                        SCE[1][:, :], -1.0, SCE[0][:, :],
                        op0=AL.add, op1=AL.add), "age", 1)

            @blk.scalar
            def _(s, wi=wi):
                for idx2 in range(32):
                    p.wge(s, "rpd", idx2 + 1)
                    p.inc(s.activation(SCE[1][:, :], SCE[2][:, :], AF.Exp),
                          "knq", 1)
        else:
            dstq = None
            @blk.scalar
            def _(s, wi=wi):
                for idx2 in range(32):
                    oc, tt = idx2 // 2, idx2 % 2
                    p.wge(s, "kpj", 32 * wi + idx2 + 1)
                    if wi == 2 and idx2 >= 2:
                        p.wge(s, "xl0", 16 * (idx2 - 1))
                    dst = (VS[:, 1024 * oc + 512 * tt:
                              1024 * oc + 512 * (tt + 1)] if wi == 1 else
                           UEV[:, 512 * (idx2 % 2):512 * (idx2 % 2 + 1)])
                    p.inc(s.activation(dst, pj[idx2 % 4][:, :], AF.Silu,
                                       bias=BCL[:, 16 * wi + oc:
                                                16 * wi + oc + 1]),
                          "age", 1)

            if wi == 2:
                @blk.sync
                def _(sy):
                    for idx2 in range(32):
                        oc, tt = idx2 // 2, idx2 % 2
                        p.wge(sy, "age", 64 + idx2 + 1)
                        p.inc(sy.dma_start(
                            out=env["u_scr"][128 * oc:128 * (oc + 1),
                                             512 * tt:512 * (tt + 1)],
                            in_=UEV[:, 512 * (idx2 % 2):512 * (idx2 % 2 + 1)]),
                            "xl0", 16)

    @blk.sync
    def _(sy):
        p.wge(sy, "age", 32)
        for cc in range(16):
            p.inc(sy.dma_start(out=env["qe_scr"][128 * cc:128 * (cc + 1), :],
                               in_=EQK[:, 1024 * cc:1024 * (cc + 1)]),
                  "cst0", 16)

    for h in range(16):
        @blk.tensor
        def _(t, h=h):
            if h == 0:
                p.wge(t, "age", 64)
            for tb in range(8):
                for src in (EQK, VS):
                    dstap = rtp.pre_transpose(t)
                    ins = t.transpose(dstap,
                                      src[:, 1024 * h + 128 * tb:
                                          1024 * h + 128 * (tb + 1)],
                                      IDB[:, :])
                    rtp.post_transpose(ins)

        @blk.vector
        def _(v, h=h):
            if h >= 1:
                p.wge(v, "csq", h)
            for tb in range(8):
                for j, dstt in enumerate((ETM, VTM)):
                    srcap = rtp.pre_evac(v)
                    ins = v.tensor_copy(dstt[:, 128 * tb:128 * (tb + 1)], srcap)
                    rtp.post_evac(ins)

        @blk.tensor
        def _(t, h=h):
            p.wge(t, "tp2e", 16 * (h + 1))
            if h >= 1:
                p.wge(t, "pcv", h)
            for tb in range(8):
                ins = t.matmul(pkv[:, :], ETM[:, 128 * tb:128 * (tb + 1)],
                               VTM[:, 128 * tb:128 * (tb + 1)],
                               start=(tb == 0), stop=(tb == 7))
                if tb == 7:
                    p.inc(ins, "csq", 1)

        @blk.vector
        def _(v, h=h):
            p.wge(v, "csq", h + 1)
            p.inc(v.tensor_copy(KVP[:, 128 * h:128 * (h + 1)], pkv[:, :]),
                  "pcv", 1)

    @blk.sync
    def _(sy):
        p.wge(sy, "pcv", 16)
        p.inc(sy.dma_start(out=env["kvp_scr"][:, :], in_=KVP[:, :]), "cst0", 16)
        sy.wait_ge(p.sem("cst0"), p.n("cst0"))
        sy.wait_ge(p.sem("xl0"), p.n("xl0"))
    return



def emit_ffn1(p, env, ph):
    p.rebase_all()
    nc, blk = p.nc, p.block
    XNO = env["XNO"]

    GT1 = sb(ph, nc, "GT1", [128, 16 * 16], BF16)
    SELB1 = sb(ph, nc, "SELB1", [16, 16 * 128], BF16)
    IDF1 = sb(ph, nc, "IDF1", [128, 128], F32)
    EBC = sb(ph, nc, "EBC", [16, 1], F32)
    ZR1 = sb(ph, nc, "ZR1", [16, TOK], F32)
    ZB1 = sb(ph, nc, "ZB1", [16, TOK], F32)
    ZT1 = sb(ph, nc, "ZT1", [128, 8 * 16], F32)
    ZTB = sb(ph, nc, "ZTB", [128, 8 * 16], F32)
    E1T = sb(ph, nc, "E1T", [128, 16], F32)
    E2T = sb(ph, nc, "E2T", [128, 16], F32)
    ZMK1 = sb(ph, nc, "ZMK1", [128, 16], F32)
    WDT1 = sb(ph, nc, "WDT1", [128, 8 * 16], F32)
    WD1 = sb(ph, nc, "WD1", [16, TOK], BF16)
    FC1 = sb(ph, nc, "FC1", [128, 16], F32)
    WB1 = sb(ph, nc, "WB1", [128, 2 * 512], BF16)
    WS1 = [sb(ph, nc, f"WS1_{i}", [128, 2048], BF16) for i in range(2)]
    GS1 = sb(ph, nc, "GS1", [128, 8 * 1024], BF16)
    HH1 = sb(ph, nc, "HH1", [128, 8 * 1024], BF16)
    YSB1 = sb(ph, nc, "YSB1", [128, 16 * 1024], BF16)
    G1E = sb(ph, nc, "G1E", [128, 2 * 512], BF16)

    pr1 = [ps(ph, nc, f"gpr{i}", [128, 512], F32) for i in range(4)]
    pd1 = [ps(ph, nc, f"gpd{i}", [128, 512], F32) for i in range(2)]
    pt2 = ps(ph, nc, "gpt2", [128, 128], F32)
    pb2 = ps(ph, nc, "gpb2", [128, 512], F32)

    def l1pre_gate(eng):
        for nm in ["cst0", "xl0", "pcv", "csq", "age", "kpj", "wkv", "vfs",
                   "cst"]:
            eng.wait_ge(p.sem(nm), p.n(nm))

    ECH = env["ECH"]

    @blk.vector
    def _(v):
        for nm in ["cst0", "xl0", "cst"]:
            v.wait_ge(p.sem(nm), p.n(nm))
        p.inc(v.tensor_copy(ECH[0:1, 0:1], env["ONESB"][0:1, 0:1]), "ech", 1)

    def l1pre_gate_g(eng):
        eng.wait_ge(p.sem("ech"), p.n("ech"))
        for nm in ["pcv", "csq", "age", "kpj", "wkv", "vfs"]:
            eng.wait_ge(p.sem(nm), p.n(nm))

    @blk.sync
    def _(sy):
        l1pre_gate(sy)
        p.inc(sy.dma_start(out=SELB1[:, :], in_=env["selb"][:, :]), "cst", 16)
        p.inc(sy.dma_start(out=IDF1[:, :], in_=env["identf32"][:, :]),
              "cst", 16)
        p.inc(sy.dma_start(out=EBC[:, :], in_=env["l1_ffn_expert_bias"][:, :]),
              "cst", 16)

    @blk.gpsimd
    def _(g):
        l1pre_gate_g(g)
        for kc in range(16):
            p.inc(g.dma_start(out=GT1[:, 16 * kc:16 * (kc + 1)],
                              in_=env["l1_ffn_gate_w"][128 * kc:128 * (kc + 1), :]),
                  "bvl", 16)

    @blk.tensor
    def _(t):
        p.wge(t, "bvl", 256)
        p.wge(t, "cst", 48)
        for tt in range(2):
            for kc in range(16):
                ins = t.matmul(pr1[tt][:16, :], GT1[:, 16 * kc:16 * (kc + 1)],
                               XNO[:, 1024 * kc + 512 * tt:
                                   1024 * kc + 512 * (tt + 1)],
                               start=(kc == 0), stop=(kc == 15))
                if kc == 15:
                    p.inc(ins, "agp", 1)

    @blk.scalar
    def _(s):
        for tt in range(2):
            p.wge(s, "agp", tt + 1)
            p.inc(s.activation(ZR1[:, 512 * tt:512 * (tt + 1)], pr1[tt][:16, :],
                               AF.Sigmoid), "age", 1)

    @blk.vector
    def _(v):
        for tt in range(2):
            p.wge(v, "age", tt + 1)
            p.inc(v.tensor_scalar(ZB1[:, 512 * tt:512 * (tt + 1)],
                                  ZR1[:, 512 * tt:512 * (tt + 1)],
                                  EBC[:, :], None, op0=AL.add), "st1", 1)

    @blk.tensor
    def _(t):
        for b8 in range(8):
            p.wge(t, "st1", b8 // 4 + 1)
            if b8 >= 1:
                p.wge(t, "acl", 2 * b8)
            p.inc(t.transpose(pt2[:, 0:16], ZR1[:, 128 * b8:128 * (b8 + 1)],
                              IDF1[0:16, 0:16]), "act", 1)
            p.inc(t.transpose(pt2[:, 16:32], ZB1[:, 128 * b8:128 * (b8 + 1)],
                              IDF1[0:16, 0:16]), "act", 1)

    @blk.vector
    def _(v):
        for b8 in range(8):
            p.wge(v, "act", 2 * (b8 + 1))
            v.tensor_copy(ZT1[:, 16 * b8:16 * (b8 + 1)],
                          bass.AP(pt2, 0, [[128, 128], [1, 16]]))
            p.inc(v.tensor_copy(ZTB[:, 16 * b8:16 * (b8 + 1)],
                                bass.AP(pt2, 16, [[128, 128], [1, 16]])),
                  "acl", 2)

    for b8 in range(8):
        @blk.vector
        def _(v, b8=b8):
            p.wge(v, "acl", 2 * (b8 + 1))
            if b8 >= 1:
                p.wge(v, "aqe", b8)
            zb = ZTB[:, 16 * b8:16 * (b8 + 1)]
            zs = ZT1[:, 16 * b8:16 * (b8 + 1)]
            p.vred(v, FC1[:, 0:1], zb, AL.max)
            p.vfence(v, v.tensor_scalar(E1T[:, :], zb, FC1[:, 0:1], None,
                                        op0=AL.is_equal))
            p.vfence(v, v.scalar_tensor_tensor(ZMK1[:, :], E1T[:, :], -1e30,
                                               zb, op0=AL.mult, op1=AL.add))
            p.vred(v, FC1[:, 1:2], ZMK1[:, :], AL.max)
            p.vfence(v, v.tensor_scalar(E2T[:, :], zb, FC1[:, 1:2], None,
                                        op0=AL.is_equal))
            p.vfence(v, v.tensor_tensor(ZMK1[:, :], E1T[:, :], zs, op=AL.mult))
            p.vred(v, FC1[:, 2:3], ZMK1[:, :], AL.add)
            p.vfence(v, v.tensor_tensor(ZMK1[:, :], E2T[:, :], zs, op=AL.mult))
            p.vred(v, FC1[:, 3:4], ZMK1[:, :], AL.add)
            p.vfence(v, v.tensor_tensor(FC1[:, 4:5], FC1[:, 2:3], FC1[:, 3:4],
                                        op=AL.add))
            p.vfence(v, v.tensor_scalar(FC1[:, 4:5], FC1[:, 4:5], 1e-20, None,
                                        op0=AL.add))
            p.vfence(v, v.reciprocal(FC1[:, 5:6], FC1[:, 4:5]))
            p.vfence(v, v.tensor_tensor(FC1[:, 6:7], FC1[:, 2:3], FC1[:, 5:6],
                                        op=AL.mult))
            p.vfence(v, v.tensor_tensor(FC1[:, 7:8], FC1[:, 3:4], FC1[:, 5:6],
                                        op=AL.mult))
            p.vfence(v, v.tensor_scalar(WDT1[:, 16 * b8:16 * (b8 + 1)],
                                        E1T[:, :], FC1[:, 6:7], None,
                                        op0=AL.mult))
            p.vfence(v, v.tensor_scalar(ZMK1[:, :], E2T[:, :], FC1[:, 7:8],
                                        None, op0=AL.mult))
            p.inc(v.tensor_tensor(WDT1[:, 16 * b8:16 * (b8 + 1)],
                                  WDT1[:, 16 * b8:16 * (b8 + 1)],
                                  ZMK1[:, :], op=AL.add), "aqe", 1)

    @blk.tensor
    def _(t):
        for b8 in range(8):
            p.wge(t, "aqe", b8 + 1)
            if b8 >= 1:
                p.wge(t, "aqp", b8)
            p.inc(t.transpose(pt2[0:16, 0:128], WDT1[:, 16 * b8:16 * (b8 + 1)],
                              IDF1[:, :]), "aqn", 1)

    @blk.vector
    def _(v):
        for b8 in range(8):
            p.wge(v, "aqn", b8 + 1)
            p.inc(v.tensor_copy(WD1[:, 128 * b8:128 * (b8 + 1)],
                                pt2[0:16, 0:128]), "aqp", 1)

    for half, wname in [(0, "l1_ffn_shared_gate_w"), (1, "l1_ffn_shared_up_w")]:
        @blk.gpsimd
        def _(g, half=half, wname=wname):
            for oc in range(8):
                idx = half * 8 + oc
                if idx >= 2:
                    p.wge(g, "scg", 2 * (idx - 1))
                for kc in range(16):
                    p.inc(g.dma_start(
                        out=WS1[idx % 2][:, 128 * kc:128 * (kc + 1)],
                        in_=env[wname][128 * kc:128 * (kc + 1),
                                       128 * oc:128 * (oc + 1)]),
                        ("wkv", "vfs")[idx % 2], 16)

        @blk.tensor
        def _(t, half=half):
            for oc in range(8):
                idx = half * 8 + oc
                p.wge(t, ("wkv", "vfs")[idx % 2], 256 * (idx // 2 + 1))
                for tt in range(2):
                    gi = 2 * idx + tt
                    if gi >= 4:
                        p.wge(t, "xpg", gi - 3)
                    elif gi < 2:
                        p.wge(t, "age", gi + 1)
                    for kc in range(16):
                        ins = t.matmul(pr1[gi % 4][:, :],
                                       WS1[idx % 2][:, 128 * kc:128 * (kc + 1)],
                                       XNO[:, 1024 * kc + 512 * tt:
                                           1024 * kc + 512 * (tt + 1)],
                                       start=(kc == 0), stop=(kc == 15))
                        if kc == 15:
                            p.inc(ins, "scg", 1)

        if half == 0:
            @blk.scalar
            def _(s):
                for gi in range(16):
                    oc, tt = gi // 2, gi % 2
                    p.wge(s, "scg", gi + 1)
                    p.inc(s.activation(GS1[:, 1024 * oc + 512 * tt:
                                           1024 * oc + 512 * (tt + 1)],
                                       pr1[gi % 4][:, :], AF.Silu), "xpg", 1)
        else:
            @blk.vector
            def _(v):
                for gi in range(16):
                    oc, tt = gi // 2, gi % 2
                    p.wge(v, "scg", 16 + gi + 1)
                    sl = slice(1024 * oc + 512 * tt, 1024 * oc + 512 * (tt + 1))
                    p.inc(v.tensor_tensor(HH1[:, sl], GS1[:, sl],
                                          pr1[gi % 4][:, :], op=AL.mult),
                          "xpg", 1)

    @blk.gpsimd
    def _(g):
        for oc in range(16):
            if oc >= 2:
                p.wge(g, "bnq", 2 * (oc - 1))
            else:
                p.wge(g, "scg", 32)
            for kc in range(8):
                p.inc(g.dma_start(
                    out=WS1[oc % 2][:, 128 * kc:128 * (kc + 1)],
                    in_=env["l1_ffn_shared_down_w"][128 * kc:128 * (kc + 1),
                                                    128 * oc:128 * (oc + 1)]),
                    ("bvl", "bcwg")[oc % 2], 16)

    @blk.tensor
    def _(t):
        p.wge(t, "xpg", 32)
        for oc in range(16):
            p.wge(t, ("bvl", "bcwg")[oc % 2],
                  (256 if oc % 2 == 0 else 0) + 128 * (oc // 2 + 1))
            for tt in range(2):
                gi = 2 * oc + tt
                if gi >= 2:
                    p.wge(t, "bnr", gi - 1)
                for kc in range(8):
                    ins = t.matmul(pd1[gi % 2][:, :],
                                   WS1[oc % 2][:, 128 * kc:128 * (kc + 1)],
                                   HH1[:, 1024 * kc + 512 * tt:
                                       1024 * kc + 512 * (tt + 1)],
                                   start=(kc == 0), stop=(kc == 7))
                    if kc == 7:
                        p.inc(ins, "bnq", 1)

    @blk.vector
    def _(v):
        for gi in range(32):
            oc, tt = gi // 2, gi % 2
            p.wge(v, "bnq", gi + 1)
            p.inc(v.tensor_copy(YSB1[:, 1024 * oc + 512 * tt:
                                     1024 * oc + 512 * (tt + 1)],
                                pd1[gi % 2][:, :]), "bnr", 1)

    for e in range(E1):
        @blk.tensor
        def _(t, e=e):
            if e == 0:
                p.wge(t, "aqp", 8)
                p.wge(t, "bnr", 32)
            else:
                p.wge(t, "wtg", 16 * e)
            for tt in range(2):
                if e or tt:
                    p.wge(t, "mbe", 2 * e + tt)
                p.inc(t.matmul(pb2[:, :], SELB1[:, 128 * e:128 * (e + 1)],
                               WD1[:, 512 * tt:512 * (tt + 1)],
                               start=True, stop=True, skip_group_check=True),
                      "mbq", 1)

        @blk.vector
        def _(v, e=e):
            for tt in range(2):
                p.wge(v, "mbq", 2 * e + tt + 1)
                p.inc(v.tensor_copy(WB1[:, 512 * tt:512 * (tt + 1)],
                                    pb2[:, :]), "mbe", 1)

        for half in range(2):
            @blk.gpsimd
            def _(g, e=e, half=half):
                for oc in range(8):
                    idx = (2 * e + half) * 8 + oc
                    if idx >= 2:
                        p.wge(g, "bkc", 2 * (idx - 1))
                    if idx % 16 < 2:
                        if e == 0:
                            p.wge(g, "bnq", 32)
                        else:
                            p.wge(g, "pvg", 32 * e)
                    for kc in range(16):
                        p.inc(g.dma_start(
                            out=WS1[idx % 2][:, 128 * kc:128 * (kc + 1)],
                            in_=env["l1_ffn_experts_gate_up"][
                                e, 128 * kc:128 * (kc + 1),
                                1024 * half + 128 * oc:
                                1024 * half + 128 * (oc + 1)]),
                            ("wkv", "vfs")[idx % 2], 16)

            @blk.tensor
            def _(t, e=e, half=half):
                for oc in range(8):
                    idx = (2 * e + half) * 8 + oc
                    p.wge(t, ("wkv", "vfs")[idx % 2],
                          256 * 8 + 256 * (idx // 2 + 1))
                    for tt in range(2):
                        j = 2 * (half * 8 + oc) + tt
                        if e == 0 and j < 4:
                            p.wge(t, "xpg", 29 + j)
                        elif j < 4:
                            p.wge(t, "wtg", 16 * e - 3 + j)
                        elif j < 20:
                            p.wge(t, "bkq", 16 * e + j - 3)
                        else:
                            p.wge(t, "wtg", 16 * e + j - 19)
                        for kc in range(16):
                            ins = t.matmul(pr1[j % 4][:, :],
                                           WS1[idx % 2][:, 128 * kc:
                                                        128 * (kc + 1)],
                                           XNO[:, 1024 * kc + 512 * tt:
                                               1024 * kc + 512 * (tt + 1)],
                                           start=(kc == 0), stop=(kc == 15))
                            if kc == 15:
                                p.inc(ins, "bkc", 1)

            if half == 0:
                @blk.scalar
                def _(s, e=e):
                    for gi16 in range(16):
                        oc, tt = gi16 // 2, gi16 % 2
                        gg = 32 * e + gi16
                        p.wge(s, "bkc", gg + 1)
                        p.inc(s.activation(GS1[:, 1024 * oc + 512 * tt:
                                               1024 * oc + 512 * (tt + 1)],
                                           pr1[gg % 4][:, :], AF.Silu),
                              "bkq", 1)
            else:
                @blk.vector
                def _(v, e=e):
                    for gi16 in range(16):
                        oc, tt = gi16 // 2, gi16 % 2
                        gg = 32 * e + 16 + gi16
                        p.wge(v, "bkc", gg + 1)
                        if gi16 == 0:
                            p.wge(v, "mbe", 2 * (e + 1))
                            p.wge(v, "pvg", 32 * e)
                        sl = slice(1024 * oc + 512 * tt,
                                   1024 * oc + 512 * (tt + 1))
                        v.tensor_tensor(GS1[:, sl] if False else
                                        HH1[:, sl], GS1[:, sl],
                                        pr1[gg % 4][:, :], op=AL.mult)
                        p.inc(v.tensor_tensor(HH1[:, sl], HH1[:, sl],
                                              WB1[:, 512 * tt:512 * (tt + 1)],
                                              op=AL.mult), "wtg", 1)

        @blk.gpsimd
        def _(g, e=e):
            for oc in range(16):
                idx = e * 16 + oc
                if idx >= 2:
                    p.wge(g, "pvg", 2 * (idx - 1))
                if oc < 2:
                    p.wge(g, "bkc", 32 * (e + 1))
                for kc in range(8):
                    p.inc(g.dma_start(
                        out=WS1[idx % 2][:, 128 * kc:128 * (kc + 1)],
                        in_=env["l1_ffn_experts_down"][
                            e, 128 * kc:128 * (kc + 1),
                            128 * oc:128 * (oc + 1)]),
                        ("bvl", "bcwg")[idx % 2], 16)

        @blk.tensor
        def _(t, e=e):
            p.wge(t, "wtg", 16 * (e + 1))
            for oc in range(16):
                idx = e * 16 + oc
                p.wge(t, ("bvl", "bcwg")[idx % 2],
                      (1280 if idx % 2 == 0 else 1024) + 128 * (idx // 2 + 1))
                for tt in range(2):
                    gi = 2 * idx + tt
                    if 32 * e + 2 * oc + tt >= 2:
                        p.wge(t, "mgd", 32 * e + 2 * oc + tt - 1)
                    else:
                        p.wge(t, "bnr", 31 + 2 * oc + tt)
                    for kc in range(8):
                        ins = t.matmul(pd1[gi % 2][:, :],
                                       WS1[idx % 2][:, 128 * kc:128 * (kc + 1)],
                                       HH1[:, 1024 * kc + 512 * tt:
                                           1024 * kc + 512 * (tt + 1)],
                                       start=(kc == 0), stop=(kc == 7))
                        if kc == 7:
                            p.inc(ins, "pvg", 1)

        @blk.vector
        def _(v, e=e):
            for gi in range(32):
                oc, tt = gi // 2, gi % 2
                p.wge(v, "pvg", 32 * e + gi + 1)
                sl = slice(1024 * oc + 512 * tt, 1024 * oc + 512 * (tt + 1))
                p.inc(v.tensor_tensor(YSB1[:, sl], YSB1[:, sl],
                                      pd1[gi % 2][:, :], op=AL.add), "mgd", 1)

    for gi in range(32):
        oc, tt = gi // 2, gi % 2

        @blk.scalar
        def _(s, gi=gi, oc=oc, tt=tt):
            p.wge(s, "mgd", 480 + gi + 1)
            if gi >= 2:
                p.wge(s, "cst0", 16 * (gi - 1))
            p.inc(s.activation(G1E[:, 512 * (gi % 2):512 * (gi % 2 + 1)],
                               YSB1[:, 1024 * oc + 512 * tt:
                                    1024 * oc + 512 * (tt + 1)],
                               AF.Silu), "msr", 1)

        @blk.sync
        def _(sy, gi=gi, oc=oc, tt=tt):
            p.wge(sy, "msr", gi + 1)
            p.inc(sy.dma_start(
                out=env["g1_scr"][128 * oc:128 * (oc + 1),
                                  512 * tt:512 * (tt + 1)],
                in_=G1E[:, 512 * (gi % 2):512 * (gi % 2 + 1)]), "cst0", 16)

    @blk.sync
    def _(sy):
        sy.wait_ge(p.sem("cst0"), p.n("cst0"))

    return



def emit_l1post(p, env, ph):
    p.rebase_all()
    nc, blk = p.nc, p.block

    KVS = sb(ph, nc, "KVS", [128, 2048], F32)
    KVL = sb(ph, nc, "KVL", [128, 2 * 2048], F32)
    KVB = sb(ph, nc, "KVB", [128, 2048], BF16)
    QEL = [sb(ph, nc, f"QEL{i}", [128, TOK], BF16) for i in range(2)]
    OH = sb(ph, nc, "OH", [128, 16 * TOK], BF16)
    UL = sb(ph, nc, "UL", [128, 2 * TOK], BF16)
    GBC = sb(ph, nc, "GBC", [128, 2 * 16], F32)
    BOC = sb(ph, nc, "BOC", [128, 16], F32)
    WO1 = [sb(ph, nc, f"WO1_{i}", [128, 2048], BF16) for i in range(2)]
    X1L = sb(ph, nc, "X1L", [128, 2 * TOK], F32)
    G1L = sb(ph, nc, "G1L", [128, 2 * TOK], BF16)
    OTM = sb(ph, nc, "OTM", [128, 8 * D], F32)
    OFE = sb(ph, nc, "OFE", [128, 2 * 512], F32)
    SQ1 = sb(ph, nc, "SQ1b", [128, 2 * 512], BF16)
    ONE1R = sb(ph, nc, "ONE1Rb", [1, 128], BF16)
    ONECL = sb(ph, nc, "ONECLb", [128, 1], BF16)
    ONECLB = ONECL
    SRow = sb(ph, nc, "SRowb", [1, 5 * 512], F32)
    IDFB = sb(ph, nc, "IDFB", [128, 128], F32)

    pq = [ps(ph, nc, f"qpq{i}", [128, 512], F32) for i in range(2)]
    PSA = ps(ph, nc, "qPSA", [128, 512], F32)
    PSB = ps(ph, nc, "qPSB", [128, 512], F32)
    PBC = ps(ph, nc, "qPBC", [128, 512], F32)
    ptp = [ps(ph, nc, f"qptp{i}", [128, 4 * 128], F32) for i in range(2)]

    @blk.sync
    def _(sy):
        p.inc(sy.dma_start(out=KVL[:, 0:2048], in_=env["kvp_all"][0, :, :]),
              "xl0", 16)
        p.inc(sy.dma_start(out=KVL[:, 2048:4096], in_=env["kvp_all"][1, :, :]),
              "xl1", 16)
        p.inc(sy.dma_start(out=IDFB[:, :], in_=env["identf32"][:, :]),
              "cst", 16)
        for i, nm in enumerate(["l1_attn_ln_g", "l1_attn_ln_b"]):
            for cc in range(16):
                p.inc(sy.dma_start(out=GBC[:, 16 * i + cc:16 * i + cc + 1],
                                   in_=env[nm][128 * cc:128 * (cc + 1), :]),
                      "cst", 16)
        for cc in range(16):
            p.inc(sy.dma_start(out=BOC[:, cc:cc + 1],
                               in_=env["l1_attn_bo"][128 * cc:128 * (cc + 1), :]),
                  "cst", 16)

    @blk.vector
    def _(v):
        p.wge(v, "xl0", 16)
        p.wge(v, "xl1", 16)
        p.inc(v.tensor_tensor(KVS[:, :], KVL[:, 0:2048], KVL[:, 2048:4096],
                              op=AL.add), "kva", 1)

    @blk.sync
    def _(sy):
        p.wge(sy, "kva", 1)
        p.inc(sy.dma_start(out=KVL[:, 0:2048], in_=env["kvp_all"][2, :, :]),
              "xl0", 16)
        p.inc(sy.dma_start(out=KVL[:, 2048:4096], in_=env["kvp_all"][3, :, :]),
              "xl1", 16)

    @blk.vector
    def _(v):
        p.wge(v, "xl0", 32)
        p.wge(v, "xl1", 32)
        v.tensor_tensor(KVS[:, :], KVS[:, :], KVL[:, 0:2048], op=AL.add)
        v.tensor_tensor(KVS[:, :], KVS[:, :], KVL[:, 2048:4096], op=AL.add)
        p.inc(v.tensor_copy(KVB[:, :], KVS[:, :]), "kva", 1)

    @blk.gpsimd
    def _(g):
        for h in range(16):
            if h >= 2:
                p.wge(g, "kpj", 2 * (h - 1))
            p.inc(g.dma_start(out=QEL[h % 2][:, :],
                              in_=env["qe_scr"][128 * h:128 * (h + 1), :]),
                  ("wkv", "vfs")[h % 2], 16)

    @blk.tensor
    def _(t):
        p.wge(t, "kva", 2)
        for h in range(16):
            p.wge(t, ("wkv", "vfs")[h % 2], 16 * (h // 2 + 1))
            for tt in range(2):
                if 2 * h + tt >= 2:
                    p.wge(t, "age", 2 * h + tt - 1)
                ins = t.matmul(pq[tt][:, :], KVB[:, 128 * h:128 * (h + 1)],
                               QEL[h % 2][:, 512 * tt:512 * (tt + 1)],
                               start=True, stop=True, skip_group_check=True)
                p.inc(ins, "kpj", 1)

    @blk.vector
    def _(v):
        for h in range(16):
            for tt in range(2):
                p.wge(v, "kpj", 2 * h + tt + 1)
                p.inc(v.tensor_copy(OH[:, 1024 * h + 512 * tt:
                                        1024 * h + 512 * (tt + 1)],
                                    pq[tt][:, :]), "age", 1)

    @blk.vector
    def _(v):
        p.wge(v, "age", 32)
        v.wait_ge(p.sem("cst"), p.n("cst"))

    gamma = [GBC[:, cc:cc + 1] for cc in range(16)]
    beta = [GBC[:, 16 + cc:16 + cc + 1] for cc in range(16)]
    emit_fm_ln(p, env, ph, OH, OH, (PSA, PSB, PBC),
               (SQ1, ONE1R, ONECL, ONECLB, SRow), gamma=gamma, beta=beta)

    @blk.gpsimd
    def _(g):
        for cc in range(16):
            if cc >= 2:
                p.wge(g, "vpj", cc - 1)
            p.inc(g.dma_start(out=UL[:, 1024 * (cc % 2):1024 * (cc % 2 + 1)],
                              in_=env["u_scr"][128 * cc:128 * (cc + 1), :]),
                  ("bvl", "bcwg")[cc % 2], 16)

    @blk.vector
    def _(v):
        p.wge(v, "xn", 32)
        for cc in range(16):
            p.wge(v, ("bvl", "bcwg")[cc % 2], 16 * (cc // 2 + 1))
            sl = slice(1024 * cc, 1024 * (cc + 1))
            p.inc(v.tensor_tensor(OH[:, sl], OH[:, sl],
                                  UL[:, 1024 * (cc % 2):1024 * (cc % 2 + 1)],
                                  op=AL.mult), "vpj", 1)

    @blk.gpsimd
    def _(g):
        for oc in range(16):
            if oc >= 2:
                p.wge(g, "cso", 2 * (oc - 1))
            for kc in range(16):
                p.inc(g.dma_start(
                    out=WO1[oc % 2][:, 128 * kc:128 * (kc + 1)],
                    in_=env["l1_attn_wo"][128 * kc:128 * (kc + 1),
                                          128 * oc:128 * (oc + 1)]),
                    ("acq", "fglg")[oc % 2], 16)
            p.inc(g.dma_start(out=X1L[:, 1024 * (oc % 2):1024 * (oc % 2 + 1)],
                              in_=env["x1_scr"][128 * oc:128 * (oc + 1), :]),
                  ("acq", "fglg")[oc % 2], 16)
            p.inc(g.dma_start(out=G1L[:, 1024 * (oc % 2):1024 * (oc % 2 + 1)],
                              in_=env["g1_scr"][128 * oc:128 * (oc + 1), :]),
                  ("acq", "fglg")[oc % 2], 16)

    tpk = {"n": 0, "ke": 0}

    @blk.tensor
    def _(t):
        p.wge(t, "vpj", 16)

        def trans(gi):
            for q4 in range(4):
                k = tpk["n"]
                if k >= 8:
                    p.wge(t, "tpe", k - 4)
                bk, slt = (k // 4) % 2, k % 4
                ins = t.transpose(ptp[bk][:, 128 * slt:128 * (slt + 1)],
                                  OFE[:, 512 * (gi % 2) + 128 * q4:
                                      512 * (gi % 2) + 128 * (q4 + 1)],
                                  IDFB[:, :])
                p.inc(ins, "tpq", 1)
                tpk["n"] += 1

        for oc in range(16):
            p.wge(t, ("acq", "fglg")[oc % 2], 16 * 18 * (oc // 2 + 1))
            for tt in range(2):
                gi = 2 * oc + tt
                if gi >= 2:
                    p.wge(t, "cse", gi - 1)
                for kc in range(16):
                    ins = t.matmul(pq[gi % 2][:, :],
                                   WO1[oc % 2][:, 128 * kc:128 * (kc + 1)],
                                   OH[:, 1024 * kc + 512 * tt:
                                       1024 * kc + 512 * (tt + 1)],
                                   start=(kc == 0), stop=(kc == 15))
                    if kc == 15:
                        p.inc(ins, "cso", 1)
                if gi >= 2:
                    trans(gi - 2)
        p.wge(t, "cse", 31)
        trans(30)
        p.wge(t, "cse", 32)
        trans(31)

    @blk.vector
    def _(v):
        for oc in range(16):
            for tt in range(2):
                gi = 2 * oc + tt
                p.wge(v, "cso", gi + 1)
                if gi >= 2:
                    p.wge(v, "tpq", 4 * (gi - 1))
                xsl = slice(1024 * (oc % 2) + 512 * tt,
                            1024 * (oc % 2) + 512 * (tt + 1))
                v.tensor_scalar(SQ1[:, 0:512], pq[gi % 2][:, :],
                                BOC[:, oc:oc + 1], None, op0=AL.add)
                v.tensor_tensor(SQ1[:, 512:1024], SQ1[:, 0:512], G1L[:, xsl],
                                op=AL.mult)
                p.inc(v.tensor_tensor(OFE[:, 512 * (gi % 2):
                                          512 * (gi % 2 + 1)],
                                      X1L[:, xsl], SQ1[:, 512:1024],
                                      op=AL.add), "cse", 1)

    @blk.scalar
    def _(s):
        for oc in range(16):
            for tt in range(2):
                for q4 in range(4):
                    k = tpk["ke"]
                    p.wge(s, "tpq", (k // 4 + 1) * 4)
                    bk, slt = (k // 4) % 2, k % 4
                    tb = tt * 4 + q4
                    ins = s.copy(OTM[:, 2048 * tb + 128 * oc:
                                     2048 * tb + 128 * (oc + 1)],
                                 ptp[bk][:, 128 * slt:128 * (slt + 1)])
                    p.inc(ins, "tpe", 1)
                    tpk["ke"] += 1

    @blk.sync
    def _(sy):
        for tb in range(8):
            p.wge(sy, "tpe", 121 + tb)
            p.inc(sy.dma_start(out=env["o_out"][128 * tb:128 * (tb + 1), :],
                               in_=OTM[:, 2048 * tb:2048 * (tb + 1)]),
                  "fin", 16)
        sy.wait_ge(p.sem("fin"), p.n("fin"))

    return


_NPDT = {np.dtype(np.float32): F32, np.dtype(BF): BF16, np.dtype(np.int32): I32}


def declare_inputs(nc, env, sample, names):
    for n in names:
        a = np.asarray(sample[n])
        env[n] = nc.dram_tensor(n, list(a.shape), _NPDT[a.dtype],
                                kind="ExternalInput").ap()


PRELUDE_INS = ["x", "wkv", "cos", "sin", "identbf", "identf32", "onesblk"]
ATTN_INS = PRELUDE_INS + ["wq", "wgate", "wo", "wck0", "wck1", "wcv",
                          "masks", "rep32", "sel3"]


def build_l0_attn_test(sample):
    nc = bass.Bass("TRN2", target_bir_lowering=False)
    env = {}
    with contextlib.ExitStack() as st:
        declare_inputs(nc, env, sample, ATTN_INS)
        env["vfm_scr"] = nc.dram_tensor("vfm_scr", [512, T], BF16).ap()
        env["so_scr"] = nc.dram_tensor("so_scr", [2048, TOK], BF16).ap()
        env["s0_scr"] = nc.dram_tensor("s0_scr", [2048, TOK], BF16).ap()
        o_s0 = nc.dram_tensor("o_s0", [2048, TOK], BF16,
                              kind="ExternalOutput").ap()
        block = st.enter_context(nc.Block())
        p = Prog(nc, block, st)
        with contextlib.ExitStack() as ph1:
            emit_prelude(p, env, ph1)
        with contextlib.ExitStack() as ph:
            emit_attn(p, env, ph)

            @block.sync
            def _(sy):
                sy.wait_ge(p.sem("cst0"), p.n("cst0"))
                for jc in range(16):
                    p.inc(sy.dma_start(out=o_s0[128 * jc:128 * (jc + 1), :],
                                       in_=env["s0_scr"][128 * jc:128 * (jc + 1), :]),
                          "fin", 16)
                sy.wait_ge(p.sem("fin"), p.n("fin"))
    return nc


def build_prelude_test(sample):
    nc = bass.Bass("TRN2", target_bir_lowering=False)
    env = {}
    outs = {}
    with contextlib.ExitStack() as st:
        declare_inputs(nc, env, sample, PRELUDE_INS)
        env["vfm_scr"] = nc.dram_tensor("vfm_scr", [512, T], BF16).ap()
        for name, shape, dt in [
            ("o_stats", [128, 6 * 32], F32),
            ("o_xno", [128, 16 * TOK], BF16),
            ("o_kx0a", [128, T], BF16), ("o_kx0b", [128, T], BF16),
            ("o_kx1a", [128, T], BF16), ("o_kx1b", [128, T], BF16),
            ("o_vt", [128, 32 * 8 * 65], BF16), ("o_kmax", [4, 16], F32),
        ]:
            outs[name] = nc.dram_tensor(name, shape, dt, kind="ExternalOutput").ap()
        block = st.enter_context(nc.Block())
        p = Prog(nc, block, st)
        with contextlib.ExitStack() as ph:
            emit_prelude(p, env, ph)

            @block.sync
            def _(sy):
                sy.wait_ge(p.sem("xno"), p.n("xno"))
                sy.wait_ge(p.sem("tpe"), p.n("tpe"))
                sy.wait_ge(p.sem("knm"), p.n("knm"))
                stt = env["_stats"]
                for i, nm in enumerate(["s1", "s2", "mean", "nbias", "stdv", "rstd"]):
                    p.inc(sy.dma_start(out=outs["o_stats"][:, 32 * i:32 * (i + 1)],
                                       in_=stt[nm][:, :]), "fin", 16)
                for name, tile in [
                    ("o_xno", env["XNO"]),
                    ("o_kx0a", env["KX0"][0]), ("o_kx0b", env["KX0"][1]),
                    ("o_kx1a", env["KX1"][0]), ("o_kx1b", env["KX1"][1]),
                    ("o_vt", env["VT"]), ("o_kmax", env["KMAX"]),
                ]:
                    p.inc(sy.dma_start(out=outs[name][:, :], in_=tile[:, :]),
                          "fin", 16)
                sy.wait_ge(p.sem("fin"), p.n("fin"))
    return nc


FFN0_INS = ATTN_INS + ["gw0t", "selb", "qup", "sup", "qdn", "sdn",
                       "gup", "gdn"]


def build_l0_full_test(sample):
    nc = bass.Bass("TRN2", target_bir_lowering=False)
    env = {}
    with contextlib.ExitStack() as st:
        declare_inputs(nc, env, sample, FFN0_INS)
        env["vfm_scr"] = nc.dram_tensor("vfm_scr", [512, T], BF16).ap()
        env["so_scr"] = nc.dram_tensor("so_scr", [2048, TOK], BF16).ap()
        env["s0_scr"] = nc.dram_tensor("s0_scr", [2048, TOK], BF16).ap()
        env["g0_scr"] = nc.dram_tensor("g0_scr", [2048, TOK], BF16).ap()
        o_s0 = nc.dram_tensor("o_s0", [2048, TOK], BF16,
                              kind="ExternalOutput").ap()
        o_g0 = nc.dram_tensor("o_g0", [2048, TOK], BF16,
                              kind="ExternalOutput").ap()
        o_wd = nc.dram_tensor("o_wd", [16, TOK], BF16,
                              kind="ExternalOutput").ap()
        o_gs = nc.dram_tensor("o_gs", [128, 2], F32,
                              kind="ExternalOutput").ap()
        o_ysb = nc.dram_tensor("o_ysb", [128, 16 * 1024], BF16,
                               kind="ExternalOutput").ap()
        block = st.enter_context(nc.Block())
        p = Prog(nc, block, st)
        with contextlib.ExitStack() as ph1:
            emit_prelude(p, env, ph1)
        with contextlib.ExitStack() as ph2:
            emit_attn(p, env, ph2)
        with contextlib.ExitStack() as ph3:
            emit_ffn0(p, env, ph3)

            @block.sync
            def _(sy):
                sy.wait_ge(p.sem("cst0"), p.n("cst0"))
                sy.wait_ge(p.sem("fgo"), p.n("fgo"))
                WDd, GUPBd, GDNBd, YSBd = env["_dbg0"]
                p.inc(sy.dma_start(out=o_wd[:, :], in_=WDd[:, :]), "fin", 16)
                p.inc(sy.dma_start(out=o_gs[:, 0:1], in_=GUPBd[:, :],
                                   allow_slow_non_contiguous=True), "fin", 16)
                p.inc(sy.dma_start(out=o_gs[:, 1:2], in_=GDNBd[:, :],
                                   allow_slow_non_contiguous=True), "fin", 16)
                p.inc(sy.dma_start(out=o_ysb[:, :], in_=YSBd[:, :]), "fin", 16)
                for jc in range(16):
                    p.inc(sy.dma_start(out=o_s0[128 * jc:128 * (jc + 1), :],
                                       in_=env["s0_scr"][128 * jc:128 * (jc + 1), :]),
                          "fin", 16)
                    p.inc(sy.dma_start(out=o_g0[128 * jc:128 * (jc + 1), :],
                                       in_=env["g0_scr"][128 * jc:128 * (jc + 1), :]),
                          "fin", 16)
                sy.wait_ge(p.sem("fin"), p.n("fin"))
    return nc


LAUNCH_A_INS = ATTN_INS + [
    "gw0t", "selb", "qup", "sup", "qdn", "sdn", "gup", "gdn",
    "l1_attn_wqk", "l1_attn_wv", "l1_attn_wu",
    "l1_attn_bqk", "l1_attn_bv", "l1_attn_bu",
    "l1_ffn_gate_w", "l1_ffn_expert_bias", "l1_ffn_shared_gate_w",
    "l1_ffn_shared_up_w", "l1_ffn_shared_down_w",
    "l1_ffn_experts_gate_up", "l1_ffn_experts_down",
]
LAUNCH_B_INS = ["identf32", "l1_attn_wo", "l1_attn_bo",
                "l1_attn_ln_g", "l1_attn_ln_b"]


def build_launch_a(sample):
    nc = bass.Bass("TRN2", target_bir_lowering=False)
    env = {}
    with contextlib.ExitStack() as st:
        declare_inputs(nc, env, sample, LAUNCH_A_INS)
        env["vfm_scr"] = nc.dram_tensor("vfm_scr", [512, T], BF16).ap()
        env["so_scr"] = nc.dram_tensor("so_scr", [2048, TOK], BF16).ap()
        env["s0_scr"] = nc.dram_tensor("s0_scr", [2048, TOK], BF16).ap()
        env["g0_scr"] = nc.dram_tensor("g0_scr", [2048, TOK], BF16).ap()
        env["x1_scr"] = nc.dram_tensor("x1_scr", [2048, TOK], F32,
                                       kind="ExternalOutput").ap()
        env["qe_scr"] = nc.dram_tensor("qe_scr", [2048, TOK], BF16,
                                       kind="ExternalOutput").ap()
        env["u_scr"] = nc.dram_tensor("u_scr", [2048, TOK], BF16,
                                      kind="ExternalOutput").ap()
        env["g1_scr"] = nc.dram_tensor("g1_scr", [2048, TOK], BF16,
                                       kind="ExternalOutput").ap()
        env["kvp_scr"] = nc.dram_tensor("kvp_scr", [128, 2048], F32,
                                        kind="ExternalOutput").ap()
        block = st.enter_context(nc.Block())
        p = Prog(nc, block, st)
        with contextlib.ExitStack() as ph1:
            emit_prelude(p, env, ph1)
        with contextlib.ExitStack() as ph2:
            emit_attn(p, env, ph2)
        with contextlib.ExitStack() as ph3:
            emit_ffn0(p, env, ph3)
        with contextlib.ExitStack() as ph4:
            emit_combine0(p, env, ph4)
        with contextlib.ExitStack() as ph5:
            emit_l1pre(p, env, ph5)
        with contextlib.ExitStack() as ph6:
            emit_ffn1(p, env, ph6)

            @block.sync
            def _(sy):
                for nm in ["cst0", "xl0", "fin", "fgo"]:
                    sy.wait_ge(p.sem(nm), p.n(nm))
    return nc


def build_launch_b(sample_b):
    nc = bass.Bass("TRN2", target_bir_lowering=False)
    env = {}
    with contextlib.ExitStack() as st:
        declare_inputs(nc, env, sample_b, LAUNCH_B_INS)
        env["kvp_all"] = nc.dram_tensor("kvp_all", [4, 128, 2048], F32,
                                        kind="ExternalInput").ap()
        for nm, dt in [("qe_scr", BF16), ("u_scr", BF16), ("x1_scr", F32),
                       ("g1_scr", BF16)]:
            env[nm] = nc.dram_tensor(nm, [2048, TOK], dt,
                                     kind="ExternalInput").ap()
        env["o_out"] = nc.dram_tensor("o_out", [TOK, 2048], F32,
                                      kind="ExternalOutput").ap()
        block = st.enter_context(nc.Block())
        p = Prog(nc, block, st)
        with contextlib.ExitStack() as ph:
            emit_l1post(p, env, ph)
    return nc


_NC_CACHE = {}


def kernel(**inputs):
    in_maps = host_prepare(inputs)
    if "A" not in _NC_CACHE:
        _NC_CACHE["A"] = build_launch_a(in_maps[0])
    ncA = _NC_CACHE["A"]
    maps_a = [{k: m[k] for k in LAUNCH_A_INS} for m in in_maps]
    resA = run_bass_kernel_spmd(ncA, maps_a, core_ids=list(range(NCORE)))

    maps_b = []
    for c in range(NCORE):
        grp = range(4 * (c // 4), 4 * (c // 4) + 4)
        mb = {k: in_maps[c][k] for k in LAUNCH_B_INS}
        mb["kvp_all"] = np.ascontiguousarray(
            np.stack([resA.results[g]["kvp_scr"] for g in grp], 0))
        for nm in ["qe_scr", "u_scr", "x1_scr", "g1_scr"]:
            mb[nm] = np.ascontiguousarray(resA.results[c][nm])
        maps_b.append(mb)
    if "B" not in _NC_CACHE:
        _NC_CACHE["B"] = build_launch_b(maps_b[0])
    ncB = _NC_CACHE["B"]
    resB = run_bass_kernel_spmd(ncB, maps_b, core_ids=list(range(NCORE)))

    out = np.zeros((B, T, D), np.float32)
    for c in range(NCORE):
        b, qoff = c // 4, (c % 4) * TOK
        out[b, qoff:qoff + TOK] = resB.results[c]["o_out"]
    return out


# version 03
# speedup vs baseline: 1.1030x; 1.1030x over previous
import contextlib

import numpy as np
import ml_dtypes

import concourse.bass as bass
import concourse.mybir as mybir
from concourse.bass_utils import run_bass_kernel_spmd
from concourse.alu_op_type import AluOpType as AL

F32 = mybir.dt.float32
BF16 = mybir.dt.bfloat16
I32 = mybir.dt.int32
AF = mybir.ActivationFunctionType
AXX = mybir.AxisListType.X

BF = ml_dtypes.bfloat16

B, T, D = 2, 4096, 2048
NCORE = 8
TOK = 1024
H, HKV, DH = 32, 8, 64
BS, WIN = 64, 512
NB = T // BS
E0, I0, GRP0 = 16, 1024, 16
E1, F1 = 16, 1024
LN_EPS = 1e-5



def _rope_tables():
    inv = 1.0 / (10000.0 ** (np.arange(0, DH, 2, dtype=np.float32) / DH))
    t = np.arange(T, dtype=np.float32)
    fr = np.outer(t, inv)
    return np.cos(fr), np.sin(fr)


def _perm_qk_cols(n_heads):
    idx_x0, idx_x1 = [], []
    for h in range(n_heads):
        for i in range(DH // 2):
            idx_x0.append(h * DH + 2 * i)
            idx_x1.append(h * DH + 2 * i + 1)
    return np.array(idx_x0), np.array(idx_x1)


def _perm64():
    return np.array([2 * i for i in range(32)] + [2 * i + 1 for i in range(32)])


def _window_masks(qoff):
    masks = np.zeros((2, 8, 128, 512), np.float32)
    sl = np.arange(128)[:, None]
    tl = np.arange(512)[None, :]
    for tt in range(2):
        for r in range(8):
            d = 512 + tl - 128 * r - sl
            mk = ((d >= 0) & (d <= 511)).astype(np.float32)
            if qoff == 0:
                ks = 128 * (4 * tt + r) + sl
                mk = mk * (ks >= 512)
            masks[tt, r] = mk
    return masks.astype(BF)


def host_prepare(inp):
    x = np.asarray(inp["L0_attn_x"])
    cos, sin = _rope_tables()

    x0q, x1q = _perm_qk_cols(H)
    x0k, x1k = _perm_qk_cols(HKV)
    Wq = np.asarray(inp["L0_attn_W_q"])
    Wk = np.asarray(inp["L0_attn_W_k"])
    Wv = np.asarray(inp["L0_attn_W_v"])
    WKV = np.ascontiguousarray(
        np.concatenate([Wk[:, x0k], Wk[:, x1k], Wv], axis=1))
    qcols = np.zeros(2048, np.int64)
    for par in range(2):
        for j in range(4):
            for half in range(2):
                for q in range(4):
                    g = half * 4 + q
                    h = 4 * g + j
                    for i in range(32):
                        cp = par * 1024 + j * 256 + half * 128 + q * 32 + i
                        qcols[cp] = h * 64 + 2 * i + par
    WQ = np.ascontiguousarray(Wq[:, qcols])
    WGATE = np.ascontiguousarray(np.asarray(inp["L0_attn_W_gate"]))
    WO = np.ascontiguousarray(np.asarray(inp["L0_attn_W_o"]))

    p64 = _perm64()
    Wck = np.asarray(inp["L0_attn_W_ck"]).reshape(BS, DH, DH)
    Wck = Wck[:, p64, :][:, :, p64]
    wck0 = Wck[:, :32, :].transpose(1, 0, 2).reshape(32, BS * DH)
    wck1 = Wck[:, 32:, :].transpose(1, 0, 2).reshape(32, BS * DH)
    WCK0 = np.ascontiguousarray(np.tile(wck0, (4, 1)))
    WCK1 = np.ascontiguousarray(np.tile(wck1, (4, 1)))
    Wcv = np.asarray(inp["L0_attn_W_cv"]).reshape(BS, DH, DH)
    wcv = Wcv.transpose(1, 0, 2).reshape(DH, BS * DH)
    WCV = np.ascontiguousarray(np.tile(wcv, (2, 1)))

    GW0T = np.ascontiguousarray(np.asarray(inp["L0_ffn_gate_weight"]).T)
    qup = np.ascontiguousarray(np.asarray(inp["L0_ffn_fc_up_qweights"]))
    qdn = np.ascontiguousarray(np.asarray(inp["L0_ffn_fc_down_qweights"]))
    SUP = np.ascontiguousarray(
        np.repeat(np.asarray(inp["L0_ffn_fc_up_blocks_scale"]), GRP0, axis=1))
    SDN = np.ascontiguousarray(
        np.repeat(np.asarray(inp["L0_ffn_fc_down_blocks_scale"]), GRP0, axis=1))
    gup = np.ascontiguousarray(
        np.asarray(inp["L0_ffn_fc_up_global_scale"]).reshape(1, 1))
    gdn = np.ascontiguousarray(
        np.asarray(inp["L0_ffn_fc_down_global_scale"]).reshape(1, 1))

    l1 = {k.lower(): np.ascontiguousarray(np.asarray(inp[k])) for k in [
        "L1_attn_Wqk", "L1_attn_Wv", "L1_attn_Wu", "L1_attn_Wo",
        "L1_ffn_gate_w", "L1_ffn_shared_gate_w", "L1_ffn_shared_up_w",
        "L1_ffn_shared_down_w", "L1_ffn_experts_gate_up", "L1_ffn_experts_down",
    ]}
    vecs = {k.lower(): np.ascontiguousarray(np.asarray(inp[k]).reshape(-1, 1))
            for k in ["L1_attn_bqk", "L1_attn_bv", "L1_attn_bu", "L1_attn_bo",
                      "L1_attn_ln_g", "L1_attn_ln_b", "L1_ffn_expert_bias"]}

    ones_blk32 = np.zeros((128, 4), np.float32)
    for g in range(4):
        ones_blk32[32 * g:32 * (g + 1), g] = 1.0

    shared = dict(
        wkv=WKV, wq=WQ, wgate=WGATE, wo=WO,
        wck0=WCK0, wck1=WCK1, wcv=WCV,
        gw0t=GW0T, qup=qup, qdn=qdn, sup=SUP, sdn=SDN, gup=gup, gdn=gdn,

        identbf=np.eye(128, dtype=np.float32).astype(BF),
        sel3=np.eye(3, dtype=np.float32).astype(BF),
        selb=np.ascontiguousarray(np.repeat(np.eye(16, dtype=np.float32), 128,
                                            axis=1).astype(BF)),
        rep32=np.ascontiguousarray(np.tile(np.eye(32, dtype=np.float32), (1, 4)).astype(BF)),
        identf32=np.eye(128, dtype=np.float32),
        onesblk=ones_blk32.astype(BF),
        **l1, **vecs,
    )

    in_maps = []
    for c in range(NCORE):
        b, qoff = c // 4, (c % 4) * TOK
        shift = qoff - 512
        xr = np.roll(x[b], -shift, axis=0)
        pos = (np.arange(T) + shift) % T
        m = dict(shared)
        m["masks"] = np.ascontiguousarray(
            _window_masks(qoff).reshape(16 * 128, 512))
        m["x"] = np.ascontiguousarray(xr)
        m["cos"] = np.ascontiguousarray(np.tile(cos[pos].T, (4, 1)).astype(BF))
        m["sin"] = np.ascontiguousarray(np.tile(sin[pos].T, (4, 1)).astype(BF))
        in_maps.append(m)
    return in_maps



class Prog:

    def __init__(self, nc, block, stack):
        self.nc, self.block, self.stack = nc, block, stack
        self._sems = {}
        self.cnt = {}
        self.base = {}

    def sem(self, name):
        if name not in self._sems:
            self._sems[name] = self.stack.enter_context(self.nc.semaphore(name))
            self.cnt[name] = 0
        return self._sems[name]

    def inc(self, instr, name, k=1):
        instr.then_inc(self.sem(name), k)
        self.cnt[name] += k
        return self.cnt[name]

    def n(self, name):
        self.sem(name)
        return self.cnt[name]

    def rebase_all(self):
        for name in list(self.cnt):
            self.base[name] = self.cnt[name]

    def wge(self, eng, name, expr):
        eng.wait_ge(self.sem(name), self.base.get(name, 0) + expr)

    def vfence(self, v, ins):
        n = self.inc(ins, "vfence", 1)
        v.wait_ge(self.sem("vfence"), n)
        return ins

    def vred(self, v, out, in_, op, **kw):
        ins = v.tensor_reduce(out, in_, axis=AXX, op=op, **kw)
        n = self.inc(ins, "vfence", 1)
        v.wait_ge(self.sem("vfence"), n)
        return ins


def sb(stack, nc, name, shape, dtype):
    return stack.enter_context(nc.sbuf_tensor(name, shape, dtype))


def ps(stack, nc, name, shape, dtype=F32):
    return stack.enter_context(nc.psum_tensor(name, shape, dtype))


class SlotRing:

    def __init__(self, p, name, banks):
        self.p, self.banks = p, banks
        self.q = name + "q"
        self.e = name + "e"
        self.k = 0
        self.ke = 0

    def pre_transpose(self, eng):
        if self.k >= 16:
            eng.wait_ge(self.p.sem(self.e), self.k - 8)
        b, s = (self.k // 8) % 2, self.k % 8
        return self.banks[b][:, 128 * s:128 * (s + 1)]

    def post_transpose(self, instr):
        self.p.inc(instr, self.q, 1)
        self.k += 1

    def pre_evac(self, eng):
        eng.wait_ge(self.p.sem(self.q), (self.ke // 8 + 1) * 8)
        b, s = (self.ke // 8) % 2, self.ke % 8
        return self.banks[b][:, 128 * s:128 * (s + 1)]

    def post_evac(self, instr):
        self.p.inc(instr, self.e, 1)
        self.ke += 1



def emit_prelude(p, env, ph):
    p.rebase_all()
    nc, blk = p.nc, p.block
    st = p.stack

    env["KX0"] = [sb(st, nc, f"KX0_{i}", [128, T], BF16) for i in range(2)]
    env["KX1"] = [sb(st, nc, f"KX1_{i}", [128, T], BF16) for i in range(2)]
    env["VT"] = sb(st, nc, "VT", [128, 32 * 8 * 65], BF16)
    env["XNO"] = sb(st, nc, "XNO", [128, 16 * TOK], BF16)
    env["IDB"] = sb(st, nc, "IDB", [128, 128], BF16)
    env["ONESB"] = sb(st, nc, "ONESB", [128, 4], BF16)
    env["KMAX"] = sb(st, nc, "KMAX", [4, 16], F32)
    env["ECH"] = sb(st, nc, "ECH", [1, 2], F32)
    env["COS"] = sb(ph, nc, "COS", [128, T], BF16)
    env["SIN"] = sb(ph, nc, "SIN", [128, T], BF16)

    KX0, KX1, VT = env["KX0"], env["KX1"], env["VT"]
    XNO = env["XNO"]
    env["IDF"] = None
    COS, SIN, IDB = env["COS"], env["SIN"], env["IDB"]

    env["dbg_s1"] = None
    xtm = [sb(ph, nc, f"xtm{i}", [128, D], F32) for i in range(2)]
    env["_stats"] = (s1, s2, mean, nbias, stdv, rstd) if False else None
    xntm = [sb(ph, nc, f"xntm{i}", [128, D], BF16) for i in range(2)]
    s1 = sb(ph, nc, "s1", [128, 32], F32)
    s2 = sb(ph, nc, "s2", [128, 32], F32)
    mean = sb(ph, nc, "mean", [128, 32], F32)
    nbias = sb(ph, nc, "nbias", [128, 32], F32)
    stdv = sb(ph, nc, "stdv", [128, 32], F32)
    rstd = sb(ph, nc, "rstd", [128, 32], F32)
    WKVT = sb(ph, nc, "WKVT", [128, 16 * 1024], BF16)
    XNF = [sb(ph, nc, f"XNF{i}", [128, 16 * 512], BF16) for i in range(2)]
    vev = sb(ph, nc, "vev", [128, 4 * 512], BF16)
    rs0 = sb(ph, nc, "rs0", [128, 512], BF16)
    rs1 = sb(ph, nc, "rs1", [128, 512], BF16)
    ksq, ksqB = rs0, rs1

    env["_stats"] = dict(s1=s1, s2=s2, mean=mean, nbias=nbias, stdv=stdv,
                         rstd=rstd)
    pj = [ps(ph, nc, f"pj{i}", [128, 512], F32) for i in range(4)]
    ptp = [ps(ph, nc, f"ptp{i}", [128, 8 * 128], BF16) for i in range(2)]
    pkn = ps(ph, nc, "pkn", [4, 512], F32)

    OWN_TB = (1, 2)
    rtp = SlotRing(p, "tp", ptp)

    @blk.gpsimd
    def _(g):
        p.inc(g.memset(VT[:, :], 1.0), "vtm", 1)
        for kc in range(16):
            p.inc(g.dma_start(out=WKVT[:, 1024 * kc:1024 * (kc + 1)],
                              in_=env["wkv"][128 * kc:128 * (kc + 1), :]),
                  "wkv", 16)

    @blk.sync
    def _(sy):
        for name, dst in [("cos", COS), ("sin", SIN), ("identbf", IDB),
                          ("onesblk", env["ONESB"])]:
            p.inc(sy.dma_start(out=dst[:, :], in_=env[name][:, :]), "cst", 16)
        for u in range(32):
            if u >= 2:
                p.wge(sy, "xn", u - 1)
                p.wge(sy, "st2", u - 1)
                p.wge(sy, "tpq", 80 * ((u - 2) // 4) + 16 * ((u - 2) % 4 + 1))
            p.inc(sy.dma_start(out=xtm[u % 2][:, :],
                               in_=env["x"][128 * u:128 * (u + 1), :]),
                  f"xl{u % 2}", 16)

    for tb in range(8):
        u0 = tb * 4

        for uu in range(4):
            u = u0 + uu

            @blk.scalar
            def _(s, u=u):
                p.wge(s, f"xl{u % 2}", 16 * (u // 2 + 1))
                if u >= 2:
                    p.wge(s, "tpq", 80 * ((u - 2) // 4) + 16 * ((u - 2) % 4 + 1))
                s.activation(xntm[u % 2][:, :], xtm[u % 2][:, :], AF.Square,
                             accum_out=s2[:, u:u + 1])
                p.wge(s, "st1", u + 1)
                p.inc(s.activation(stdv[:, u:u + 1], s2[:, u:u + 1], AF.Sqrt,
                                   bias=nbias[:, u:u + 1], scale=1.0 / D),
                      "st2", 1)

            @blk.vector
            def _(v, u=u):
                p.wge(v, f"xl{u % 2}", 16 * (u // 2 + 1))
                p.vred(v, s1[:, u:u + 1], xtm[u % 2][:, :], AL.add)
                p.vfence(v, v.tensor_scalar(mean[:, u:u + 1], s1[:, u:u + 1],
                                            1.0 / D, None, op0=AL.mult))
                p.vfence(v, v.tensor_tensor(nbias[:, u:u + 1], mean[:, u:u + 1],
                                             mean[:, u:u + 1], op=AL.mult))
                p.inc(v.tensor_scalar(nbias[:, u:u + 1], nbias[:, u:u + 1],
                                      -1.0, LN_EPS, op0=AL.mult, op1=AL.add),
                      "st1", 1)
                p.wge(v, "st2", u + 1)
                p.vfence(v, v.reciprocal(rstd[:, u:u + 1], stdv[:, u:u + 1]))
                if u >= 2:
                    p.wge(v, "tpq", 80 * ((u - 2) // 4) + 16 * ((u - 2) % 4 + 1))
                p.inc(v.tensor_scalar(xntm[u % 2][:, :], xtm[u % 2][:, :],
                                      mean[:, u:u + 1], rstd[:, u:u + 1],
                                      op0=AL.subtract, op1=AL.mult), "xn", 1)

            @blk.tensor
            def _(t, tb=tb, u=u, uu=uu):
                if tb == 0 and uu == 0:
                    t.wait_ge(p.sem("cst"), p.n("cst"))
                if tb >= 2 and uu == 0:
                    p.wge(t, "vtd", 4 * (tb - 1))
                    p.wge(t, "xno", len([o for o in OWN_TB if o <= tb - 2]))
                p.wge(t, "xn", u + 1)
                for cc in range(16):
                    dstap = rtp.pre_transpose(t)
                    ins = t.transpose(dstap,
                                      xntm[u % 2][:, 128 * cc:128 * (cc + 1)],
                                      IDB[:, :])
                    rtp.post_transpose(ins)

            @blk.scalar
            def _(s, tb=tb, uu=uu):
                for cc in range(16):
                    srcap = rtp.pre_evac(s)
                    ins = s.copy(XNF[tb % 2][:, 512 * cc + 128 * uu:
                                             512 * cc + 128 * (uu + 1)],
                                 srcap)
                    rtp.post_evac(ins)

        @blk.tensor
        def _(t, tb=tb):
            p.wge(t, "tpe", 80 * tb + 64)
            if tb == 0:
                p.wge(t, "wkv", 256)
            if tb >= 1:
                p.wge(t, "vtd", 4 * tb)
            for kc in range(16):
                for oc in range(4):
                    ins = t.matmul(pj[oc][:, :],
                                   WKVT[:, 1024 * kc + 128 * oc:
                                        1024 * kc + 128 * (oc + 1)],
                                   XNF[tb % 2][:, 512 * kc:512 * (kc + 1)],
                                   start=(kc == 0), stop=(kc == 15))
                    if kc == 15:
                        p.inc(ins, "kpj", 1)

        @blk.vector
        def _(v, tb=tb):
            v.wait_ge(p.sem("kpj"), p.n("kpj"))
            sl = slice(512 * tb, 512 * (tb + 1))
            for half in range(2):
                px0, px1 = pj[0 + half], pj[2 + half]
                v.tensor_tensor(rs0[:, :], px0[:, :], COS[:, sl], op=AL.mult)
                v.tensor_tensor(rs1[:, :], px1[:, :], SIN[:, sl], op=AL.mult)
                v.tensor_tensor(KX0[half][:, sl], rs0[:, :], rs1[:, :],
                                op=AL.subtract)
                v.tensor_tensor(rs0[:, :], px0[:, :], SIN[:, sl], op=AL.mult)
                v.tensor_tensor(rs1[:, :], px1[:, :], COS[:, sl], op=AL.mult)
                p.inc(v.tensor_tensor(KX1[half][:, sl], rs0[:, :], rs1[:, :],
                                      op=AL.add), "rpd", 1)

        @blk.tensor
        def _(t, tb=tb):
            t.wait_ge(p.sem("rpd"), p.n("rpd"))
            for kc in range(16):
                for oc in range(4):
                    ins = t.matmul(pj[oc][:, :],
                                   WKVT[:, 1024 * kc + 128 * (4 + oc):
                                        1024 * kc + 128 * (5 + oc)],
                                   XNF[tb % 2][:, 512 * kc:512 * (kc + 1)],
                                   start=(kc == 0), stop=(kc == 15))
                    if kc == 15:
                        p.inc(ins, "vpj", 1)

        @blk.vector
        def _(v, tb=tb):
            v.wait_ge(p.sem("vpj"), p.n("vpj"))
            if tb >= 1:
                p.wge(v, "vfs", 64 * tb)
            for oc in range(4):
                p.inc(v.tensor_copy(vev[:, 512 * oc:512 * (oc + 1)],
                                    pj[oc][:, :]), "vtd", 1)

        @blk.gpsimd
        def _(g, tb=tb):
            g.wait_ge(p.sem("vtd"), p.n("vtd"))
            for oc in range(4):
                p.inc(g.dma_start(
                    out=env["vfm_scr"][128 * oc:128 * (oc + 1),
                                       512 * tb:512 * (tb + 1)],
                    in_=vev[:, 512 * oc:512 * (oc + 1)]), "vfs", 16)

        @blk.tensor
        def _(t, tb=tb):
            t.wait_ge(p.sem("vtd"), p.n("vtd"))
            for vc in range(4):
                for ts4 in range(4):
                    dstap = rtp.pre_transpose(t)
                    ins = t.transpose(dstap,
                                      vev[:, 512 * vc + 128 * ts4:
                                          512 * vc + 128 * (ts4 + 1)],
                                      IDB[:, :])
                    rtp.post_transpose(ins)

        @blk.scalar
        def _(s, tb=tb):
            if tb == 0:
                p.wge(s, "vtm", 1)
            for vc in range(4):
                for ts4 in range(4):
                    bk, sl = (rtp.ke // 8) % 2, rtp.ke % 8
                    rtp.pre_evac(s)
                    stile = tb * 4 + ts4
                    dst = bass.AP(VT, (stile * 8 + 2 * vc) * 65,
                                  [[32 * 8 * 65, 128], [65, 2], [1, 64]])
                    sap = bass.AP(ptp[bk], 128 * sl,
                                  [[8 * 128, 128], [64, 2], [1, 64]])
                    ins = s.copy(dst, sap)
                    rtp.post_evac(ins)

        if tb in OWN_TB:
            @blk.vector
            def _(v, tb=tb):
                p.wge(v, "tpe", 80 * tb + 64)
                ot = OWN_TB.index(tb)
                for cc in range(16):
                    ins = v.tensor_copy(XNO[:, 1024 * cc + 512 * ot:
                                            1024 * cc + 512 * (ot + 1)],
                                        XNF[tb % 2][:, 512 * cc:512 * (cc + 1)])
                    if cc == 15:
                        p.inc(ins, "xno", 1)

    for step in range(16):
        tsl, half = step // 2, step % 2
        sl = slice(512 * tsl, 512 * (tsl + 1))

        @blk.vector
        def _(v, step=step, tsl=tsl, half=half, sl=sl):
            if step >= 1:
                p.wge(v, "knq", step)
            t0, t1 = KX0[half], KX1[half]
            v.tensor_tensor(ksq[:, :], t0[:, sl], t0[:, sl], op=AL.mult)
            p.inc(v.tensor_tensor(ksqB[:, :], t1[:, sl], t1[:, sl],
                                  op=AL.mult), "ksqn", 1)

        @blk.tensor
        def _(t, step=step):
            p.wge(t, "ksqn", step + 1)
            if step >= 1:
                p.wge(t, "knm", step)
            t.matmul(pkn[:, :], env["ONESB"][:, :], ksq[:, :],
                     start=True, stop=False)
            p.inc(t.matmul(pkn[:, :], env["ONESB"][:, :],
                           ksqB[:, :], start=False, stop=True),
                  "knq", 1)

        @blk.vector
        def _(v, step=step):
            p.wge(v, "knq", step + 1)
            p.inc(v.tensor_reduce(env["KMAX"][:, step:step + 1],
                                  pkn[:, :], axis=AXX, op=AL.max), "knm", 1)
    return



def emit_attn(p, env, ph):
    p.rebase_all()
    nc, blk = p.nc, p.block
    XNO, VT = env["XNO"], env["VT"]
    KX0, KX1 = env["KX0"], env["KX1"]
    IDB, ONESB = env["IDB"], env["ONESB"]

    QXT = sb(ph, nc, "QXT", [128, 16 * TOK], BF16)
    COSQ = sb(ph, nc, "COSQ", [128, TOK], BF16)
    SINQ = sb(ph, nc, "SINQ", [128, TOK], BF16)
    GN = sb(ph, nc, "GN", [1, 3 * TOK], BF16)
    NEGCB = sb(ph, nc, "NEGCB", [128, 1], F32)
    IDF = sb(ph, nc, "IDF", [128, 128], F32)
    ONE1 = sb(ph, nc, "ONE1", [1, 128], BF16)
    QMX = sb(ph, nc, "QMX", [4, 16], F32)
    AUX = sb(ph, nc, "AUX", [128, 64], F32)
    AUXB = sb(ph, nc, "AUXB", [128, 16], BF16)

    with contextlib.ExitStack() as phA:
        WQS = [sb(phA, nc, f"WQS{i}", [128, 2 * 2048], BF16) for i in range(2)]
        WGT = sb(phA, nc, "WGT", [128, 48], BF16)
        SEL3 = sb(phA, nc, "SEL3", [3, 3], BF16)
        GEXP = sb(phA, nc, "GEXP", [3, TOK], BF16)
        rs0 = sb(phA, nc, "ars0", [128, 512], BF16)
        rs1 = sb(phA, nc, "ars1", [128, 512], BF16)
        QSQ, QSQ2 = rs0, rs1

        pq = [ps(phA, nc, f"apq{i}", [128, 512], F32) for i in range(4)]
        pg = ps(phA, nc, "apg", [4, 512], F32)
        pgs = ps(phA, nc, "apgs", [4, 512], F32)
        pb1 = ps(phA, nc, "apb1", [128, 512], F32)

        def prelude_gate(eng):
            eng.wait_ge(p.sem("knm"), p.n("knm"))
            eng.wait_ge(p.sem("knq"), p.n("knq"))
            eng.wait_ge(p.sem("tpe"), p.n("tpe"))
            eng.wait_ge(p.sem("vfs"), p.n("vfs"))
            eng.wait_ge(p.sem("xno"), p.n("xno"))

        @blk.gpsimd
        def _(g):
            prelude_gate(g)
            p.inc(g.dma_start(out=COSQ[:, :], in_=env["cos"][:, 512:1536]),
                  "acq", 16)
            p.inc(g.dma_start(out=SINQ[:, :], in_=env["sin"][:, 512:1536]),
                  "acq", 16)
            p.inc(g.memset(ONE1[:, :], 1.0), "aon", 1)
            p.inc(g.memset(AUXB[:, :], 1.0), "aob", 1)
            p.inc(g.memset(AUX[:, 8:10], 0.0), "aoz", 1)
            p.inc(g.dma_start(out=IDF[:, :], in_=env["identf32"][:, :]),
                  "aid", 16)
            p.inc(g.dma_start(out=SEL3[:, :], in_=env["sel3"][:, :]), "agw", 16)
            for kc in range(16):
                p.inc(g.dma_start(out=WGT[:, 3 * kc:3 * (kc + 1)],
                                  in_=env["wgate"][128 * kc:128 * (kc + 1), :]),
                      "agw", 16)

        @blk.tensor
        def _(t):
            t.wait_ge(p.sem("agw"), p.n("agw"))
            for tt in range(2):
                for kc in range(16):
                    ins = t.matmul(pg[:3, :], WGT[:, 3 * kc:3 * (kc + 1)],
                                   XNO[:, 1024 * kc + 512 * tt:
                                       1024 * kc + 512 * (tt + 1)],
                                   start=(kc == 0), stop=(kc == 15))
                    if kc == 15:
                        p.inc(ins, "agp", 1)
                if tt == 0:
                    p.wge(t, "age", 1)

        @blk.scalar
        def _(s):
            for tt in range(2):
                p.wge(s, "agp", tt + 1)
                p.inc(s.activation(GEXP[:, 512 * tt:512 * (tt + 1)], pg[:3, :],
                                   AF.Exp), "age", 1)

        @blk.tensor
        def _(t):
            p.wge(t, "aob", 1)
            p.wge(t, "age", 2)
            for tt in range(2):
                if tt == 1:
                    p.wge(t, "agn", 1)
                p.inc(t.matmul(pgs[:1, 0:512], AUXB[:3, 0:1],
                               GEXP[:3, 512 * tt:512 * (tt + 1)],
                               start=True, stop=True,
                               skip_group_check=True), "ags", 1)
                for b in range(3):
                    p.inc(t.matmul(pb1[:1, 0:512], SEL3[:, b:b + 1],
                                   GEXP[:3, 512 * tt:512 * (tt + 1)],
                                   start=True, stop=True,
                                   skip_group_check=True), "agr", 1)
                    if not (tt == 1 and b == 2):
                        p.wge(t, "agv", 3 * tt + b + 1)

        RR = sb(phA, nc, "RR", [1, 512], F32)

        @blk.vector
        def _(v):
            for tt in range(2):
                sl = slice(512 * tt, 512 * (tt + 1))
                for b in range(3):
                    p.wge(v, "agr", 3 * tt + b + 1)
                    p.inc(v.tensor_copy(GN[0:1, 1024 * b + 512 * tt:
                                        1024 * b + 512 * (tt + 1)],
                                        pb1[:1, 0:512]), "agv", 1)
                p.wge(v, "ags", tt + 1)
                p.vfence(v, v.reciprocal(RR[0:1, :], pgs[0:1, 0:512]))
                for b in range(3):
                    ins = v.tensor_tensor(GN[0:1, 1024 * b + 512 * tt:
                                          1024 * b + 512 * (tt + 1)],
                                          GN[0:1, 1024 * b + 512 * tt:
                                          1024 * b + 512 * (tt + 1)],
                                          RR[0:1, :], op=AL.mult)
                    if b == 2:
                        p.inc(ins, "agn", 1)

        @blk.gpsimd
        def _(g):
            for cpair in range(8):
                if cpair >= 2:
                    p.wge(g, "aqp", 4 * (cpair - 1))
                for kc in range(16):
                    for xi in range(2):
                        p.inc(g.dma_start(
                            out=WQS[cpair % 2][:, 2048 * xi + 128 * kc:
                                               2048 * xi + 128 * (kc + 1)],
                            in_=env["wq"][128 * kc:128 * (kc + 1),
                                          128 * (8 * xi + cpair):
                                          128 * (8 * xi + cpair + 1)]),
                            f"awq{cpair % 2}", 16)

        for cpair in range(8):
            for tt in range(2):
                step = 2 * cpair + tt
                qx0 = 1024 * cpair + 512 * tt
                qx1 = 1024 * (8 + cpair) + 512 * tt

                @blk.tensor
                def _(t, cpair=cpair, tt=tt, step=step):
                    if step == 0:
                        p.wge(t, "agn", 2)
                    if tt == 0:
                        p.wge(t, f"awq{cpair % 2}", 16 * 32 * (cpair // 2 + 1))
                    if step >= 2:
                        p.wge(t, "aqe", step - 1)
                    for kc in range(16):
                        for xi in range(2):
                            ins = t.matmul(
                                pq[2 * (cpair % 2) + xi][:, :],
                                WQS[cpair % 2][:, 2048 * xi + 128 * kc:
                                               2048 * xi + 128 * (kc + 1)],
                                XNO[:, 1024 * kc + 512 * tt:
                                    1024 * kc + 512 * (tt + 1)],
                                start=(kc == 0), stop=(kc == 15))
                            if kc == 15:
                                p.inc(ins, "aqp", 1)

                @blk.vector
                def _(v, cpair=cpair, tt=tt, step=step, qx0=qx0, qx1=qx1):
                    v.wait_ge(p.sem("aqp"), p.n("aqp"))
                    if step == 0:
                        p.wge(v, "acq", 32)
                    px0, px1 = pq[2 * (cpair % 2)], pq[2 * (cpair % 2) + 1]
                    csl = slice(512 * tt, 512 * (tt + 1))
                    v.tensor_tensor(rs0[:, :], px0[:, :], COSQ[:, csl], op=AL.mult)
                    v.tensor_tensor(rs1[:, :], px1[:, :], SINQ[:, csl], op=AL.mult)
                    v.tensor_tensor(QXT[:, qx0:qx0 + 512], rs0[:, :], rs1[:, :],
                                    op=AL.subtract)
                    v.tensor_tensor(rs0[:, :], px0[:, :], SINQ[:, csl], op=AL.mult)
                    v.tensor_tensor(rs1[:, :], px1[:, :], COSQ[:, csl], op=AL.mult)
                    p.inc(v.tensor_tensor(QXT[:, qx1:qx1 + 512], rs0[:, :],
                                          rs1[:, :], op=AL.add), "aqe", 1)
                    v.tensor_tensor(QSQ[:, :], QXT[:, qx0:qx0 + 512],
                                    QXT[:, qx0:qx0 + 512], op=AL.mult)
                    p.inc(v.tensor_tensor(QSQ2[:, :], QXT[:, qx1:qx1 + 512],
                                          QXT[:, qx1:qx1 + 512], op=AL.mult),
                          "aqs", 1)

                @blk.tensor
                def _(t, step=step):
                    p.wge(t, "aqs", step + 1)
                    if step >= 1:
                        p.wge(t, "aqn", step)
                    t.matmul(pg[:, :], ONESB[:, :], QSQ[:, :], start=True,
                             stop=False)
                    p.inc(t.matmul(pg[:, :], ONESB[:, :], QSQ2[:, :],
                                   start=False, stop=True), "aqm", 1)

                @blk.vector
                def _(v, step=step):
                    p.wge(v, "aqm", step + 1)
                    p.inc(v.tensor_reduce(QMX[:, step:step + 1], pg[:, :],
                                          axis=AXX, op=AL.max), "aqn", 1)

        @blk.vector
        def _(v):
            v.wait_ge(p.sem("aqn"), p.n("aqn"))
            p.wge(v, "aoz", 1)
            p.vred(v, AUX[:4, 8:9], QMX[:, :], AL.max)
            p.inc(v.tensor_copy(AUX[0:1, 25:26], AUX[0:1, 8:9]), "aclA", 1)

        @blk.vector
        def _(v):
            p.vred(v, AUX[:4, 9:10], env["KMAX"][:, :], AL.max)
            p.inc(v.tensor_copy(AUX[0:1, 26:27], AUX[0:1, 9:10]), "acl", 1)

        @blk.tensor
        def _(t):
            p.wge(t, "acl", 1)
            p.wge(t, "aclA", 1)
            p.wge(t, "aid", 16)
            t.transpose(pb1[0:1, 0:128], AUX[:, 8:9], IDF[:, :])
            p.inc(t.transpose(pb1[0:1, 128:256], AUX[:, 9:10], IDF[:, :]),
                  "act", 1)

        @blk.vector
        def _(v):
            p.wge(v, "act", 1)
            p.vred(v, AUX[0:1, 2:3], pb1[0:1, 0:128], AL.max)
            p.vred(v, AUX[0:1, 3:4], pb1[0:1, 128:256], AL.max)
            p.vfence(v, v.tensor_tensor(AUX[0:1, 4:5], AUX[0:1, 2:3],
                                        AUX[0:1, 3:4], op=AL.mult))
            p.inc(v.tensor_copy(AUX[0:1, 5:6], AUX[0:1, 4:5]), "acm", 1)

        @blk.scalar
        def _(s):
            p.wge(s, "acm", 1)
            s.activation(AUX[0:1, 6:7], AUX[0:1, 5:6], AF.Sqrt)
            p.inc(s.mul(AUX[0:1, 7:8], AUX[0:1, 6:7], -0.125), "acs", 1)

        @blk.vector
        def _(v):
            p.wge(v, "acs", 1)
            p.inc(v.tensor_copy(AUXB[0:1, 1:2], AUX[0:1, 7:8]), "acb", 1)

        @blk.tensor
        def _(t):
            p.wge(t, "acb", 1)
            p.wge(t, "aon", 1)
            p.inc(t.matmul(pb1[:, 0:1], ONE1[:, :], AUXB[0:1, 1:2],
                           start=True, stop=True, skip_group_check=True),
                  "acp", 1)

        @blk.vector
        def _(v):
            p.wge(v, "acp", 1)
            p.inc(v.tensor_copy(NEGCB[:, :], pb1[:, 0:1]), "acv", 1)

    with contextlib.ExitStack() as phB:
        WCK0T = sb(phB, nc, "WCK0T", [128, BS * DH], BF16)
        WCK1T = sb(phB, nc, "WCK1T", [128, BS * DH], BF16)
        WCVT = sb(phB, nc, "WCVT", [128, BS * DH], BF16)
        REP32 = sb(phB, nc, "REP32", [32, 128], BF16)
        MASKS = sb(phB, nc, "MASKS", [128, 16 * 512], BF16)
        VFG = sb(phB, nc, "VFG", [64, T], BF16)
        KCS0 = sb(phB, nc, "KCS0", [128, 64], BF16)
        KCS1 = sb(phB, nc, "KCS1", [128, 64], BF16)
        KA = sb(phB, nc, "KA", [32, 64], BF16)
        KB = sb(phB, nc, "KB", [32, 64], BF16)
        KQ = sb(phB, nc, "KQ", [32, 64], BF16)
        KQ2 = sb(phB, nc, "KQ2", [32, 64], BF16)
        VCA = sb(phB, nc, "VCA", [64, 65], BF16)
        NEGCCB = sb(phB, nc, "NEGCCB", [64, 1], F32)
        PT = sb(phB, nc, "PT", [128, 4 * 512], BF16)
        PTC = sb(phB, nc, "PTC", [64, 512], BF16)
        PW = PT
        SRB = sb(phB, nc, "SRB", [1, 3 * 512], BF16)
        RR2 = sb(phB, nc, "RR2", [1, 512], F32)
        MBC = sb(phB, nc, "MBC", [128, 512], BF16)
        SOT = sb(phB, nc, "SOT", [64, 512], F32)
        SOST = [sb(phB, nc, f"SOST{i}", [64, 512], BF16) for i in range(2)]

        def stageA_gate(eng):
            eng.wait_ge(p.sem("aqp"), p.n("aqp"))
            eng.wait_ge(p.sem("aqn"), p.n("aqn"))
            eng.wait_ge(p.sem("acv"), p.n("acv"))
            eng.wait_ge(p.sem("agv"), p.n("agv"))
            eng.wait_ge(p.sem("agn"), p.n("agn"))
            eng.wait_ge(p.sem("aqe"), p.n("aqe"))

        @blk.sync
        def _(sy):
            stageA_gate(sy)
            p.inc(sy.dma_start(out=REP32[:, :], in_=env["rep32"][:, :]),
                  "bcw", 16)
            for r in range(16):
                p.inc(sy.dma_start(out=MASKS[:, 512 * r:512 * (r + 1)],
                                   in_=env["masks"][128 * r:128 * (r + 1), :]),
                      "bcw", 16)

        @blk.gpsimd
        def _(g):
            stageA_gate(g)
            for nm, dst in [("wck0", WCK0T), ("wck1", WCK1T), ("wcv", WCVT)]:
                p.inc(g.dma_start(out=dst[:, :], in_=env[nm][:, :]), "bcwg", 16)

        @blk.gpsimd
        def _(g):
            p.inc(g.memset(VCA[:, 64:65], 1.0), "bvo", 1)

        for g in range(8):
            q4 = g % 4
            half = g // 4
            KXa, KXb = KX0[half], KX1[half]

            with contextlib.ExitStack() as psB1:
                PKC = ps(psB1, nc, f"PKC{g}", [64, 64], F32)
                PVC = ps(psB1, nc, f"PVC{g}", [64, 64], F32)
                PRK = ps(psB1, nc, f"PRK{g}", [128, 64], F32)
                PNC = ps(psB1, nc, f"PNC{g}", [64, 64], F32)

                @blk.gpsimd
                def _(gg, g=g):
                    if g >= 1:
                        p.wge(gg, "bvf", g)
                    p.inc(gg.dma_start(out=VFG[:, :],
                                       in_=env["vfm_scr"][64 * g:64 * (g + 1), :]),
                          "bvl", 16)

                @blk.tensor
                def _(t, g=g, q4=q4, KXa=KXa, KXb=KXb):
                    if g == 0:
                        t.wait_ge(p.sem("bcw"), p.n("bcw"))
                        t.wait_ge(p.sem("bcwg"), p.n("bcwg"))
                        p.wge(t, "acv", 1)
                    else:
                        p.wge(t, "bka", g)
                        p.wge(t, "mgd", 8 * g)
                    for par in range(2):
                        W = WCK0T if par == 0 else WCK1T
                        KX = KXa if par == 0 else KXb
                        for pp in range(BS):
                            ins = t.matmul(
                                PKC[:, :],
                                W[32 * q4:32 * (q4 + 1), 64 * pp:64 * (pp + 1)],
                                bass.AP(KX, 32 * q4 * T + pp,
                                        [[T, 32], [64, 64]]),
                                start=(par == 0 and pp == 0),
                                stop=(par == 1 and pp == BS - 1),
                                tile_position=(32 * q4, 0),
                                skip_group_check=True)
                            if par == 1 and pp == BS - 1:
                                p.inc(ins, "bkc", 1)
                    p.wge(t, "bvl", 16 * (g + 1))
                    if g >= 1:
                        p.wge(t, "bve", g)
                    for pp in range(BS):
                        ins = t.matmul(
                            PVC[:, :],
                            bass.AP(VFG, pp, [[T, 64], [64, 64]]),
                            WCVT[0:64, 64 * pp:64 * (pp + 1)],
                            start=(pp == 0), stop=(pp == BS - 1),
                            skip_group_check=True)
                        if pp == BS - 1:
                            p.inc(ins, "bvc", 1)

                @blk.vector
                def _(v, g=g):
                    p.wge(v, "bkc", g + 1)
                    v.tensor_copy(KA[:, :], PKC[0:32, :])
                    p.inc(v.tensor_copy(KB[:, :], PKC[32:64, :]), "bka", 1)
                    p.wge(v, "bvc", g + 1)
                    if g == 0:
                        p.wge(v, "bvo", 1)
                    p.inc(v.tensor_copy(VCA[:, 0:64], PVC[:, :]), "bve", 1)
                    p.inc(v.tensor_tensor(KQ[:, :], KA[:, :], KA[:, :],
                                          op=AL.mult), "bvf", 1)
                    p.inc(v.tensor_tensor(KQ2[:, :], KB[:, :], KB[:, :],
                                          op=AL.mult), "bkq", 1)

                @blk.tensor
                def _(t, g=g):
                    p.wge(t, "bka", g + 1)
                    p.inc(t.matmul(PRK[:, :], REP32[:, :], KA[:, :],
                                   start=True, stop=True), "brk", 1)

                @blk.vector
                def _(v, g=g):
                    p.wge(v, "brk", 2 * g + 1)
                    p.inc(v.tensor_copy(KCS0[:, :], PRK[:, :]), "bk0", 1)

                @blk.tensor
                def _(t, g=g):
                    p.wge(t, "bk0", g + 1)
                    p.inc(t.matmul(PRK[:, :], REP32[:, :], KB[:, :],
                                   start=True, stop=True), "brk", 1)

                @blk.vector
                def _(v, g=g):
                    p.wge(v, "brk", 2 * g + 2)
                    p.inc(v.tensor_copy(KCS1[:, :], PRK[:, :]), "bk1", 1)

                @blk.tensor
                def _(t, g=g):
                    p.wge(t, "bkq", g + 1)
                    if g >= 1:
                        p.wge(t, "bnr", g)
                    t.matmul(PNC[0:1, 0:64], ONESB[0:32, 0:1], KQ[:, :],
                             start=True, stop=False)
                    p.inc(t.matmul(PNC[0:1, 0:64], ONESB[0:32, 0:1], KQ2[:, :],
                                   start=False, stop=True), "bnq", 1)

                @blk.vector
                def _(v, g=g):
                    p.wge(v, "bnq", g + 1)
                    p.vred(v, AUX[0:1, 16:17], PNC[0:1, 0:64], AL.max)
                    p.vfence(v, v.tensor_tensor(AUX[0:1, 17:18],
                                                AUX[0:1, 16:17],
                                                AUX[0:1, 2:3], op=AL.mult))
                    p.inc(v.tensor_copy(AUX[0:1, 18:19], AUX[0:1, 17:18]),
                          "bns", 1)

                @blk.scalar
                def _(s, g=g):
                    p.wge(s, "bns", g + 1)
                    s.activation(AUX[0:1, 19:20], AUX[0:1, 18:19], AF.Sqrt)
                    p.inc(s.mul(AUX[0:1, 20:21], AUX[0:1, 19:20], -0.125),
                          "bnt", 1)

                @blk.vector
                def _(v, g=g):
                    p.wge(v, "bnt", g + 1)
                    p.inc(v.tensor_copy(AUXB[0:1, 2:3], AUX[0:1, 20:21]),
                          "bnu", 1)

                @blk.tensor
                def _(t, g=g):
                    p.wge(t, "bnu", g + 1)
                    p.inc(t.matmul(PNC[0:64, 1:2], ONE1[0:1, 0:64],
                                   AUXB[0:1, 2:3], start=True, stop=True,
                                   skip_group_check=True), "bnb", 1)

                @blk.vector
                def _(v, g=g):
                    p.wge(v, "bnb", g + 1)
                    p.inc(v.tensor_copy(NEGCCB[:, :], PNC[0:64, 1:2]), "bnr", 1)

            with contextlib.ExitStack() as psB2:
                SC = [ps(psB2, nc, f"SC{g}_{i}", [128, 512], F32) for i in range(4)]
                ACCD = ps(psB2, nc, f"ACCD{g}", [65, 512], F32)
                ACCW = ps(psB2, nc, f"ACCW{g}", [65, 512], F32)
                ACCC = ps(psB2, nc, f"ACCC{g}", [65, 512], F32)
                MB = ps(psB2, nc, f"MB{g}", [128, 512], F32)

                for hj in range(4):
                    for tt in range(2):
                        it = p.cnt.get("_attn_it", 0)
                        p.cnt["_attn_it"] = it + 1
                        qx0 = 1024 * (2 * hj + half) + 512 * tt
                        qx1 = 1024 * (8 + 2 * hj + half) + 512 * tt
                        band = (tt, tt + 1)

                        @blk.tensor
                        def _(t, g=g, q4=q4, hj=hj, tt=tt, it=it, qx0=qx0,
                              qx1=qx1, band=band, KXa=KXa, KXb=KXb):
                            if it >= 1:
                                p.wge(t, "mgd", it)
                            if it == 0:
                                p.wge(t, "bk1", 1)
                                p.wge(t, "bnr", 1)
                                p.wge(t, "bve", 1)
                            for G in range(8):
                                for si in range(4):
                                    s4 = 4 * G + si
                                    if it >= 1 or G >= 1:
                                        pass
                                    t.matmul(SC[si][:, :],
                                             KXa[32 * q4:32 * (q4 + 1),
                                                 128 * s4:128 * (s4 + 1)],
                                             QXT[32 * q4:32 * (q4 + 1),
                                                 qx0:qx0 + 512],
                                             start=True, stop=False,
                                             tile_position=(32 * q4, 0),
                                             skip_group_check=True)
                                    ins = t.matmul(SC[si][:, :],
                                                   KXb[32 * q4:32 * (q4 + 1),
                                                       128 * s4:128 * (s4 + 1)],
                                                   QXT[32 * q4:32 * (q4 + 1),
                                                       qx1:qx1 + 512],
                                                   start=False, stop=True,
                                                   tile_position=(32 * q4, 0),
                                                   skip_group_check=True)
                                    if si == 3:
                                        p.inc(ins, "scg", 1)
                                p.wge(t, "xpg", 9 * it + G + 1)
                                for si in range(4):
                                    s4 = 4 * G + si
                                    ins = t.matmul(
                                        ACCD[:, :],
                                        bass.AP(VT, (s4 * 8 + g) * 65,
                                                [[32 * 8 * 65, 128], [1, 65]]),
                                        PT[:, 512 * si:512 * (si + 1)],
                                        start=(s4 == 0), stop=(s4 == 31),
                                        skip_group_check=True)
                                    if si == 3:
                                        p.inc(ins, "pvg", 1)
                                if G in band:
                                    p.wge(t, "wtg", 2 * it + (1 if G == band[0] else 2))
                                    for si in range(4):
                                        s4 = 4 * G + si
                                        ins = t.matmul(
                                            ACCW[:, :],
                                            bass.AP(VT, (s4 * 8 + g) * 65,
                                                    [[32 * 8 * 65, 128], [1, 65]]),
                                            PW[:, 512 * si:512 * (si + 1)],
                                            start=(G == band[0] and si == 0),
                                            stop=(G == band[1] and si == 3),
                                            skip_group_check=True)
                                        if si == 3:
                                            p.inc(ins, "wpg", 1)
                            p.wge(t, "xpg", 9 * it + 8)
                            t.matmul(SC[0][0:64, :],
                                     KCS0[32 * q4:32 * (q4 + 1), :],
                                     QXT[32 * q4:32 * (q4 + 1), qx0:qx0 + 512],
                                     start=True, stop=False,
                                     tile_position=(32 * q4, 0),
                                     skip_group_check=True)
                            p.inc(t.matmul(SC[0][0:64, :],
                                           KCS1[32 * q4:32 * (q4 + 1), :],
                                           QXT[32 * q4:32 * (q4 + 1),
                                               qx1:qx1 + 512],
                                           start=False, stop=True,
                                           tile_position=(32 * q4, 0),
                                           skip_group_check=True), "csq", 1)

                        @blk.scalar
                        def _(s, it=it, band=band):
                            for G in range(8):
                                p.wge(s, "scg", 8 * it + G + 1)
                                if it >= 1 or G >= 1:
                                    p.wge(s, "pvg", 8 * it + G)
                                    nwt = 2 * it + len([b for b in band if b < G])
                                    p.wge(s, "wtg", nwt)
                                    p.wge(s, "wpg", nwt)
                                for si in range(4):
                                    ins = s.activation(
                                        PT[:, 512 * si:512 * (si + 1)],
                                        SC[si][:, :], AF.Exp,
                                        bias=NEGCB[:, :], scale=0.125)
                                    if si == 3:
                                        p.inc(ins, "xpg", 1)
                            p.wge(s, "csq", it + 1)
                            p.wge(s, "pcv", it)
                            p.inc(s.activation(PTC[:, :], SC[0][0:64, :], AF.Exp,
                                               bias=NEGCCB[:, :], scale=0.125),
                                  "xpg", 1)

                        @blk.tensor
                        def _(t, it=it):
                            p.wge(t, "xpg", 9 * it + 9)
                            p.inc(t.matmul(ACCC[:, :], VCA[:, :], PTC[:, :],
                                           start=True, stop=True,
                                           skip_group_check=True), "pcv", 1)

                        @blk.vector
                        def _(v, it=it, band=band, tt=tt):
                            for bi, G in enumerate(band):
                                p.wge(v, "xpg", 9 * it + G + 1)
                                p.wge(v, "pvg", 8 * it + G + 1)
                                for si in range(4):
                                    s4 = 4 * G + si
                                    r = 8 * tt + s4 - 4 * tt
                                    ins = v.tensor_tensor(
                                        PW[:, 512 * si:512 * (si + 1)],
                                        PT[:, 512 * si:512 * (si + 1)],
                                        MASKS[:, 512 * r:512 * (r + 1)],
                                        op=AL.mult)
                                    if si == 3:
                                        p.inc(ins, "wtg", 1)

                        @blk.vector
                        def _(v, it=it, tt=tt):
                            p.wge(v, "pvg", 8 * (it + 1))
                            p.wge(v, "wpg", 2 * (it + 1))
                            p.wge(v, "pcv", it + 1)
                            for b, ACC in [(1, ACCD), (2, ACCW), (0, ACCC)]:
                                p.vfence(v, v.reciprocal(RR2[0:1, :],
                                                         ACC[64:65, :]))
                                p.inc(v.tensor_tensor(
                                    SRB[0:1, 512 * b:512 * (b + 1)],
                                    RR2[0:1, :],
                                    GN[0:1, 1024 * b + 512 * tt:
                                       1024 * b + 512 * (tt + 1)],
                                    op=AL.mult), "msr", 1)

                        @blk.tensor
                        def _(t, it=it):
                            for bi in range(3):
                                b = [1, 2, 0][bi]
                                p.wge(t, "msr", 3 * it + bi + 1)
                                if it or bi:
                                    p.wge(t, "mbe", 3 * it + bi)
                                p.inc(t.matmul(MB[:, :], ONE1[:, :],
                                               SRB[0:1, 512 * b:512 * (b + 1)],
                                               start=True, stop=True,
                                               skip_group_check=True), "mbq", 1)

                        @blk.vector
                        def _(v, it=it):
                            if it >= 2:
                                p.wge(v, "sost", 16 * (it - 1))
                            for bi, ACC in enumerate([ACCD, ACCW, ACCC]):
                                p.wge(v, "mbq", 3 * it + bi + 1)
                                p.inc(v.tensor_copy(MBC[:, :], MB[:, :]),
                                      "mbe", 1)
                                if bi == 0:
                                    v.tensor_tensor(SOT[:, :], ACC[0:64, :],
                                                    MBC[0:64, :], op=AL.mult)
                                elif bi == 1:
                                    v.tensor_tensor(MBC[0:64, :], ACC[0:64, :],
                                                    MBC[0:64, :], op=AL.mult)
                                    v.tensor_tensor(SOT[:, :], SOT[:, :],
                                                    MBC[0:64, :], op=AL.add)
                                else:
                                    v.tensor_tensor(MBC[0:64, :], ACC[0:64, :],
                                                    MBC[0:64, :], op=AL.mult)
                                    p.inc(v.tensor_tensor(SOST[it % 2][:, :],
                                                          SOT[:, :],
                                                          MBC[0:64, :],
                                                          op=AL.add), "mgd", 1)

                        @blk.sync
                        def _(sy, it=it, hj=hj, tt=tt, g=g):
                            h = 4 * g + hj
                            p.wge(sy, "mgd", it + 1)
                            p.inc(sy.dma_start(
                                out=env["so_scr"][64 * h:64 * (h + 1),
                                                  512 * tt:512 * (tt + 1)],
                                in_=SOST[it % 2][:, :]), "sost", 16)

    with contextlib.ExitStack() as phC:
        WOS = [sb(phC, nc, f"WOS{i}", [128, 2048], BF16) for i in range(2)]
        SOC = sb(phC, nc, "SOC", [128, 16 * 512], BF16)
        S0E = sb(phC, nc, "S0E", [128, 2 * 512], BF16)
        pw = [ps(phC, nc, f"cpw{i}", [128, 512], F32) for i in range(4)]

        def stageB_gate(eng):
            for nm in ["mgd", "sost", "wtg", "mbe", "msr", "bnr", "bk1",
                       "bve", "xpg", "pcv", "pvg", "wpg", "bvf", "bkq"]:
                eng.wait_ge(p.sem(nm), p.n(nm))

        @blk.gpsimd
        def _(g):
            stageB_gate(g)
            for tt in range(2):
                for jc in range(16):
                    idx = tt * 16 + jc
                    if idx >= 2:
                        p.wge(g, "cso", idx - 1)
                    for kc in range(16):
                        p.inc(g.dma_start(
                            out=WOS[idx % 2][:, 128 * kc:128 * (kc + 1)],
                            in_=env["wo"][128 * kc:128 * (kc + 1),
                                          128 * jc:128 * (jc + 1)]),
                            f"cwo{idx % 2}", 16)

        for tt in range(2):
            @blk.sync
            def _(sy, tt=tt):
                if tt == 0:
                    stageB_gate(sy)
                sy.wait_ge(p.sem("sost"), p.n("sost"))
                if tt == 1:
                    p.wge(sy, "cso", 16)
                for kc in range(16):
                    p.inc(sy.dma_start(
                        out=SOC[:, 512 * kc:512 * (kc + 1)],
                        in_=env["so_scr"][128 * kc:128 * (kc + 1),
                                          512 * tt:512 * (tt + 1)]),
                        f"csl{tt}", 16)

            for jc in range(16):
                idx = tt * 16 + jc

                @blk.tensor
                def _(t, jc=jc, tt=tt, idx=idx):
                    if jc == 0:
                        p.wge(t, f"csl{tt}", 256)
                    p.wge(t, f"cwo{idx % 2}", 16 * 16 * (idx // 2 + 1))
                    if idx >= 4:
                        p.wge(t, "cse", idx - 3)
                    for kc in range(16):
                        ins = t.matmul(pw[idx % 4][:, :],
                                       WOS[idx % 2][:, 128 * kc:128 * (kc + 1)],
                                       SOC[:, 512 * kc:512 * (kc + 1)],
                                       start=(kc == 0), stop=(kc == 15))
                        if kc == 15:
                            p.inc(ins, "cso", 1)

                @blk.vector
                def _(v, idx=idx):
                    p.wge(v, "cso", idx + 1)
                    if idx >= 2:
                        p.wge(v, "cst0", 16 * (idx - 1))
                    p.inc(v.tensor_copy(S0E[:, 512 * (idx % 2):
                                            512 * (idx % 2 + 1)],
                                        pw[idx % 4][:, :]), "cse", 1)

                @blk.sync
                def _(sy, jc=jc, tt=tt, idx=idx):
                    p.wge(sy, "cse", idx + 1)
                    p.inc(sy.dma_start(
                        out=env["s0_scr"][128 * jc:128 * (jc + 1),
                                          512 * tt:512 * (tt + 1)],
                        in_=S0E[:, 512 * (idx % 2):512 * (idx % 2 + 1)]),
                        "cst0", 16)

        @blk.sync
        def _(sy):
            sy.wait_ge(p.sem("cst0"), p.n("cst0"))

    return



def emit_ffn0(p, env, ph):
    p.rebase_all()
    nc, blk = p.nc, p.block
    XNO = env["XNO"]

    GT = sb(ph, nc, "GT", [128, 16 * 16], BF16)
    SELB = sb(ph, nc, "SELB", [16, 16 * 128], BF16)
    IDF0 = sb(ph, nc, "IDF0", [128, 128], F32)
    ONE1F = sb(ph, nc, "ONE1F", [1, 128], BF16)
    ZR = sb(ph, nc, "ZR", [16, TOK], F32)
    ZT = sb(ph, nc, "ZT", [128, 8 * 16], F32)
    ET1 = sb(ph, nc, "ET1", [128, 16], F32)
    ET2 = sb(ph, nc, "ET2", [128, 16], F32)
    ZMK = sb(ph, nc, "ZMK", [128, 16], F32)
    WDT = sb(ph, nc, "WDT", [128, 8 * 16], F32)
    WD = sb(ph, nc, "WD", [16, TOK], BF16)
    FC = sb(ph, nc, "FC", [128, 24], F32)
    WB = sb(ph, nc, "WB", [128, 2 * 512], BF16)
    GUPB = sb(ph, nc, "GUPB", [128, 1], F32)
    GDNB = sb(ph, nc, "GDNB", [128, 1], F32)
    GSB = sb(ph, nc, "GSB", [1, 2], BF16)
    QW = [sb(ph, nc, f"QW{i}", [128, 1024], BF16) for i in range(2)]
    SW = [sb(ph, nc, f"SW{i}", [128, 1024], BF16) for i in range(2)]
    WUP = sb(ph, nc, "WUP", [128, 16 * 512], BF16)
    WDN = sb(ph, nc, "WDN", [128, 8 * 1024], BF16)
    RL = sb(ph, nc, "RL", [128, 512], BF16)
    RL2 = sb(ph, nc, "RL2", [128, 512], BF16)
    HH = sb(ph, nc, "HH", [128, 8 * 1024], BF16)
    YSB = sb(ph, nc, "YSB", [128, 16 * 1024], BF16)
    G0E = sb(ph, nc, "G0E", [128, 2 * 512], BF16)

    pr = [ps(ph, nc, f"fpr{i}", [128, 512], F32) for i in range(4)]
    pd = [ps(ph, nc, f"fpd{i}", [128, 512], F32) for i in range(2)]
    pt1 = ps(ph, nc, "fpt1", [128, 128], F32)
    pb = ps(ph, nc, "fpb", [128, 512], F32)

    def attn_gate(eng):
        for nm in ["cst0", "cse", "cso", "csl0", "csl1", "sost", "mgd"]:
            eng.wait_ge(p.sem(nm), p.n(nm))

    @blk.sync
    def _(sy):
        attn_gate(sy)
        p.inc(sy.dma_start(out=SELB[:, :], in_=env["selb"][:, :]), "cst", 16)
        p.inc(sy.dma_start(out=IDF0[:, :], in_=env["identf32"][:, :]),
              "cst", 16)

    @blk.gpsimd
    def _(g):
        for kc in range(16):
            p.inc(g.dma_start(out=GT[:, 16 * kc:16 * (kc + 1)],
                              in_=env["gw0t"][128 * kc:128 * (kc + 1), :]),
                  "fglg", 16)

    @blk.gpsimd
    def _(g):
        attn_gate(g)
        p.inc(g.memset(ONE1F[:, :], 1.0), "aon", 1)
        p.inc(g.dma_start(out=FC[0:1, 0:1], in_=env["gup"][:, :]), "acq", 16)
        p.inc(g.dma_start(out=FC[0:1, 1:2], in_=env["gdn"][:, :]), "acq", 16)

    @blk.vector
    def _(v):
        p.wge(v, "acq", 32)
        p.inc(v.tensor_copy(GSB[0:1, :], FC[0:1, 0:2]), "acb", 1)

    @blk.tensor
    def _(t):
        p.wge(t, "acb", 1)
        p.wge(t, "aon", 1)
        t.matmul(pb[:, 0:1], ONE1F[:, :], GSB[0:1, 0:1], start=True, stop=True,
                 skip_group_check=True)
        p.inc(t.matmul(pb[:, 1:2], ONE1F[:, :], GSB[0:1, 1:2], start=True,
                       stop=True, skip_group_check=True), "acp", 1)

    @blk.vector
    def _(v):
        p.wge(v, "acp", 1)
        v.tensor_copy(GUPB[:, :], pb[:, 0:1])
        p.inc(v.tensor_copy(GDNB[:, :], pb[:, 1:2]), "acv", 1)

    @blk.tensor
    def _(t):
        t.wait_ge(p.sem("cst"), p.n("cst"))
        t.wait_ge(p.sem("fglg"), p.n("fglg"))
        for tt in range(2):
            for kc in range(16):
                ins = t.matmul(pr[tt][:16, :], GT[:, 16 * kc:16 * (kc + 1)],
                               XNO[:, 1024 * kc + 512 * tt:
                                   1024 * kc + 512 * (tt + 1)],
                               start=(kc == 0), stop=(kc == 15))
                if kc == 15:
                    p.inc(ins, "agp", 1)

    @blk.vector
    def _(v):
        for tt in range(2):
            p.wge(v, "agp", tt + 1)
            p.inc(v.tensor_copy(ZR[:, 512 * tt:512 * (tt + 1)],
                                pr[tt][:16, :]), "age", 1)

    @blk.tensor
    def _(t):
        p.wge(t, "age", 2)
        for b8 in range(8):
            if b8 >= 1:
                p.wge(t, "acl", b8)
            p.inc(t.transpose(pt1[:, 0:16], ZR[:, 128 * b8:128 * (b8 + 1)],
                              IDF0[0:16, 0:16]), "act", 1)

    @blk.vector
    def _(v):
        for b8 in range(8):
            p.wge(v, "act", b8 + 1)
            p.inc(v.tensor_copy(ZT[:, 16 * b8:16 * (b8 + 1)],
                                bass.AP(pt1, 0, [[128, 128], [1, 16]])),
                  "acl", 1)

    for b8 in range(8):
        @blk.vector
        def _(v, b8=b8):
            p.wge(v, "acl", b8 + 1)
            if b8 >= 1:
                p.wge(v, "aqe", b8)
            zb = ZT[:, 16 * b8:16 * (b8 + 1)]
            p.vred(v, FC[:, 8:9], zb, AL.max)
            p.vfence(v, v.tensor_scalar(ET1[:, :], zb, FC[:, 8:9], None,
                                        op0=AL.is_equal))
            p.vfence(v, v.scalar_tensor_tensor(ZMK[:, :], ET1[:, :], -1e30,
                                               zb, op0=AL.mult, op1=AL.add))
            p.vred(v, FC[:, 9:10], ZMK[:, :], AL.max)
            p.vfence(v, v.tensor_scalar(ET2[:, :], zb, FC[:, 9:10], None,
                                        op0=AL.is_equal))
            v.tensor_scalar(FC[:, 10:11], FC[:, 8:9], -1.0, None, op0=AL.mult)
            p.inc(v.tensor_scalar(FC[:, 11:12], FC[:, 9:10], -1.0, None,
                                  op0=AL.mult), "st1", 1)

        @blk.scalar
        def _(s, b8=b8):
            p.wge(s, "st1", b8 + 1)
            s.activation(FC[:, 12:13], FC[:, 10:11], AF.Exp)
            s.activation(FC[:, 12:13], FC[:, 12:13], AF.Ln, bias=1.0)
            s.activation(FC[:, 13:14], FC[:, 11:12], AF.Exp)
            p.inc(s.activation(FC[:, 13:14], FC[:, 13:14], AF.Ln, bias=1.0),
                  "st2", 1)

        @blk.vector
        def _(v, b8=b8):
            p.wge(v, "st2", b8 + 1)
            p.inc(v.tensor_tensor(FC[:, 14:15], FC[:, 13:14], FC[:, 12:13],
                                  op=AL.subtract), "aqs", 1)

        @blk.scalar
        def _(s, b8=b8):
            p.wge(s, "aqs", b8 + 1)
            p.inc(s.activation(FC[:, 16:17], FC[:, 14:15], AF.Sigmoid),
                  "aqm", 1)

        @blk.vector
        def _(v, b8=b8):
            p.wge(v, "aqm", b8 + 1)
            p.vfence(v, v.tensor_scalar(FC[:, 17:18], FC[:, 16:17], -1.0, 1.0,
                                        op0=AL.mult, op1=AL.add))
            p.vfence(v, v.tensor_scalar(WDT[:, 16 * b8:16 * (b8 + 1)],
                                        ET1[:, :], FC[:, 16:17], None,
                                        op0=AL.mult))
            p.vfence(v, v.tensor_scalar(ZMK[:, :], ET2[:, :], FC[:, 17:18],
                                        None, op0=AL.mult))
            p.inc(v.tensor_tensor(WDT[:, 16 * b8:16 * (b8 + 1)],
                                  WDT[:, 16 * b8:16 * (b8 + 1)],
                                  ZMK[:, :], op=AL.add), "aqe", 1)

    @blk.tensor
    def _(t):
        for b8 in range(8):
            p.wge(t, "aqe", b8 + 1)
            if b8 >= 1:
                p.wge(t, "aqp", b8)
            p.inc(t.transpose(pt1[0:16, 0:128], WDT[:, 16 * b8:16 * (b8 + 1)],
                              IDF0[:, :]), "aqn", 1)

    @blk.vector
    def _(v):
        for b8 in range(8):
            p.wge(v, "aqn", b8 + 1)
            p.inc(v.tensor_copy(WD[:, 128 * b8:128 * (b8 + 1)],
                                pt1[0:16, 0:128]), "aqp", 1)

    for e in range(E0):
        @blk.tensor
        def _(t, e=e):
            if e == 0:
                p.wge(t, "aqp", 8)
                p.wge(t, "acv", 1)
            else:
                p.wge(t, "wtg", 16 * e)
            for tt in range(2):
                if e or tt:
                    p.wge(t, "mbe", 2 * e + tt)
                p.inc(t.matmul(pb[:, :], SELB[:, 128 * e:128 * (e + 1)],
                               WD[:, 512 * tt:512 * (tt + 1)],
                               start=True, stop=True, skip_group_check=True),
                      "mbq", 1)

        @blk.vector
        def _(v, e=e):
            for tt in range(2):
                p.wge(v, "mbq", 2 * e + tt + 1)
                p.inc(v.tensor_copy(WB[:, 512 * tt:512 * (tt + 1)], pb[:, :]),
                      "mbe", 1)

        for ih in range(2):
            eh = 2 * e + ih

            @blk.gpsimd
            def _(g, e=e, ih=ih, eh=eh):
                for kc in range(16):
                    par = kc % 2
                    if eh * 16 + kc >= 2:
                        p.wge(g, "bkc", eh * 16 + kc - 1)
                    if ih == 0 and e >= 1 and kc < 2:
                        p.wge(g, "bnq", 16 * e)
                    p.inc(g.dma_start(
                        out=QW[par][:, 0:512],
                        in_=env["qup"][e, 128 * kc:128 * (kc + 1),
                                       512 * ih:512 * (ih + 1)]),
                        ("wkv", "vfs")[par], 16)
                    p.inc(g.dma_start(
                        out=SW[par][:, 0:512],
                        in_=env["sup"][e, 128 * kc:128 * (kc + 1),
                                       512 * ih:512 * (ih + 1)]),
                        ("wkv", "vfs")[par], 16)

            @blk.vector
            def _(v, e=e, ih=ih, eh=eh):
                for kc in range(16):
                    par = kc % 2
                    p.wge(v, ("wkv", "vfs")[par], 32 * (eh * 8 + kc // 2 + 1))
                    if kc == 0:
                        p.wge(v, "scg", 8 * eh)
                    p.inc(v.scalar_tensor_tensor(
                        WUP[:, 512 * kc:512 * (kc + 1)],
                        QW[par][:, 0:512], -8.0, SW[par][:, 0:512],
                        op0=AL.add, op1=AL.mult), "bkc", 1)

            @blk.tensor
            def _(t, e=e, ih=ih, eh=eh):
                p.wge(t, "bkc", 16 * (eh + 1))
                for tt in range(2):
                    for ic2 in range(4):
                        idx = tt * 4 + ic2
                        gidx = eh * 8 + idx
                        if gidx >= 4:
                            p.wge(t, "xpg", gidx - 3)
                        for kc in range(16):
                            ins = t.matmul(pr[idx % 4][:, :],
                                           WUP[:, 512 * kc + 128 * ic2:
                                               512 * kc + 128 * (ic2 + 1)],
                                           XNO[:, 1024 * kc + 512 * tt:
                                               1024 * kc + 512 * (tt + 1)],
                                           start=(kc == 0), stop=(kc == 15))
                            if kc == 15:
                                p.inc(ins, "scg", 1)

            @blk.scalar
            def _(s, e=e, ih=ih, eh=eh):
                for idx in range(8):
                    gidx = eh * 8 + idx
                    p.wge(s, "scg", gidx + 1)
                    if gidx >= 1:
                        p.wge(s, "wtg", gidx)
                    p.inc(s.activation(RL[:, :], pr[idx % 4][:, :], AF.Relu,
                                       scale=GUPB[:, :]), "xpg", 1)

            @blk.vector
            def _(v, e=e, ih=ih, eh=eh):
                for idx in range(8):
                    tt, ic2 = idx // 4, idx % 4
                    ic = 4 * ih + ic2
                    gidx = eh * 8 + idx
                    p.wge(v, "xpg", gidx + 1)
                    if idx == 0 and ih == 0:
                        p.wge(v, "mbe", 2 * (e + 1))
                        p.wge(v, "pvg", 32 * e)
                    v.tensor_tensor(RL2[:, :], RL[:, :], RL[:, :], op=AL.mult)
                    p.inc(v.tensor_tensor(HH[:, 1024 * ic + 512 * tt:
                                              1024 * ic + 512 * (tt + 1)],
                                          RL2[:, :],
                                          WB[:, 512 * tt:512 * (tt + 1)],
                                          op=AL.mult), "wtg", 1)

        for oh in range(2):
            eo = 2 * e + oh

            @blk.gpsimd
            def _(g, e=e, oh=oh, eo=eo):
                if oh == 0:
                    p.wge(g, "bkc", 16 * 2 * (e + 1))
                for kc in range(8):
                    par = kc % 2
                    if eo * 8 + kc >= 2:
                        p.wge(g, "bnq", 8 * eo + kc - 1)
                    p.inc(g.dma_start(
                        out=QW[par][:, 0:1024],
                        in_=env["qdn"][e, 128 * kc:128 * (kc + 1),
                                       1024 * oh:1024 * (oh + 1)]),
                        ("bvl", "bcwg")[par], 16)
                    p.inc(g.dma_start(
                        out=SW[par][:, 0:1024],
                        in_=env["sdn"][e, 128 * kc:128 * (kc + 1),
                                       1024 * oh:1024 * (oh + 1)]),
                        ("bvl", "bcwg")[par], 16)

            @blk.vector
            def _(v, e=e, oh=oh, eo=eo):
                for kc in range(8):
                    par = kc % 2
                    p.wge(v, ("bvl", "bcwg")[par], 32 * (eo * 4 + kc // 2 + 1))
                    if kc == 0:
                        p.wge(v, "pvg", 16 * eo)
                    p.inc(v.scalar_tensor_tensor(
                        WDN[:, 1024 * kc:1024 * (kc + 1)],
                        QW[par][:, 0:1024], -8.0, SW[par][:, 0:1024],
                        op0=AL.add, op1=AL.mult), "bnq", 1)

            @blk.tensor
            def _(t, e=e, oh=oh, eo=eo):
                p.wge(t, "bnq", 8 * (eo + 1))
                if oh == 0:
                    p.wge(t, "wtg", 16 * (e + 1))
                for tt in range(2):
                    for oc2 in range(8):
                        idx = tt * 8 + oc2
                        gidx = eo * 16 + idx
                        if gidx >= 2:
                            p.wge(t, "mgd", gidx - 1)
                        for kc in range(8):
                            ins = t.matmul(pd[idx % 2][:, :],
                                           WDN[:, 1024 * kc + 128 * oc2:
                                               1024 * kc + 128 * (oc2 + 1)],
                                           HH[:, 1024 * kc + 512 * tt:
                                              1024 * kc + 512 * (tt + 1)],
                                           start=(kc == 0), stop=(kc == 7))
                            if kc == 7:
                                p.inc(ins, "pvg", 1)

            @blk.vector
            def _(v, e=e, oh=oh, eo=eo):
                for idx in range(16):
                    tt, oc2 = idx // 8, idx % 8
                    oc = 8 * oh + oc2
                    gidx = eo * 16 + idx
                    p.wge(v, "pvg", gidx + 1)
                    dst = YSB[:, 1024 * oc + 512 * tt:
                              1024 * oc + 512 * (tt + 1)]
                    if e == 0:
                        p.inc(v.tensor_copy(dst, pd[idx % 2][:, :]), "mgd", 1)
                    else:
                        p.inc(v.tensor_tensor(dst, dst, pd[idx % 2][:, :],
                                              op=AL.add), "mgd", 1)

    for idx in range(32):
        tt, oc = idx % 2, idx // 2

        @blk.scalar
        def _(s, idx=idx, tt=tt, oc=oc):
            oh, oc2 = oc // 8, oc % 8
            p.wge(s, "mgd", (2 * 15 + oh) * 16 + tt * 8 + oc2 + 1)
            if idx >= 2:
                p.wge(s, "cst0", 16 * (idx - 1))
            p.inc(s.activation(G0E[:, 512 * (idx % 2):512 * (idx % 2 + 1)],
                               YSB[:, 1024 * oc + 512 * tt:
                                   1024 * oc + 512 * (tt + 1)],
                               AF.Silu, scale=GDNB[:, :]), "msr", 1)

        @blk.sync
        def _(sy, idx=idx, tt=tt, oc=oc):
            p.wge(sy, "msr", idx + 1)
            p.inc(sy.dma_start(
                out=env["g0_scr"][128 * oc:128 * (oc + 1),
                                  512 * tt:512 * (tt + 1)],
                in_=G0E[:, 512 * (idx % 2):512 * (idx % 2 + 1)]), "cst0", 16)

    @blk.sync
    def _(sy):
        sy.wait_ge(p.sem("cst0"), p.n("cst0"))

    env["_dbg0"] = (WD, GUPB, GDNB, YSB)
    return



def emit_fm_ln(p, env, ph, SRC, DST, banks, tiles, gamma=None, beta=None):
    nc, blk = p.nc, p.block
    PSA, PSB, PBC = banks
    SQ1, ONE1R, ONECL, ONECLB, SRow = tiles

    @blk.gpsimd
    def _(g):
        p.inc(g.memset(ONECL[:, :], 1.0), "aon", 1)
        p.inc(g.memset(ONECLB[:, :], 1.0), "aon", 1)
        p.inc(g.memset(ONE1R[:, :], 1.0), "aon", 1)

    for tt in range(2):
        @blk.tensor
        def _(t, tt=tt):
            t.wait_ge(p.sem("aon"), p.n("aon"))
            if tt == 1:
                p.wge(t, "xn", 16)
            for cc in range(16):
                ins = t.matmul(PSA[0:1, :], ONECL[:, :],
                               SRC[:, 1024 * cc + 512 * tt:
                                   1024 * cc + 512 * (tt + 1)],
                               start=(cc == 0), stop=(cc == 15),
                               skip_group_check=True)
                if cc == 15:
                    p.inc(ins, "st1", 1)

        @blk.vector
        def _(v, tt=tt):
            for cc in range(16):
                if 16 * tt + cc >= 2:
                    p.wge(v, "aqm", 16 * tt + cc - 1)
                p.inc(v.tensor_tensor(SQ1[:, 512 * (cc % 2):512 * (cc % 2 + 1)],
                                      SRC[:, 1024 * cc + 512 * tt:
                                          1024 * cc + 512 * (tt + 1)],
                                      SRC[:, 1024 * cc + 512 * tt:
                                          1024 * cc + 512 * (tt + 1)],
                                      op=AL.mult), "aqs", 1)

        @blk.tensor
        def _(t, tt=tt):
            for cc in range(16):
                p.wge(t, "aqs", 16 * tt + cc + 1)
                ins = t.matmul(PSB[0:1, :], ONECLB[:, :],
                               SQ1[:, 512 * (cc % 2):512 * (cc % 2 + 1)],
                               start=(cc == 0), stop=(cc == 15),
                               skip_group_check=True)
                p.inc(ins, "aqm", 1)

        @blk.vector
        def _(v, tt=tt):
            p.wge(v, "st1", tt + 1)
            p.wge(v, "aqm", 16 * (tt + 1))
            p.vfence(v, v.tensor_scalar(SRow[0:1, 0:512], PSA[0:1, :],
                                        1.0 / D, None, op0=AL.mult))
            p.vfence(v, v.tensor_scalar(SRow[0:1, 512:1024], PSB[0:1, :],
                                        1.0 / D, None, op0=AL.mult))
            p.vfence(v, v.tensor_tensor(SRow[0:1, 1024:1536], SRow[0:1, 0:512],
                                        SRow[0:1, 0:512], op=AL.mult))
            p.inc(v.tensor_tensor(SRow[0:1, 1024:1536], SRow[0:1, 512:1024],
                                  SRow[0:1, 1024:1536], op=AL.subtract),
                  "lns", 1)

        @blk.scalar
        def _(s, tt=tt):
            p.wge(s, "lns", tt + 1)
            p.inc(s.activation(SRow[0:1, 1536:2048], SRow[0:1, 1024:1536],
                               AF.Sqrt, bias=np.float32(LN_EPS)), "lnq", 1)

        @blk.vector
        def _(v, tt=tt):
            p.wge(v, "lnq", tt + 1)
            p.vfence(v, v.reciprocal(SRow[0:1, 2048:2560],
                                     SRow[0:1, 1536:2048]))
            p.inc(v.tensor_copy(SQ1[0:1, 0:512], SRow[0:1, 0:512]), "lnb", 1)
            p.inc(v.tensor_copy(SQ1[0:1, 512:1024], SRow[0:1, 2048:2560]),
                  "lnb", 1)

        @blk.tensor
        def _(t, tt=tt):
            p.wge(t, "lnb", 2 * (tt + 1))
            t.matmul(PBC[:, :], ONE1R[:, :], SQ1[0:1, 0:512], start=True,
                     stop=True, skip_group_check=True)
            p.inc(t.matmul(PSA[:, :], ONE1R[:, :], SQ1[0:1, 512:1024],
                           start=True, stop=True, skip_group_check=True),
                  "acp", 1)

        @blk.vector
        def _(v, tt=tt):
            p.wge(v, "acp", tt + 1)
            for cc in range(16):
                sl = slice(1024 * cc + 512 * tt, 1024 * cc + 512 * (tt + 1))
                v.tensor_tensor(SQ1[:, 0:512], SRC[:, sl], PBC[:, :],
                                op=AL.subtract)
                if gamma is None:
                    p.inc(v.tensor_tensor(DST[:, sl], SQ1[:, 0:512],
                                          PSA[:, :], op=AL.mult), "xn", 1)
                else:
                    v.tensor_tensor(SQ1[:, 512:1024], SQ1[:, 0:512],
                                    PSA[:, :], op=AL.mult)
                    p.inc(v.tensor_scalar(DST[:, sl], SQ1[:, 512:1024],
                                          gamma[cc], beta[cc],
                                          op0=AL.mult, op1=AL.add), "xn", 1)

    return



def emit_combine0(p, env, ph):
    p.rebase_all()
    nc, blk = p.nc, p.block
    XNO = env["XNO"]

    XTM = [sb(ph, nc, f"XTM{i}", [128, D], F32) for i in range(2)]
    X1F = sb(ph, nc, "X1F", [128, 16 * TOK], F32)
    SGL = sb(ph, nc, "SGL", [128, 4 * TOK], BF16)
    IDFC = sb(ph, nc, "IDFC", [128, 128], F32)
    SQ1 = sb(ph, nc, "SQ1", [128, 2 * 512], BF16)
    ONE1R = sb(ph, nc, "ONE1R", [1, 128], BF16)
    ONECL = sb(ph, nc, "ONECL", [128, 1], F32)
    ONECLB = sb(ph, nc, "ONECLB", [128, 1], BF16)
    SRow = sb(ph, nc, "SRow", [1, 5 * 512], F32)

    ptp = [ps(ph, nc, f"cptp{i}", [128, 4 * 128], F32) for i in range(2)]
    PSA = ps(ph, nc, "cPSA", [128, 512], F32)
    PSB = ps(ph, nc, "cPSB", [128, 512], F32)
    PBC = ps(ph, nc, "cPBC", [128, 512], F32)

    def ffn0_gate(eng):
        for nm in ["cst0", "msr", "mgd", "bnq", "pvg", "bkc", "wkv", "vfs",
                   "bvl", "bcwg"]:
            eng.wait_ge(p.sem(nm), p.n(nm))

    ECH = env["ECH"]

    @blk.vector
    def _(v):
        v.wait_ge(p.sem("cst0"), p.n("cst0"))
        p.inc(v.tensor_copy(ECH[0:1, 0:1], env["ONESB"][0:1, 0:1]), "ech", 1)

    def ffn0_gate_g(eng):
        eng.wait_ge(p.sem("ech"), p.n("ech"))
        for nm in ["msr", "mgd", "bnq", "pvg", "bkc", "wkv", "vfs",
                   "bvl", "bcwg"]:
            eng.wait_ge(p.sem(nm), p.n(nm))

    @blk.sync
    def _(sy):
        ffn0_gate(sy)
        p.inc(sy.dma_start(out=IDFC[:, :], in_=env["identf32"][:, :]),
              "cst", 16)
        for u in range(8):
            if u >= 2:
                p.wge(sy, "tpq", 16 * (u - 1))
            p.inc(sy.dma_start(out=XTM[u % 2][:, :],
                               in_=env["x"][512 + 128 * u:
                                            512 + 128 * (u + 1), :]),
                  ("xl0", "xl1")[u % 2], 16)

    tpk = {"n": 0, "ke": 0}

    @blk.tensor
    def _(t):
        p.wge(t, "cst", 16)
        for u in range(8):
            p.wge(t, ("xl0", "xl1")[u % 2], 16 * (u // 2 + 1))
            for cc in range(16):
                k = tpk["n"]
                if k >= 8:
                    p.wge(t, "tpe", k - 4)
                bk, slt = (k // 4) % 2, k % 4
                ins = t.transpose(ptp[bk][:, 128 * slt:128 * (slt + 1)],
                                  XTM[u % 2][:, 128 * cc:128 * (cc + 1)],
                                  IDFC[:, :])
                p.inc(ins, "tpq", 1)
                tpk["n"] += 1

    @blk.scalar
    def _(s):
        for u in range(8):
            for cc in range(16):
                k = tpk["ke"]
                p.wge(s, "tpq", (k // 4 + 1) * 4)
                bk, slt = (k // 4) % 2, k % 4
                ins = s.copy(X1F[:, 1024 * cc + 128 * u:
                                 1024 * cc + 128 * (u + 1)],
                             ptp[bk][:, 128 * slt:128 * (slt + 1)])
                p.inc(ins, "tpe", 1)
                tpk["ke"] += 1

    @blk.gpsimd
    def _(g):
        ffn0_gate_g(g)
        for cc in range(16):
            sl = slice(2048 * (cc % 2), 2048 * (cc % 2) + 1024)
            sl2 = slice(2048 * (cc % 2) + 1024, 2048 * (cc % 2) + 2048)
            if cc >= 2:
                p.wge(g, "age", cc - 1)
            p.inc(g.dma_start(out=SGL[:, sl],
                              in_=env["s0_scr"][128 * cc:128 * (cc + 1), :]),
                  ("wkv", "vfs")[cc % 2], 16)
            p.inc(g.dma_start(out=SGL[:, sl2],
                              in_=env["g0_scr"][128 * cc:128 * (cc + 1), :]),
                  ("wkv", "vfs")[cc % 2], 16)

    @blk.vector
    def _(v):
        for cc in range(16):
            p.wge(v, ("wkv", "vfs")[cc % 2], 32 * (cc // 2 + 1))
            p.wge(v, "tpe", 128)
            sl = slice(2048 * (cc % 2), 2048 * (cc % 2) + 1024)
            sl2 = slice(2048 * (cc % 2) + 1024, 2048 * (cc % 2) + 2048)
            xsl = slice(1024 * cc, 1024 * (cc + 1))
            v.tensor_tensor(SGL[:, sl], SGL[:, sl], SGL[:, sl2], op=AL.mult)
            p.inc(v.tensor_tensor(X1F[:, xsl], X1F[:, xsl], SGL[:, sl],
                                  op=AL.add), "age", 1)

    @blk.sync
    def _(sy):
        for cc in range(16):
            p.wge(sy, "age", cc + 1)
            p.inc(sy.dma_start(out=env["x1_scr"][128 * cc:128 * (cc + 1), :],
                               in_=X1F[:, 1024 * cc:1024 * (cc + 1)]),
                  "cst0", 16)

    @blk.vector
    def _(v):
        p.wge(v, "age", 16)

    emit_fm_ln(p, env, ph, X1F, XNO, (PSA, PSB, PBC),
               (SQ1, ONE1R, ONECL, ONECLB, SRow))
    return



def emit_l1pre(p, env, ph):
    p.rebase_all()
    nc, blk = p.nc, p.block
    XNO = env["XNO"]
    IDB = env["IDB"]

    WQK = [sb(ph, nc, f"l1w{i}", [128, 2048], BF16) for i in range(2)]
    BCL = sb(ph, nc, "BCL", [128, 3 * 16], F32)
    EQK = sb(ph, nc, "EQK", [128, 16 * TOK], BF16)
    VS = sb(ph, nc, "VS", [128, 16 * TOK], BF16)
    ETM = sb(ph, nc, "ETM", [128, 8 * 128], BF16)
    VTM = sb(ph, nc, "VTM", [128, 8 * 128], BF16)
    KVP = sb(ph, nc, "KVP", [128, 16 * 128], F32)
    SCE = [sb(ph, nc, f"SCE{i}", [128, 512], BF16) for i in range(3)]
    UEV = sb(ph, nc, "UEV", [128, 2 * 512], BF16)

    pj = [ps(ph, nc, f"lpj{i}", [128, 512], F32) for i in range(4)]
    ptp = [ps(ph, nc, f"lptp{i}", [128, 8 * 128], BF16) for i in range(2)]
    pkv = ps(ph, nc, "lpkv", [128, 128], F32)

    rtp = SlotRing(p, "tp2", ptp)

    def comb_gate(eng):
        for nm in ["cst0", "xn", "tpe", "age", "acp"]:
            eng.wait_ge(p.sem(nm), p.n(nm))

    ECH = env["ECH"]

    @blk.vector
    def _(v):
        v.wait_ge(p.sem("cst0"), p.n("cst0"))
        p.inc(v.tensor_copy(ECH[0:1, 0:1], env["ONESB"][0:1, 0:1]), "ech", 1)

    def comb_gate_g(eng):
        eng.wait_ge(p.sem("ech"), p.n("ech"))
        for nm in ["xn", "tpe", "age", "acp"]:
            eng.wait_ge(p.sem(nm), p.n(nm))

    @blk.sync
    def _(sy):
        comb_gate(sy)
        for i, nm in enumerate(["l1_attn_bqk", "l1_attn_bv", "l1_attn_bu"]):
            for cc in range(16):
                p.inc(sy.dma_start(out=BCL[:, 16 * i + cc:16 * i + cc + 1],
                                   in_=env[nm][128 * cc:128 * (cc + 1), :]),
                      "cst", 16)

    for wi, (wname, sem) in enumerate([("l1_attn_wqk", "awq0"),
                                       ("l1_attn_wv", "awq0"),
                                       ("l1_attn_wu", "awq0")]):
        @blk.gpsimd
        def _(g, wi=wi, wname=wname):
            if wi == 0:
                comb_gate_g(g)
            for oc in range(16):
                idx = wi * 16 + oc
                if idx >= 2:
                    p.wge(g, "kpj", 2 * (idx - 1))
                p.inc(g.dma_start(
                    out=bass.AP(WQK[idx % 2], 0,
                                [[2048, 128], [128, 16], [1, 128]]),
                    in_=bass.AP(env[wname].tensor, 128 * oc,
                                [[2048, 128], [128 * 2048, 16], [1, 128]])),
                    ("wkv", "vfs")[idx % 2], 16)

        @blk.tensor
        def _(t, wi=wi):
            for oc in range(16):
                idx = wi * 16 + oc
                p.wge(t, ("wkv", "vfs")[idx % 2], 16 * (idx // 2 + 1))
                for tt in range(2):
                    if 2 * idx + tt >= 4:
                        p.wge(t, "age", 2 * idx + tt - 3)
                    for kc in range(16):
                        ins = t.matmul(pj[(2 * idx + tt) % 4][:, :],
                                       WQK[idx % 2][:, 128 * kc:128 * (kc + 1)],
                                       XNO[:, 1024 * kc + 512 * tt:
                                           1024 * kc + 512 * (tt + 1)],
                                       start=(kc == 0), stop=(kc == 15))
                        if kc == 15:
                            p.inc(ins, "kpj", 1)

        if wi == 0:
            @blk.vector
            def _(v, wi=wi):
                p.wge(v, "cst", 256)
                for idx2 in range(32):
                    oc, tt = idx2 // 2, idx2 % 2
                    p.wge(v, "kpj", idx2 + 1)
                    bank = pj[idx2 % 4]
                    v.tensor_scalar(SCE[0][:, :], bank[:, :],
                                    BCL[:, oc:oc + 1], 0.0,
                                    op0=AL.add, op1=AL.max)
                    v.tensor_scalar(SCE[1][:, :], bank[:, :],
                                    BCL[:, oc:oc + 1], None, op0=AL.add)
                    p.inc(v.tensor_tensor(SCE[2][:, :], SCE[1][:, :],
                                          SCE[0][:, :], op=AL.subtract),
                          "rpd", 1)
                    p.wge(v, "knq", idx2 + 1)
                    p.inc(v.scalar_tensor_tensor(
                        EQK[:, 1024 * oc + 512 * tt:1024 * oc + 512 * (tt + 1)],
                        SCE[1][:, :], -1.0, SCE[0][:, :],
                        op0=AL.add, op1=AL.add), "age", 1)

            @blk.scalar
            def _(s, wi=wi):
                for idx2 in range(32):
                    p.wge(s, "rpd", idx2 + 1)
                    p.inc(s.activation(SCE[1][:, :], SCE[2][:, :], AF.Exp),
                          "knq", 1)
        else:
            dstq = None
            @blk.scalar
            def _(s, wi=wi):
                for idx2 in range(32):
                    oc, tt = idx2 // 2, idx2 % 2
                    p.wge(s, "kpj", 32 * wi + idx2 + 1)
                    if wi == 2 and idx2 >= 2:
                        p.wge(s, "xl0", 16 * (idx2 - 1))
                    dst = (VS[:, 1024 * oc + 512 * tt:
                              1024 * oc + 512 * (tt + 1)] if wi == 1 else
                           UEV[:, 512 * (idx2 % 2):512 * (idx2 % 2 + 1)])
                    p.inc(s.activation(dst, pj[idx2 % 4][:, :], AF.Silu,
                                       bias=BCL[:, 16 * wi + oc:
                                                16 * wi + oc + 1]),
                          "age", 1)

            if wi == 2:
                @blk.sync
                def _(sy):
                    for idx2 in range(32):
                        oc, tt = idx2 // 2, idx2 % 2
                        p.wge(sy, "age", 64 + idx2 + 1)
                        p.inc(sy.dma_start(
                            out=env["u_scr"][128 * oc:128 * (oc + 1),
                                             512 * tt:512 * (tt + 1)],
                            in_=UEV[:, 512 * (idx2 % 2):512 * (idx2 % 2 + 1)]),
                            "xl0", 16)

    @blk.sync
    def _(sy):
        p.wge(sy, "age", 32)
        for cc in range(16):
            p.inc(sy.dma_start(out=env["qe_scr"][128 * cc:128 * (cc + 1), :],
                               in_=EQK[:, 1024 * cc:1024 * (cc + 1)]),
                  "cst0", 16)

    for h in range(16):
        @blk.tensor
        def _(t, h=h):
            if h == 0:
                p.wge(t, "age", 64)
            for tb in range(8):
                for src in (EQK, VS):
                    dstap = rtp.pre_transpose(t)
                    ins = t.transpose(dstap,
                                      src[:, 1024 * h + 128 * tb:
                                          1024 * h + 128 * (tb + 1)],
                                      IDB[:, :])
                    rtp.post_transpose(ins)

        @blk.vector
        def _(v, h=h):
            if h >= 1:
                p.wge(v, "csq", h)
            for tb in range(8):
                for j, dstt in enumerate((ETM, VTM)):
                    srcap = rtp.pre_evac(v)
                    ins = v.tensor_copy(dstt[:, 128 * tb:128 * (tb + 1)], srcap)
                    rtp.post_evac(ins)

        @blk.tensor
        def _(t, h=h):
            p.wge(t, "tp2e", 16 * (h + 1))
            if h >= 1:
                p.wge(t, "pcv", h)
            for tb in range(8):
                ins = t.matmul(pkv[:, :], ETM[:, 128 * tb:128 * (tb + 1)],
                               VTM[:, 128 * tb:128 * (tb + 1)],
                               start=(tb == 0), stop=(tb == 7))
                if tb == 7:
                    p.inc(ins, "csq", 1)

        @blk.vector
        def _(v, h=h):
            p.wge(v, "csq", h + 1)
            p.inc(v.tensor_copy(KVP[:, 128 * h:128 * (h + 1)], pkv[:, :]),
                  "pcv", 1)

    @blk.sync
    def _(sy):
        p.wge(sy, "pcv", 16)
        p.inc(sy.dma_start(out=env["kvp_scr"][:, :], in_=KVP[:, :]), "cst0", 16)
        sy.wait_ge(p.sem("cst0"), p.n("cst0"))
        sy.wait_ge(p.sem("xl0"), p.n("xl0"))
    return



def emit_ffn1(p, env, ph):
    p.rebase_all()
    nc, blk = p.nc, p.block
    XNO = env["XNO"]

    GT1 = sb(ph, nc, "GT1", [128, 16 * 16], BF16)
    SELB1 = sb(ph, nc, "SELB1", [16, 16 * 128], BF16)
    IDF1 = sb(ph, nc, "IDF1", [128, 128], F32)
    EBC = sb(ph, nc, "EBC", [16, 1], F32)
    ZR1 = sb(ph, nc, "ZR1", [16, TOK], F32)
    ZB1 = sb(ph, nc, "ZB1", [16, TOK], F32)
    ZT1 = sb(ph, nc, "ZT1", [128, 8 * 16], F32)
    ZTB = sb(ph, nc, "ZTB", [128, 8 * 16], F32)
    E1T = sb(ph, nc, "E1T", [128, 16], F32)
    E2T = sb(ph, nc, "E2T", [128, 16], F32)
    ZMK1 = sb(ph, nc, "ZMK1", [128, 16], F32)
    WDT1 = sb(ph, nc, "WDT1", [128, 8 * 16], F32)
    WD1 = sb(ph, nc, "WD1", [16, TOK], BF16)
    FC1 = sb(ph, nc, "FC1", [128, 16], F32)
    WB1 = sb(ph, nc, "WB1", [128, 2 * 512], BF16)
    WS1 = [sb(ph, nc, f"WS1_{i}", [128, 2048], BF16) for i in range(2)]
    GS1 = sb(ph, nc, "GS1", [128, 8 * 1024], BF16)
    HH1 = sb(ph, nc, "HH1", [128, 8 * 1024], BF16)
    YSB1 = sb(ph, nc, "YSB1", [128, 16 * 1024], BF16)
    G1E = sb(ph, nc, "G1E", [128, 2 * 512], BF16)

    pr1 = [ps(ph, nc, f"gpr{i}", [128, 512], F32) for i in range(4)]
    pd1 = [ps(ph, nc, f"gpd{i}", [128, 512], F32) for i in range(2)]
    pt2 = ps(ph, nc, "gpt2", [128, 128], F32)
    pb2 = ps(ph, nc, "gpb2", [128, 512], F32)

    def l1pre_gate(eng):
        for nm in ["cst0", "xl0", "pcv", "csq", "age", "kpj", "wkv", "vfs",
                   "cst"]:
            eng.wait_ge(p.sem(nm), p.n(nm))

    ECH = env["ECH"]

    @blk.vector
    def _(v):
        for nm in ["cst0", "xl0", "cst"]:
            v.wait_ge(p.sem(nm), p.n(nm))
        p.inc(v.tensor_copy(ECH[0:1, 0:1], env["ONESB"][0:1, 0:1]), "ech", 1)

    def l1pre_gate_g(eng):
        eng.wait_ge(p.sem("ech"), p.n("ech"))
        for nm in ["pcv", "csq", "age", "kpj", "wkv", "vfs"]:
            eng.wait_ge(p.sem(nm), p.n(nm))

    @blk.sync
    def _(sy):
        l1pre_gate(sy)
        p.inc(sy.dma_start(out=SELB1[:, :], in_=env["selb"][:, :]), "cst", 16)
        p.inc(sy.dma_start(out=IDF1[:, :], in_=env["identf32"][:, :]),
              "cst", 16)
        p.inc(sy.dma_start(out=EBC[:, :], in_=env["l1_ffn_expert_bias"][:, :]),
              "cst", 16)

    @blk.gpsimd
    def _(g):
        l1pre_gate_g(g)
        p.inc(g.dma_start(
            out=bass.AP(GT1, 0, [[256, 128], [16, 16], [1, 16]]),
            in_=bass.AP(env["l1_ffn_gate_w"].tensor, 0,
                        [[16, 128], [128 * 16, 16], [1, 16]])), "bvl", 16)

    @blk.tensor
    def _(t):
        p.wge(t, "bvl", 16)
        p.wge(t, "cst", 48)
        for tt in range(2):
            for kc in range(16):
                ins = t.matmul(pr1[tt][:16, :], GT1[:, 16 * kc:16 * (kc + 1)],
                               XNO[:, 1024 * kc + 512 * tt:
                                   1024 * kc + 512 * (tt + 1)],
                               start=(kc == 0), stop=(kc == 15))
                if kc == 15:
                    p.inc(ins, "agp", 1)

    @blk.scalar
    def _(s):
        for tt in range(2):
            p.wge(s, "agp", tt + 1)
            p.inc(s.activation(ZR1[:, 512 * tt:512 * (tt + 1)], pr1[tt][:16, :],
                               AF.Sigmoid), "age", 1)

    @blk.vector
    def _(v):
        for tt in range(2):
            p.wge(v, "age", tt + 1)
            p.inc(v.tensor_scalar(ZB1[:, 512 * tt:512 * (tt + 1)],
                                  ZR1[:, 512 * tt:512 * (tt + 1)],
                                  EBC[:, :], None, op0=AL.add), "st1", 1)

    @blk.tensor
    def _(t):
        for b8 in range(8):
            p.wge(t, "st1", b8 // 4 + 1)
            if b8 >= 1:
                p.wge(t, "acl", 2 * b8)
            p.inc(t.transpose(pt2[:, 0:16], ZR1[:, 128 * b8:128 * (b8 + 1)],
                              IDF1[0:16, 0:16]), "act", 1)
            p.inc(t.transpose(pt2[:, 16:32], ZB1[:, 128 * b8:128 * (b8 + 1)],
                              IDF1[0:16, 0:16]), "act", 1)

    @blk.vector
    def _(v):
        for b8 in range(8):
            p.wge(v, "act", 2 * (b8 + 1))
            v.tensor_copy(ZT1[:, 16 * b8:16 * (b8 + 1)],
                          bass.AP(pt2, 0, [[128, 128], [1, 16]]))
            p.inc(v.tensor_copy(ZTB[:, 16 * b8:16 * (b8 + 1)],
                                bass.AP(pt2, 16, [[128, 128], [1, 16]])),
                  "acl", 2)

    for b8 in range(8):
        @blk.vector
        def _(v, b8=b8):
            p.wge(v, "acl", 2 * (b8 + 1))
            if b8 >= 1:
                p.wge(v, "aqe", b8)
            zb = ZTB[:, 16 * b8:16 * (b8 + 1)]
            zs = ZT1[:, 16 * b8:16 * (b8 + 1)]
            p.vred(v, FC1[:, 0:1], zb, AL.max)
            p.vfence(v, v.tensor_scalar(E1T[:, :], zb, FC1[:, 0:1], None,
                                        op0=AL.is_equal))
            p.vfence(v, v.scalar_tensor_tensor(ZMK1[:, :], E1T[:, :], -1e30,
                                               zb, op0=AL.mult, op1=AL.add))
            p.vred(v, FC1[:, 1:2], ZMK1[:, :], AL.max)
            p.vfence(v, v.tensor_scalar(E2T[:, :], zb, FC1[:, 1:2], None,
                                        op0=AL.is_equal))
            p.vfence(v, v.tensor_tensor(ZMK1[:, :], E1T[:, :], zs, op=AL.mult))
            p.vred(v, FC1[:, 2:3], ZMK1[:, :], AL.add)
            p.vfence(v, v.tensor_tensor(ZMK1[:, :], E2T[:, :], zs, op=AL.mult))
            p.vred(v, FC1[:, 3:4], ZMK1[:, :], AL.add)
            p.vfence(v, v.tensor_tensor(FC1[:, 4:5], FC1[:, 2:3], FC1[:, 3:4],
                                        op=AL.add))
            p.vfence(v, v.tensor_scalar(FC1[:, 4:5], FC1[:, 4:5], 1e-20, None,
                                        op0=AL.add))
            p.vfence(v, v.reciprocal(FC1[:, 5:6], FC1[:, 4:5]))
            p.vfence(v, v.tensor_tensor(FC1[:, 6:7], FC1[:, 2:3], FC1[:, 5:6],
                                        op=AL.mult))
            p.vfence(v, v.tensor_tensor(FC1[:, 7:8], FC1[:, 3:4], FC1[:, 5:6],
                                        op=AL.mult))
            p.vfence(v, v.tensor_scalar(WDT1[:, 16 * b8:16 * (b8 + 1)],
                                        E1T[:, :], FC1[:, 6:7], None,
                                        op0=AL.mult))
            p.vfence(v, v.tensor_scalar(ZMK1[:, :], E2T[:, :], FC1[:, 7:8],
                                        None, op0=AL.mult))
            p.inc(v.tensor_tensor(WDT1[:, 16 * b8:16 * (b8 + 1)],
                                  WDT1[:, 16 * b8:16 * (b8 + 1)],
                                  ZMK1[:, :], op=AL.add), "aqe", 1)

    @blk.tensor
    def _(t):
        for b8 in range(8):
            p.wge(t, "aqe", b8 + 1)
            if b8 >= 1:
                p.wge(t, "aqp", b8)
            p.inc(t.transpose(pt2[0:16, 0:128], WDT1[:, 16 * b8:16 * (b8 + 1)],
                              IDF1[:, :]), "aqn", 1)

    @blk.vector
    def _(v):
        for b8 in range(8):
            p.wge(v, "aqn", b8 + 1)
            p.inc(v.tensor_copy(WD1[:, 128 * b8:128 * (b8 + 1)],
                                pt2[0:16, 0:128]), "aqp", 1)

    for half, wname in [(0, "l1_ffn_shared_gate_w"), (1, "l1_ffn_shared_up_w")]:
        @blk.gpsimd
        def _(g, half=half, wname=wname):
            for oc in range(8):
                idx = half * 8 + oc
                if idx >= 2:
                    p.wge(g, "scg", 2 * (idx - 1))
                p.inc(g.dma_start(
                    out=bass.AP(WS1[idx % 2], 0,
                                [[2048, 128], [128, 16], [1, 128]]),
                    in_=bass.AP(env[wname].tensor, 128 * oc,
                                [[1024, 128], [128 * 1024, 16], [1, 128]])),
                    ("wkv", "vfs")[idx % 2], 16)

        @blk.tensor
        def _(t, half=half):
            for oc in range(8):
                idx = half * 8 + oc
                p.wge(t, ("wkv", "vfs")[idx % 2], 16 * (idx // 2 + 1))
                for tt in range(2):
                    gi = 2 * idx + tt
                    if gi >= 4:
                        p.wge(t, "xpg", gi - 3)
                    elif gi < 2:
                        p.wge(t, "age", gi + 1)
                    for kc in range(16):
                        ins = t.matmul(pr1[gi % 4][:, :],
                                       WS1[idx % 2][:, 128 * kc:128 * (kc + 1)],
                                       XNO[:, 1024 * kc + 512 * tt:
                                           1024 * kc + 512 * (tt + 1)],
                                       start=(kc == 0), stop=(kc == 15))
                        if kc == 15:
                            p.inc(ins, "scg", 1)

        if half == 0:
            @blk.scalar
            def _(s):
                for gi in range(16):
                    oc, tt = gi // 2, gi % 2
                    p.wge(s, "scg", gi + 1)
                    p.inc(s.activation(GS1[:, 1024 * oc + 512 * tt:
                                           1024 * oc + 512 * (tt + 1)],
                                       pr1[gi % 4][:, :], AF.Silu), "xpg", 1)
        else:
            @blk.vector
            def _(v):
                for gi in range(16):
                    oc, tt = gi // 2, gi % 2
                    p.wge(v, "scg", 16 + gi + 1)
                    sl = slice(1024 * oc + 512 * tt, 1024 * oc + 512 * (tt + 1))
                    p.inc(v.tensor_tensor(HH1[:, sl], GS1[:, sl],
                                          pr1[gi % 4][:, :], op=AL.mult),
                          "xpg", 1)

    @blk.gpsimd
    def _(g):
        for oc in range(16):
            if oc >= 2:
                p.wge(g, "bnq", 2 * (oc - 1))
            else:
                p.wge(g, "scg", 32)
            p.inc(g.dma_start(
                out=bass.AP(WS1[oc % 2], 0,
                            [[2048, 128], [128, 8], [1, 128]]),
                in_=bass.AP(env["l1_ffn_shared_down_w"].tensor, 128 * oc,
                            [[2048, 128], [128 * 2048, 8], [1, 128]])),
                ("bvl", "bcwg")[oc % 2], 16)

    @blk.tensor
    def _(t):
        p.wge(t, "xpg", 32)
        for oc in range(16):
            p.wge(t, ("bvl", "bcwg")[oc % 2],
                  (16 if oc % 2 == 0 else 0) + 16 * (oc // 2 + 1))
            for tt in range(2):
                gi = 2 * oc + tt
                if gi >= 2:
                    p.wge(t, "bnr", gi - 1)
                for kc in range(8):
                    ins = t.matmul(pd1[gi % 2][:, :],
                                   WS1[oc % 2][:, 128 * kc:128 * (kc + 1)],
                                   HH1[:, 1024 * kc + 512 * tt:
                                       1024 * kc + 512 * (tt + 1)],
                                   start=(kc == 0), stop=(kc == 7))
                    if kc == 7:
                        p.inc(ins, "bnq", 1)

    @blk.vector
    def _(v):
        for gi in range(32):
            oc, tt = gi // 2, gi % 2
            p.wge(v, "bnq", gi + 1)
            p.inc(v.tensor_copy(YSB1[:, 1024 * oc + 512 * tt:
                                     1024 * oc + 512 * (tt + 1)],
                                pd1[gi % 2][:, :]), "bnr", 1)

    for e in range(E1):
        @blk.tensor
        def _(t, e=e):
            if e == 0:
                p.wge(t, "aqp", 8)
                p.wge(t, "bnr", 32)
            else:
                p.wge(t, "wtg", 16 * e)
            for tt in range(2):
                if e or tt:
                    p.wge(t, "mbe", 2 * e + tt)
                p.inc(t.matmul(pb2[:, :], SELB1[:, 128 * e:128 * (e + 1)],
                               WD1[:, 512 * tt:512 * (tt + 1)],
                               start=True, stop=True, skip_group_check=True),
                      "mbq", 1)

        @blk.vector
        def _(v, e=e):
            for tt in range(2):
                p.wge(v, "mbq", 2 * e + tt + 1)
                p.inc(v.tensor_copy(WB1[:, 512 * tt:512 * (tt + 1)],
                                    pb2[:, :]), "mbe", 1)

        for half in range(2):
            @blk.gpsimd
            def _(g, e=e, half=half):
                for oc in range(8):
                    idx = (2 * e + half) * 8 + oc
                    if idx >= 2:
                        p.wge(g, "bkc", 2 * (idx - 1))
                    if idx % 16 < 2:
                        if e == 0:
                            p.wge(g, "bnq", 32)
                        else:
                            p.wge(g, "pvg", 32 * e)
                    p.inc(g.dma_start(
                        out=bass.AP(WS1[idx % 2], 0,
                                    [[2048, 128], [128, 16], [1, 128]]),
                        in_=bass.AP(env["l1_ffn_experts_gate_up"].tensor,
                                    e * 2048 * 2048 + 1024 * half + 128 * oc,
                                    [[2048, 128], [128 * 2048, 16], [1, 128]])),
                        ("wkv", "vfs")[idx % 2], 16)

            @blk.tensor
            def _(t, e=e, half=half):
                for oc in range(8):
                    idx = (2 * e + half) * 8 + oc
                    p.wge(t, ("wkv", "vfs")[idx % 2],
                          128 + 16 * (idx // 2 + 1))
                    for tt in range(2):
                        j = 2 * (half * 8 + oc) + tt
                        if e == 0 and j < 4:
                            p.wge(t, "xpg", 29 + j)
                        elif j < 4:
                            p.wge(t, "wtg", 16 * e - 3 + j)
                        elif j < 20:
                            p.wge(t, "bkq", 16 * e + j - 3)
                        else:
                            p.wge(t, "wtg", 16 * e + j - 19)
                        for kc in range(16):
                            ins = t.matmul(pr1[j % 4][:, :],
                                           WS1[idx % 2][:, 128 * kc:
                                                        128 * (kc + 1)],
                                           XNO[:, 1024 * kc + 512 * tt:
                                               1024 * kc + 512 * (tt + 1)],
                                           start=(kc == 0), stop=(kc == 15))
                            if kc == 15:
                                p.inc(ins, "bkc", 1)

            if half == 0:
                @blk.scalar
                def _(s, e=e):
                    for gi16 in range(16):
                        oc, tt = gi16 // 2, gi16 % 2
                        gg = 32 * e + gi16
                        p.wge(s, "bkc", gg + 1)
                        p.inc(s.activation(GS1[:, 1024 * oc + 512 * tt:
                                               1024 * oc + 512 * (tt + 1)],
                                           pr1[gg % 4][:, :], AF.Silu),
                              "bkq", 1)
            else:
                @blk.vector
                def _(v, e=e):
                    for gi16 in range(16):
                        oc, tt = gi16 // 2, gi16 % 2
                        gg = 32 * e + 16 + gi16
                        p.wge(v, "bkc", gg + 1)
                        if gi16 == 0:
                            p.wge(v, "mbe", 2 * (e + 1))
                            p.wge(v, "pvg", 32 * e)
                        sl = slice(1024 * oc + 512 * tt,
                                   1024 * oc + 512 * (tt + 1))
                        v.tensor_tensor(GS1[:, sl] if False else
                                        HH1[:, sl], GS1[:, sl],
                                        pr1[gg % 4][:, :], op=AL.mult)
                        p.inc(v.tensor_tensor(HH1[:, sl], HH1[:, sl],
                                              WB1[:, 512 * tt:512 * (tt + 1)],
                                              op=AL.mult), "wtg", 1)

        @blk.gpsimd
        def _(g, e=e):
            for oc in range(16):
                idx = e * 16 + oc
                if idx >= 2:
                    p.wge(g, "pvg", 2 * (idx - 1))
                if oc < 2:
                    p.wge(g, "bkc", 32 * (e + 1))
                p.inc(g.dma_start(
                    out=bass.AP(WS1[idx % 2], 0,
                                [[2048, 128], [128, 8], [1, 128]]),
                    in_=bass.AP(env["l1_ffn_experts_down"].tensor,
                                e * 1024 * 2048 + 128 * oc,
                                [[2048, 128], [128 * 2048, 8], [1, 128]])),
                    ("bvl", "bcwg")[idx % 2], 16)

        @blk.tensor
        def _(t, e=e):
            p.wge(t, "wtg", 16 * (e + 1))
            for oc in range(16):
                idx = e * 16 + oc
                p.wge(t, ("bvl", "bcwg")[idx % 2],
                      (144 if idx % 2 == 0 else 128) + 16 * (idx // 2 + 1))
                for tt in range(2):
                    gi = 2 * idx + tt
                    if 32 * e + 2 * oc + tt >= 2:
                        p.wge(t, "mgd", 32 * e + 2 * oc + tt - 1)
                    else:
                        p.wge(t, "bnr", 31 + 2 * oc + tt)
                    for kc in range(8):
                        ins = t.matmul(pd1[gi % 2][:, :],
                                       WS1[idx % 2][:, 128 * kc:128 * (kc + 1)],
                                       HH1[:, 1024 * kc + 512 * tt:
                                           1024 * kc + 512 * (tt + 1)],
                                       start=(kc == 0), stop=(kc == 7))
                        if kc == 7:
                            p.inc(ins, "pvg", 1)

        @blk.vector
        def _(v, e=e):
            for gi in range(32):
                oc, tt = gi // 2, gi % 2
                p.wge(v, "pvg", 32 * e + gi + 1)
                sl = slice(1024 * oc + 512 * tt, 1024 * oc + 512 * (tt + 1))
                p.inc(v.tensor_tensor(YSB1[:, sl], YSB1[:, sl],
                                      pd1[gi % 2][:, :], op=AL.add), "mgd", 1)

    for gi in range(32):
        oc, tt = gi // 2, gi % 2

        @blk.scalar
        def _(s, gi=gi, oc=oc, tt=tt):
            p.wge(s, "mgd", 480 + gi + 1)
            if gi >= 2:
                p.wge(s, "cst0", 16 * (gi - 1))
            p.inc(s.activation(G1E[:, 512 * (gi % 2):512 * (gi % 2 + 1)],
                               YSB1[:, 1024 * oc + 512 * tt:
                                    1024 * oc + 512 * (tt + 1)],
                               AF.Silu), "msr", 1)

        @blk.sync
        def _(sy, gi=gi, oc=oc, tt=tt):
            p.wge(sy, "msr", gi + 1)
            p.inc(sy.dma_start(
                out=env["g1_scr"][128 * oc:128 * (oc + 1),
                                  512 * tt:512 * (tt + 1)],
                in_=G1E[:, 512 * (gi % 2):512 * (gi % 2 + 1)]), "cst0", 16)

    @blk.sync
    def _(sy):
        sy.wait_ge(p.sem("cst0"), p.n("cst0"))

    return



def emit_l1post(p, env, ph):
    p.rebase_all()
    nc, blk = p.nc, p.block

    KVS = sb(ph, nc, "KVS", [128, 2048], F32)
    KVL = sb(ph, nc, "KVL", [128, 2 * 2048], F32)
    KVB = sb(ph, nc, "KVB", [128, 2048], BF16)
    QEL = [sb(ph, nc, f"QEL{i}", [128, TOK], BF16) for i in range(2)]
    OH = sb(ph, nc, "OH", [128, 16 * TOK], BF16)
    UL = sb(ph, nc, "UL", [128, 2 * TOK], BF16)
    GBC = sb(ph, nc, "GBC", [128, 2 * 16], F32)
    BOC = sb(ph, nc, "BOC", [128, 16], F32)
    WO1 = [sb(ph, nc, f"WO1_{i}", [128, 2048], BF16) for i in range(2)]
    X1L = sb(ph, nc, "X1L", [128, 2 * TOK], F32)
    G1L = sb(ph, nc, "G1L", [128, 2 * TOK], BF16)
    OTM = sb(ph, nc, "OTM", [128, 8 * D], F32)
    OFE = sb(ph, nc, "OFE", [128, 2 * 512], F32)
    SQ1 = sb(ph, nc, "SQ1b", [128, 2 * 512], BF16)
    ONE1R = sb(ph, nc, "ONE1Rb", [1, 128], BF16)
    ONECL = sb(ph, nc, "ONECLb", [128, 1], BF16)
    ONECLB = ONECL
    SRow = sb(ph, nc, "SRowb", [1, 5 * 512], F32)
    IDFB = sb(ph, nc, "IDFB", [128, 128], F32)

    pq = [ps(ph, nc, f"qpq{i}", [128, 512], F32) for i in range(2)]
    PSA = ps(ph, nc, "qPSA", [128, 512], F32)
    PSB = ps(ph, nc, "qPSB", [128, 512], F32)
    PBC = ps(ph, nc, "qPBC", [128, 512], F32)
    ptp = [ps(ph, nc, f"qptp{i}", [128, 4 * 128], F32) for i in range(2)]

    @blk.sync
    def _(sy):
        p.inc(sy.dma_start(out=KVL[:, 0:2048], in_=env["kvp_all"][0, :, :]),
              "xl0", 16)
        p.inc(sy.dma_start(out=KVL[:, 2048:4096], in_=env["kvp_all"][1, :, :]),
              "xl1", 16)
        p.inc(sy.dma_start(out=IDFB[:, :], in_=env["identf32"][:, :]),
              "cst", 16)
        for i, nm in enumerate(["l1_attn_ln_g", "l1_attn_ln_b"]):
            for cc in range(16):
                p.inc(sy.dma_start(out=GBC[:, 16 * i + cc:16 * i + cc + 1],
                                   in_=env[nm][128 * cc:128 * (cc + 1), :]),
                      "cst", 16)
        for cc in range(16):
            p.inc(sy.dma_start(out=BOC[:, cc:cc + 1],
                               in_=env["l1_attn_bo"][128 * cc:128 * (cc + 1), :]),
                  "cst", 16)

    @blk.vector
    def _(v):
        p.wge(v, "xl0", 16)
        p.wge(v, "xl1", 16)
        p.inc(v.tensor_tensor(KVS[:, :], KVL[:, 0:2048], KVL[:, 2048:4096],
                              op=AL.add), "kva", 1)

    @blk.sync
    def _(sy):
        p.wge(sy, "kva", 1)
        p.inc(sy.dma_start(out=KVL[:, 0:2048], in_=env["kvp_all"][2, :, :]),
              "xl0", 16)
        p.inc(sy.dma_start(out=KVL[:, 2048:4096], in_=env["kvp_all"][3, :, :]),
              "xl1", 16)

    @blk.vector
    def _(v):
        p.wge(v, "xl0", 32)
        p.wge(v, "xl1", 32)
        v.tensor_tensor(KVS[:, :], KVS[:, :], KVL[:, 0:2048], op=AL.add)
        v.tensor_tensor(KVS[:, :], KVS[:, :], KVL[:, 2048:4096], op=AL.add)
        p.inc(v.tensor_copy(KVB[:, :], KVS[:, :]), "kva", 1)

    @blk.gpsimd
    def _(g):
        for h in range(16):
            if h >= 2:
                p.wge(g, "kpj", 2 * (h - 1))
            p.inc(g.dma_start(out=QEL[h % 2][:, :],
                              in_=env["qe_scr"][128 * h:128 * (h + 1), :]),
                  ("wkv", "vfs")[h % 2], 16)

    @blk.tensor
    def _(t):
        p.wge(t, "kva", 2)
        for h in range(16):
            p.wge(t, ("wkv", "vfs")[h % 2], 16 * (h // 2 + 1))
            for tt in range(2):
                if 2 * h + tt >= 2:
                    p.wge(t, "age", 2 * h + tt - 1)
                ins = t.matmul(pq[tt][:, :], KVB[:, 128 * h:128 * (h + 1)],
                               QEL[h % 2][:, 512 * tt:512 * (tt + 1)],
                               start=True, stop=True, skip_group_check=True)
                p.inc(ins, "kpj", 1)

    @blk.vector
    def _(v):
        for h in range(16):
            for tt in range(2):
                p.wge(v, "kpj", 2 * h + tt + 1)
                p.inc(v.tensor_copy(OH[:, 1024 * h + 512 * tt:
                                        1024 * h + 512 * (tt + 1)],
                                    pq[tt][:, :]), "age", 1)

    @blk.vector
    def _(v):
        p.wge(v, "age", 32)
        v.wait_ge(p.sem("cst"), p.n("cst"))

    gamma = [GBC[:, cc:cc + 1] for cc in range(16)]
    beta = [GBC[:, 16 + cc:16 + cc + 1] for cc in range(16)]
    emit_fm_ln(p, env, ph, OH, OH, (PSA, PSB, PBC),
               (SQ1, ONE1R, ONECL, ONECLB, SRow), gamma=gamma, beta=beta)

    @blk.gpsimd
    def _(g):
        for cc in range(16):
            if cc >= 2:
                p.wge(g, "vpj", cc - 1)
            p.inc(g.dma_start(out=UL[:, 1024 * (cc % 2):1024 * (cc % 2 + 1)],
                              in_=env["u_scr"][128 * cc:128 * (cc + 1), :]),
                  ("bvl", "bcwg")[cc % 2], 16)

    @blk.vector
    def _(v):
        p.wge(v, "xn", 32)
        for cc in range(16):
            p.wge(v, ("bvl", "bcwg")[cc % 2], 16 * (cc // 2 + 1))
            sl = slice(1024 * cc, 1024 * (cc + 1))
            p.inc(v.tensor_tensor(OH[:, sl], OH[:, sl],
                                  UL[:, 1024 * (cc % 2):1024 * (cc % 2 + 1)],
                                  op=AL.mult), "vpj", 1)

    @blk.gpsimd
    def _(g):
        for oc in range(16):
            if oc >= 2:
                p.wge(g, "cso", 2 * (oc - 1))
            for kc in range(16):
                p.inc(g.dma_start(
                    out=WO1[oc % 2][:, 128 * kc:128 * (kc + 1)],
                    in_=env["l1_attn_wo"][128 * kc:128 * (kc + 1),
                                          128 * oc:128 * (oc + 1)]),
                    ("acq", "fglg")[oc % 2], 16)
            p.inc(g.dma_start(out=X1L[:, 1024 * (oc % 2):1024 * (oc % 2 + 1)],
                              in_=env["x1_scr"][128 * oc:128 * (oc + 1), :]),
                  ("acq", "fglg")[oc % 2], 16)
            p.inc(g.dma_start(out=G1L[:, 1024 * (oc % 2):1024 * (oc % 2 + 1)],
                              in_=env["g1_scr"][128 * oc:128 * (oc + 1), :]),
                  ("acq", "fglg")[oc % 2], 16)

    tpk = {"n": 0, "ke": 0}

    @blk.tensor
    def _(t):
        p.wge(t, "vpj", 16)

        def trans(gi):
            for q4 in range(4):
                k = tpk["n"]
                if k >= 8:
                    p.wge(t, "tpe", k - 4)
                bk, slt = (k // 4) % 2, k % 4
                ins = t.transpose(ptp[bk][:, 128 * slt:128 * (slt + 1)],
                                  OFE[:, 512 * (gi % 2) + 128 * q4:
                                      512 * (gi % 2) + 128 * (q4 + 1)],
                                  IDFB[:, :])
                p.inc(ins, "tpq", 1)
                tpk["n"] += 1

        for oc in range(16):
            p.wge(t, ("acq", "fglg")[oc % 2], 16 * 18 * (oc // 2 + 1))
            for tt in range(2):
                gi = 2 * oc + tt
                if gi >= 2:
                    p.wge(t, "cse", gi - 1)
                for kc in range(16):
                    ins = t.matmul(pq[gi % 2][:, :],
                                   WO1[oc % 2][:, 128 * kc:128 * (kc + 1)],
                                   OH[:, 1024 * kc + 512 * tt:
                                       1024 * kc + 512 * (tt + 1)],
                                   start=(kc == 0), stop=(kc == 15))
                    if kc == 15:
                        p.inc(ins, "cso", 1)
                if gi >= 2:
                    trans(gi - 2)
        p.wge(t, "cse", 31)
        trans(30)
        p.wge(t, "cse", 32)
        trans(31)

    @blk.vector
    def _(v):
        for oc in range(16):
            for tt in range(2):
                gi = 2 * oc + tt
                p.wge(v, "cso", gi + 1)
                if gi >= 2:
                    p.wge(v, "tpq", 4 * (gi - 1))
                xsl = slice(1024 * (oc % 2) + 512 * tt,
                            1024 * (oc % 2) + 512 * (tt + 1))
                v.tensor_scalar(SQ1[:, 0:512], pq[gi % 2][:, :],
                                BOC[:, oc:oc + 1], None, op0=AL.add)
                v.tensor_tensor(SQ1[:, 512:1024], SQ1[:, 0:512], G1L[:, xsl],
                                op=AL.mult)
                p.inc(v.tensor_tensor(OFE[:, 512 * (gi % 2):
                                          512 * (gi % 2 + 1)],
                                      X1L[:, xsl], SQ1[:, 512:1024],
                                      op=AL.add), "cse", 1)

    @blk.scalar
    def _(s):
        for oc in range(16):
            for tt in range(2):
                for q4 in range(4):
                    k = tpk["ke"]
                    p.wge(s, "tpq", (k // 4 + 1) * 4)
                    bk, slt = (k // 4) % 2, k % 4
                    tb = tt * 4 + q4
                    ins = s.copy(OTM[:, 2048 * tb + 128 * oc:
                                     2048 * tb + 128 * (oc + 1)],
                                 ptp[bk][:, 128 * slt:128 * (slt + 1)])
                    p.inc(ins, "tpe", 1)
                    tpk["ke"] += 1

    @blk.sync
    def _(sy):
        for tb in range(8):
            p.wge(sy, "tpe", 121 + tb)
            p.inc(sy.dma_start(out=env["o_out"][128 * tb:128 * (tb + 1), :],
                               in_=OTM[:, 2048 * tb:2048 * (tb + 1)]),
                  "fin", 16)
        sy.wait_ge(p.sem("fin"), p.n("fin"))

    return


_NPDT = {np.dtype(np.float32): F32, np.dtype(BF): BF16, np.dtype(np.int32): I32}


def declare_inputs(nc, env, sample, names):
    for n in names:
        a = np.asarray(sample[n])
        env[n] = nc.dram_tensor(n, list(a.shape), _NPDT[a.dtype],
                                kind="ExternalInput").ap()


PRELUDE_INS = ["x", "wkv", "cos", "sin", "identbf", "identf32", "onesblk"]
ATTN_INS = PRELUDE_INS + ["wq", "wgate", "wo", "wck0", "wck1", "wcv",
                          "masks", "rep32", "sel3"]


def build_l0_attn_test(sample):
    nc = bass.Bass("TRN2", target_bir_lowering=False)
    env = {}
    with contextlib.ExitStack() as st:
        declare_inputs(nc, env, sample, ATTN_INS)
        env["vfm_scr"] = nc.dram_tensor("vfm_scr", [512, T], BF16).ap()
        env["so_scr"] = nc.dram_tensor("so_scr", [2048, TOK], BF16).ap()
        env["s0_scr"] = nc.dram_tensor("s0_scr", [2048, TOK], BF16).ap()
        o_s0 = nc.dram_tensor("o_s0", [2048, TOK], BF16,
                              kind="ExternalOutput").ap()
        block = st.enter_context(nc.Block())
        p = Prog(nc, block, st)
        with contextlib.ExitStack() as ph1:
            emit_prelude(p, env, ph1)
        with contextlib.ExitStack() as ph:
            emit_attn(p, env, ph)

            @block.sync
            def _(sy):
                sy.wait_ge(p.sem("cst0"), p.n("cst0"))
                for jc in range(16):
                    p.inc(sy.dma_start(out=o_s0[128 * jc:128 * (jc + 1), :],
                                       in_=env["s0_scr"][128 * jc:128 * (jc + 1), :]),
                          "fin", 16)
                sy.wait_ge(p.sem("fin"), p.n("fin"))
    return nc


def build_prelude_test(sample):
    nc = bass.Bass("TRN2", target_bir_lowering=False)
    env = {}
    outs = {}
    with contextlib.ExitStack() as st:
        declare_inputs(nc, env, sample, PRELUDE_INS)
        env["vfm_scr"] = nc.dram_tensor("vfm_scr", [512, T], BF16).ap()
        for name, shape, dt in [
            ("o_stats", [128, 6 * 32], F32),
            ("o_xno", [128, 16 * TOK], BF16),
            ("o_kx0a", [128, T], BF16), ("o_kx0b", [128, T], BF16),
            ("o_kx1a", [128, T], BF16), ("o_kx1b", [128, T], BF16),
            ("o_vt", [128, 32 * 8 * 65], BF16), ("o_kmax", [4, 16], F32),
        ]:
            outs[name] = nc.dram_tensor(name, shape, dt, kind="ExternalOutput").ap()
        block = st.enter_context(nc.Block())
        p = Prog(nc, block, st)
        with contextlib.ExitStack() as ph:
            emit_prelude(p, env, ph)

            @block.sync
            def _(sy):
                sy.wait_ge(p.sem("xno"), p.n("xno"))
                sy.wait_ge(p.sem("tpe"), p.n("tpe"))
                sy.wait_ge(p.sem("knm"), p.n("knm"))
                stt = env["_stats"]
                for i, nm in enumerate(["s1", "s2", "mean", "nbias", "stdv", "rstd"]):
                    p.inc(sy.dma_start(out=outs["o_stats"][:, 32 * i:32 * (i + 1)],
                                       in_=stt[nm][:, :]), "fin", 16)
                for name, tile in [
                    ("o_xno", env["XNO"]),
                    ("o_kx0a", env["KX0"][0]), ("o_kx0b", env["KX0"][1]),
                    ("o_kx1a", env["KX1"][0]), ("o_kx1b", env["KX1"][1]),
                    ("o_vt", env["VT"]), ("o_kmax", env["KMAX"]),
                ]:
                    p.inc(sy.dma_start(out=outs[name][:, :], in_=tile[:, :]),
                          "fin", 16)
                sy.wait_ge(p.sem("fin"), p.n("fin"))
    return nc


FFN0_INS = ATTN_INS + ["gw0t", "selb", "qup", "sup", "qdn", "sdn",
                       "gup", "gdn"]


def build_l0_full_test(sample):
    nc = bass.Bass("TRN2", target_bir_lowering=False)
    env = {}
    with contextlib.ExitStack() as st:
        declare_inputs(nc, env, sample, FFN0_INS)
        env["vfm_scr"] = nc.dram_tensor("vfm_scr", [512, T], BF16).ap()
        env["so_scr"] = nc.dram_tensor("so_scr", [2048, TOK], BF16).ap()
        env["s0_scr"] = nc.dram_tensor("s0_scr", [2048, TOK], BF16).ap()
        env["g0_scr"] = nc.dram_tensor("g0_scr", [2048, TOK], BF16).ap()
        o_s0 = nc.dram_tensor("o_s0", [2048, TOK], BF16,
                              kind="ExternalOutput").ap()
        o_g0 = nc.dram_tensor("o_g0", [2048, TOK], BF16,
                              kind="ExternalOutput").ap()
        o_wd = nc.dram_tensor("o_wd", [16, TOK], BF16,
                              kind="ExternalOutput").ap()
        o_gs = nc.dram_tensor("o_gs", [128, 2], F32,
                              kind="ExternalOutput").ap()
        o_ysb = nc.dram_tensor("o_ysb", [128, 16 * 1024], BF16,
                               kind="ExternalOutput").ap()
        block = st.enter_context(nc.Block())
        p = Prog(nc, block, st)
        with contextlib.ExitStack() as ph1:
            emit_prelude(p, env, ph1)
        with contextlib.ExitStack() as ph2:
            emit_attn(p, env, ph2)
        with contextlib.ExitStack() as ph3:
            emit_ffn0(p, env, ph3)

            @block.sync
            def _(sy):
                sy.wait_ge(p.sem("cst0"), p.n("cst0"))
                sy.wait_ge(p.sem("fgo"), p.n("fgo"))
                WDd, GUPBd, GDNBd, YSBd = env["_dbg0"]
                p.inc(sy.dma_start(out=o_wd[:, :], in_=WDd[:, :]), "fin", 16)
                p.inc(sy.dma_start(out=o_gs[:, 0:1], in_=GUPBd[:, :],
                                   allow_slow_non_contiguous=True), "fin", 16)
                p.inc(sy.dma_start(out=o_gs[:, 1:2], in_=GDNBd[:, :],
                                   allow_slow_non_contiguous=True), "fin", 16)
                p.inc(sy.dma_start(out=o_ysb[:, :], in_=YSBd[:, :]), "fin", 16)
                for jc in range(16):
                    p.inc(sy.dma_start(out=o_s0[128 * jc:128 * (jc + 1), :],
                                       in_=env["s0_scr"][128 * jc:128 * (jc + 1), :]),
                          "fin", 16)
                    p.inc(sy.dma_start(out=o_g0[128 * jc:128 * (jc + 1), :],
                                       in_=env["g0_scr"][128 * jc:128 * (jc + 1), :]),
                          "fin", 16)
                sy.wait_ge(p.sem("fin"), p.n("fin"))
    return nc


LAUNCH_A_INS = ATTN_INS + [
    "gw0t", "selb", "qup", "sup", "qdn", "sdn", "gup", "gdn",
    "l1_attn_wqk", "l1_attn_wv", "l1_attn_wu",
    "l1_attn_bqk", "l1_attn_bv", "l1_attn_bu",
    "l1_ffn_gate_w", "l1_ffn_expert_bias", "l1_ffn_shared_gate_w",
    "l1_ffn_shared_up_w", "l1_ffn_shared_down_w",
    "l1_ffn_experts_gate_up", "l1_ffn_experts_down",
]
LAUNCH_B_INS = ["identf32", "l1_attn_wo", "l1_attn_bo",
                "l1_attn_ln_g", "l1_attn_ln_b"]


def build_launch_a(sample):
    nc = bass.Bass("TRN2", target_bir_lowering=False)
    env = {}
    with contextlib.ExitStack() as st:
        declare_inputs(nc, env, sample, LAUNCH_A_INS)
        env["vfm_scr"] = nc.dram_tensor("vfm_scr", [512, T], BF16).ap()
        env["so_scr"] = nc.dram_tensor("so_scr", [2048, TOK], BF16).ap()
        env["s0_scr"] = nc.dram_tensor("s0_scr", [2048, TOK], BF16).ap()
        env["g0_scr"] = nc.dram_tensor("g0_scr", [2048, TOK], BF16).ap()
        env["x1_scr"] = nc.dram_tensor("x1_scr", [2048, TOK], F32,
                                       kind="ExternalOutput").ap()
        env["qe_scr"] = nc.dram_tensor("qe_scr", [2048, TOK], BF16,
                                       kind="ExternalOutput").ap()
        env["u_scr"] = nc.dram_tensor("u_scr", [2048, TOK], BF16,
                                      kind="ExternalOutput").ap()
        env["g1_scr"] = nc.dram_tensor("g1_scr", [2048, TOK], BF16,
                                       kind="ExternalOutput").ap()
        env["kvp_scr"] = nc.dram_tensor("kvp_scr", [128, 2048], F32,
                                        kind="ExternalOutput").ap()
        block = st.enter_context(nc.Block())
        p = Prog(nc, block, st)
        with contextlib.ExitStack() as ph1:
            emit_prelude(p, env, ph1)
        with contextlib.ExitStack() as ph2:
            emit_attn(p, env, ph2)
        with contextlib.ExitStack() as ph3:
            emit_ffn0(p, env, ph3)
        with contextlib.ExitStack() as ph4:
            emit_combine0(p, env, ph4)
        with contextlib.ExitStack() as ph5:
            emit_l1pre(p, env, ph5)
        with contextlib.ExitStack() as ph6:
            emit_ffn1(p, env, ph6)

            @block.sync
            def _(sy):
                for nm in ["cst0", "xl0", "fin", "fgo"]:
                    sy.wait_ge(p.sem(nm), p.n(nm))
    return nc


def build_launch_b(sample_b):
    nc = bass.Bass("TRN2", target_bir_lowering=False)
    env = {}
    with contextlib.ExitStack() as st:
        declare_inputs(nc, env, sample_b, LAUNCH_B_INS)
        env["kvp_all"] = nc.dram_tensor("kvp_all", [4, 128, 2048], F32,
                                        kind="ExternalInput").ap()
        for nm, dt in [("qe_scr", BF16), ("u_scr", BF16), ("x1_scr", F32),
                       ("g1_scr", BF16)]:
            env[nm] = nc.dram_tensor(nm, [2048, TOK], dt,
                                     kind="ExternalInput").ap()
        env["o_out"] = nc.dram_tensor("o_out", [TOK, 2048], F32,
                                      kind="ExternalOutput").ap()
        block = st.enter_context(nc.Block())
        p = Prog(nc, block, st)
        with contextlib.ExitStack() as ph:
            emit_l1post(p, env, ph)
    return nc


_NC_CACHE = {}


def kernel(**inputs):
    in_maps = host_prepare(inputs)
    if "A" not in _NC_CACHE:
        _NC_CACHE["A"] = build_launch_a(in_maps[0])
    ncA = _NC_CACHE["A"]
    maps_a = [{k: m[k] for k in LAUNCH_A_INS} for m in in_maps]
    resA = run_bass_kernel_spmd(ncA, maps_a, core_ids=list(range(NCORE)))

    maps_b = []
    for c in range(NCORE):
        grp = range(4 * (c // 4), 4 * (c // 4) + 4)
        mb = {k: in_maps[c][k] for k in LAUNCH_B_INS}
        mb["kvp_all"] = np.ascontiguousarray(
            np.stack([resA.results[g]["kvp_scr"] for g in grp], 0))
        for nm in ["qe_scr", "u_scr", "x1_scr", "g1_scr"]:
            mb[nm] = np.ascontiguousarray(resA.results[c][nm])
        maps_b.append(mb)
    if "B" not in _NC_CACHE:
        _NC_CACHE["B"] = build_launch_b(maps_b[0])
    ncB = _NC_CACHE["B"]
    resB = run_bass_kernel_spmd(ncB, maps_b, core_ids=list(range(NCORE)))

    out = np.zeros((B, T, D), np.float32)
    for c in range(NCORE):
        b, qoff = c // 4, (c % 4) * TOK
        out[b, qoff:qoff + TOK] = resB.results[c]["o_out"]
    return out
